# version 06
# baseline (speedup 1.0000x reference)
import numpy as np

import concourse.bass as bass
import concourse.mybir as mybir
from concourse.bass_utils import run_bass_kernel_spmd

F32 = mybir.dt.float32
BF16 = mybir.dt.bfloat16
U16 = mybir.dt.uint16
ALU = mybir.AluOpType
AF = mybir.ActivationFunctionType
AX = mybir.AxisListType

D = 1024
KD = D // 128
NST = 16
DFF = 3072
KFF = DFF // 128
DI = 2816
KI = DI // 128
SEQ = 8192
T = 512
TB = T // 128
EPS = 1e-5
AR_BYTES = 165 * 1024

DEBUG_SET = set()
PHASES = {"prep", "p1a", "p1b", "p2", "p3a", "p3b", "p3c", "p4w", "p4b",
          "p4dw", "p4c"}


class Prog:
    CENG = ("t", "v", "s", "g", "q")
    FENCE_DIST = 2

    def __init__(self, nc, sems, dsems):
        self.nc = nc
        self.sem = sems
        self.dsem = dsems
        self.cnt = {e: 0 for e in self.CENG}
        self.dcnt = {"q": 0, "g": 0}
        self.wm = {e: {} for e in self.CENG}
        self.bufs = {}
        self.thunks = {e: [] for e in self.CENG}

    def _buf(self, key):
        return self.bufs.setdefault(key, {"w": None, "r": {}})

    def _sem_of(self, sk):
        return self.sem[sk] if isinstance(sk, str) else self.dsem[sk[1]]

    def _record_wait(self, e, sk, val):
        if val <= 0 or self.wm[e].get(sk, 0) >= val:
            return
        sem = self._sem_of(sk)
        self.thunks[e].append(
            lambda eng, sem=sem, val=val: eng.wait_ge(sem, val))
        self.wm[e][sk] = val

    def _dma_val(self, sk, c):
        if isinstance(sk, tuple):
            return self.dcnt[sk[1]]
        return c

    def _record_waits(self, e, reads, writes):
        need = {}
        for key in tuple(reads) + tuple(writes):
            b = self._buf(key)
            if b["w"] is not None:
                sk, c = b["w"]
                if sk != e:
                    need[sk] = max(need.get(sk, 0), self._dma_val(sk, c))
                elif e in self.cnt:
                    need[sk] = max(need.get(sk, 0), c)
        for key in writes:
            for sk, c in self._buf(key)["r"].items():
                if sk != e:
                    need[sk] = max(need.get(sk, 0), self._dma_val(sk, c))
                elif e in self.cnt:
                    need[sk] = max(need.get(sk, 0), c)
        for sk, c in sorted(need.items(), key=str):
            self._record_wait(e, sk, c)

    def _mark(self, sk_self, c, reads, writes):
        for key in writes:
            b = self._buf(key)
            b["w"] = (sk_self, c)
            b["r"] = {}
        for key in reads:
            self._buf(key)["r"][sk_self] = c

    def run(self, e, fn, reads=(), writes=()):
        self._record_waits(e, reads, writes)
        self.cnt[e] += 1
        sem = self.sem[e]

        def thunk(eng, fn=fn, sem=sem):
            fn(eng).then_inc(sem, 1)

        self.thunks[e].append(thunk)
        self._mark(e, self.cnt[e], reads, writes)

    def mm_group(self, fns, reads=(), writes=()):
        self._record_waits("t", reads, writes)
        self.cnt["t"] += 1
        sem = self.sem["t"]

        def thunk(eng, fns=tuple(fns), sem=sem):
            instr = None
            for f in fns:
                instr = f(eng)
            instr.then_inc(sem, 1)

        self.thunks["t"].append(thunk)
        self._mark("t", self.cnt["t"], reads, writes)

    def dma(self, qe, out, in_, reads=(), writes=(), transpose=False):
        self._record_waits(qe, reads, writes)
        self.dcnt[qe] += 16
        dsem = self.dsem[qe]

        def thunk(eng, out=out, in_=in_, t=transpose, dsem=dsem):
            if t:
                eng.dma_start(out=out, in_=in_, transpose=True).then_inc(
                    dsem, 16)
            else:
                eng.dma_start(out=out, in_=in_).then_inc(dsem, 16)

        self.thunks[qe].append(thunk)
        self._mark(("D", qe), self.dcnt[qe], reads, writes)

    def barrier(self):
        snap = dict(self.cnt)
        dsnap = dict(self.dcnt)
        for e in self.CENG:
            for f in self.CENG:
                if f != e:
                    self._record_wait(e, f, snap[f])
            for qd in ("q", "g"):
                self._record_wait(e, ("D", qd), dsnap[qd])
        self.bufs = {}


class Arena:

    def __init__(self, ar_bf16, ar_f32):
        self.b = ar_bf16
        self.f4 = ar_f32
        self.off = 0

    def reset(self):
        self.off = 0

    def _take(self, nbytes, align=128):
        o = (self.off + align - 1) // align * align
        self.off = o + nbytes
        assert self.off <= AR_BYTES, f"arena overflow: {self.off}"
        return o

    def bf(self, elems, parts=128):
        o = self._take(elems * 2)
        return self.b[0:parts, o // 2: o // 2 + elems]

    def f(self, elems, parts=128):
        o = self._take(elems * 4)
        return self.f4[0:parts, o // 4: o // 4 + elems]

    def fp(self, elems, parts=128):
        o = self._take(elems * 4)
        return (self.f4[0:parts, o // 4: o // 4 + elems],
                self.b[0:parts, o // 2: o // 2 + 2 * elems].bitcast(U16))


def ln_tok(P, x_ap, out_ap, tmp_ap, stat, kx, kout, ktmp, kstat, eps_ap,
           grep_ap=None, brep_ap=None, kgrep=None):
    W = x_ap.shape[-1]
    s1, m, var, sd = (stat[:, i:i + 1] for i in range(4))
    P.run("v", lambda e: e.reduce_sum(s1, x_ap, AX.X), (kx,), (kstat,))
    P.run("s", lambda e: e.activation(m, s1, AF.Identity, scale=1.0 / W),
          (kstat,), (kstat,))
    P.run("v", lambda e: e.tensor_single_scalar(out_ap, x_ap, m,
                                                ALU.subtract),
          (kx, kstat), (kout,))
    P.run("v", lambda e: e.tensor_mul(tmp_ap, out_ap, out_ap),
          (kout,), (ktmp,))
    P.run("v", lambda e: e.reduce_sum(var, tmp_ap, AX.X), (ktmp,), (kstat,))
    P.run("s", lambda e: e.activation(var, var, AF.Identity, scale=1.0 / W),
          (kstat,), (kstat,))
    P.run("s", lambda e: e.activation(sd, var, AF.Sqrt, bias=eps_ap),
          (kstat, "epsc"), (kstat,))
    P.run("v", lambda e: e.reciprocal(sd, sd), (kstat,), (kstat,))
    P.run("v", lambda e: e.tensor_single_scalar(out_ap, out_ap, sd,
                                                ALU.mult),
          (kout, kstat), (kout,))
    if grep_ap is not None:
        P.run("v", lambda e: e.tensor_mul(out_ap, out_ap, grep_ap),
              (kout, kgrep), (kout,))
        P.run("v", lambda e: e.tensor_add(out_ap, out_ap, brep_ap),
              (kout, kgrep), (kout,))


def ln_tok_multi(P, jobs, eps_ap):
    js = []
    for (x, o, t, st, kx, ko, kt, ks, g, b, kg) in jobs:
        js.append((x, o, t, st[:, 0:1], st[:, 1:2], st[:, 2:3], st[:, 3:4],
                   kx, ko, kt, ks, g, b, kg, 1.0 / x.shape[-1]))
    for x, o, t, s1, m, var, sd, kx, ko, kt, ks, g, b, kg, iw in js:
        P.run("v", lambda e, s1=s1, x=x: e.reduce_sum(s1, x, AX.X),
              (kx,), (ks,))
    for x, o, t, s1, m, var, sd, kx, ko, kt, ks, g, b, kg, iw in js:
        P.run("s", lambda e, m=m, s1=s1, iw=iw: e.activation(
            m, s1, AF.Identity, scale=iw), (ks,), (ks,))
    for x, o, t, s1, m, var, sd, kx, ko, kt, ks, g, b, kg, iw in js:
        P.run("v", lambda e, o=o, x=x, m=m: e.tensor_single_scalar(
            o, x, m, ALU.subtract), (kx, ks), (ko,))
    for x, o, t, s1, m, var, sd, kx, ko, kt, ks, g, b, kg, iw in js:
        P.run("v", lambda e, t=t, o=o: e.tensor_mul(t, o, o), (ko,), (kt,))
    for x, o, t, s1, m, var, sd, kx, ko, kt, ks, g, b, kg, iw in js:
        P.run("v", lambda e, var=var, t=t: e.reduce_sum(var, t, AX.X),
              (kt,), (ks,))
    for x, o, t, s1, m, var, sd, kx, ko, kt, ks, g, b, kg, iw in js:
        P.run("s", lambda e, var=var, iw=iw: e.activation(
            var, var, AF.Identity, scale=iw), (ks,), (ks,))
    for x, o, t, s1, m, var, sd, kx, ko, kt, ks, g, b, kg, iw in js:
        P.run("s", lambda e, sd=sd, var=var: e.activation(
            sd, var, AF.Sqrt, bias=eps_ap), (ks, "epsc"), (ks,))
    for x, o, t, s1, m, var, sd, kx, ko, kt, ks, g, b, kg, iw in js:
        P.run("v", lambda e, sd=sd: e.reciprocal(sd, sd), (ks,), (ks,))
    for x, o, t, s1, m, var, sd, kx, ko, kt, ks, g, b, kg, iw in js:
        P.run("v", lambda e, o=o, sd=sd: e.tensor_single_scalar(
            o, o, sd, ALU.mult), (ko, ks), (ko,))
    if js[0][11] is not None:
        for x, o, t, s1, m, var, sd, kx, ko, kt, ks, g, b, kg, iw in js:
            P.run("v", lambda e, o=o, g=g: e.tensor_mul(o, o, g),
                  (ko, kg), (ko,))
        for x, o, t, s1, m, var, sd, kx, ko, kt, ks, g, b, kg, iw in js:
            P.run("v", lambda e, o=o, b=b: e.tensor_add(o, o, b),
                  (ko, kg), (ko,))


MIDS = [(0.25, 0.5), (0.75, 0.5), (1.25, 0.5), (1.75, 0.5),
        (2.5, 1.0), (3.5, 1.0), (5.0, 2.0)]


def quant_tok_multi(P, jobs, g_adds=False):
    js = []
    for (x, o, t0, t1, t2, sc0, sc1, kx, ko, kt, ks) in jobs:
        nb = x.shape[-1] // 16
        js.append((x.rearrange("p (s n) -> p s n", n=16), o,
                   o.rearrange("p (s n) -> p s n", n=16),
                   t0, t0.rearrange("p (s n) -> p s n", n=16),
                   t1, t2, t2.rearrange("p (s n) -> p s n", n=16),
                   sc0, sc1,
                   sc0.rearrange("p (s o) -> p s o", o=1).to_broadcast(
                       [128, nb, 16]),
                   sc1.rearrange("p (s o) -> p s o", o=1).to_broadcast(
                       [128, nb, 16]),
                   kx, ko, kt, ks))
    AE = "g" if g_adds else "v"
    for xv, o, ov, t0, t0v, t1, t2, t2v, sc0, sc1, rb, sb, kx, ko, kt, ks in js:
        P.run("v", lambda e, sc0=sc0, xv=xv: e.tensor_reduce(
            sc0, xv, AX.X, ALU.max, apply_absolute_value=True), (kx,), (ks,))
    for xv, o, ov, t0, t0v, t1, t2, t2v, sc0, sc1, rb, sb, kx, ko, kt, ks in js:
        P.run("v", lambda e, sc1=sc1, sc0=sc0: e.tensor_scalar(
            sc1, sc0, 1.0 / 6.0, 1e-12, ALU.mult, ALU.max), (ks,), (ks,))
    for xv, o, ov, t0, t0v, t1, t2, t2v, sc0, sc1, rb, sb, kx, ko, kt, ks in js:
        P.run("v", lambda e, sc0=sc0, sc1=sc1: e.reciprocal(sc0, sc1),
              (ks,), (ks,))
    for xv, o, ov, t0, t0v, t1, t2, t2v, sc0, sc1, rb, sb, kx, ko, kt, ks in js:
        P.run("v", lambda e, t0v=t0v, xv=xv, rb=rb: e.tensor_tensor(
            t0v, xv, rb, ALU.mult), (kx, ks), (kt,))
    for xv, o, ov, t0, t0v, t1, t2, t2v, sc0, sc1, rb, sb, kx, ko, kt, ks in js:
        P.run("v", lambda e, t1=t1, t0=t0: e.scalar_tensor_tensor(
            t1, t0, -1.0, t0, ALU.mult, ALU.max), (kt,), (kt,))
    m0, i0 = MIDS[0]
    for xv, o, ov, t0, t0v, t1, t2, t2v, sc0, sc1, rb, sb, kx, ko, kt, ks in js:
        P.run("v", lambda e, t2=t2, t1=t1: e.tensor_scalar(
            t2, t1, m0, i0, ALU.is_gt, ALU.mult), (kt,), (kt,))
    for m_i, inc in MIDS[1:]:
        for xv, o, ov, t0, t0v, t1, t2, t2v, sc0, sc1, rb, sb, kx, ko, kt, \
                ks in js:
            P.run("v", lambda e, o=o, t1=t1, m_i=m_i, inc=inc: e.tensor_scalar(
                o, t1, m_i, inc, ALU.is_gt, ALU.mult), (kt,), (ko,))
        for xv, o, ov, t0, t0v, t1, t2, t2v, sc0, sc1, rb, sb, kx, ko, kt, \
                ks in js:
            P.run(AE, lambda e, t2=t2, o=o: e.tensor_add(t2, t2, o),
                  (kt, ko), (kt,))
    for xv, o, ov, t0, t0v, t1, t2, t2v, sc0, sc1, rb, sb, kx, ko, kt, ks in js:
        P.run("v", lambda e, t1=t1, t0=t0: e.tensor_scalar(
            t1, t0, 0.0, 2.0, ALU.is_ge, ALU.mult), (kt,), (kt,))
    for xv, o, ov, t0, t0v, t1, t2, t2v, sc0, sc1, rb, sb, kx, ko, kt, ks in js:
        P.run("v", lambda e, t1=t1: e.tensor_single_scalar(
            t1, t1, -1.0, ALU.add), (kt,), (kt,))
    for xv, o, ov, t0, t0v, t1, t2, t2v, sc0, sc1, rb, sb, kx, ko, kt, ks in js:
        P.run("v", lambda e, t2=t2, t1=t1: e.tensor_mul(t2, t2, t1),
              (kt,), (kt,))
    for xv, o, ov, t0, t0v, t1, t2, t2v, sc0, sc1, rb, sb, kx, ko, kt, ks in js:
        P.run("v", lambda e, ov=ov, t2v=t2v, sb=sb: e.tensor_tensor(
            ov, t2v, sb, ALU.mult), (kt, ks), (ko,))


def quant_tok(P, x_ap, out_ap, t0, t1, t2, sc0, sc1, kx, kout, kt, ksc):
    W = x_ap.shape[-1]
    nb = W // 16
    xv = x_ap.rearrange("p (s n) -> p s n", n=16)
    P.run("v", lambda e: e.tensor_reduce(sc0, xv, AX.X, ALU.max,
                                         apply_absolute_value=True),
          (kx,), (ksc,))
    P.run("v", lambda e: e.tensor_scalar(sc1, sc0, 1.0 / 6.0, 1e-12,
                                         ALU.mult, ALU.max), (ksc,), (ksc,))
    P.run("v", lambda e: e.reciprocal(sc0, sc1), (ksc,), (ksc,))
    recb = sc0.rearrange("p (s o) -> p s o", o=1).to_broadcast([128, nb, 16])
    sclb = sc1.rearrange("p (s o) -> p s o", o=1).to_broadcast([128, nb, 16])
    t0v = t0.rearrange("p (s n) -> p s n", n=16)
    P.run("v", lambda e: e.tensor_tensor(t0v, xv, recb, ALU.mult),
          (kx, ksc), (kt,))
    P.run("v", lambda e: e.scalar_tensor_tensor(t1, t0, -1.0, t0, ALU.mult,
                                                ALU.max), (kt,), (kt,))
    m0, i0 = MIDS[0]
    P.run("v", lambda e: e.tensor_scalar(t2, t1, m0, i0, ALU.is_gt,
                                         ALU.mult), (kt,), (kt,))
    for m_i, inc in MIDS[1:]:
        P.run("v", lambda e, m_i=m_i, inc=inc: e.tensor_scalar(
            out_ap, t1, m_i, inc, ALU.is_gt, ALU.mult), (kt,), (kout,))
        P.run("v", lambda e: e.tensor_add(t2, t2, out_ap), (kt, kout), (kt,))
    P.run("v", lambda e: e.tensor_scalar(t1, t0, 0.0, 2.0, ALU.is_ge,
                                         ALU.mult), (kt,), (kt,))
    P.run("v", lambda e: e.tensor_single_scalar(t1, t1, -1.0, ALU.add),
          (kt,), (kt,))
    P.run("v", lambda e: e.tensor_mul(t2, t2, t1), (kt,), (kt,))
    t2v = t2.rearrange("p (s n) -> p s n", n=16)
    ov = out_ap.rearrange("p (s n) -> p s n", n=16)
    P.run("v", lambda e: e.tensor_tensor(ov, t2v, sclb, ALU.mult),
          (kt, ksc), (kout,))


def xbar_tok2fm(P, tok_ap, fm_ap, kin, kout, kd, tb_blocks):
    W = kd * 128
    for tb in range(tb_blocks):
        out = fm_ap.rearrange("p (k b t) -> p b k t", k=kd, b=tb_blocks,
                              t=128)[:, tb:tb + 1]
        P.dma("q", out, tok_ap[:, tb * W:(tb + 1) * W], reads=(kin,),
              writes=(kout,), transpose=True)


def build_nc():
    nc = bass.Bass("TRN2", target_bir_lowering=False)
    NCH = SEQ // T

    def din(name, shape):
        return nc.dram_tensor(name, shape, F32, kind="ExternalInput").ap()

    def dscr(name, shape, dtype=BF16):
        kind = "ExternalOutput" if name in DEBUG_SET else "Internal"
        return nc.dram_tensor(name, shape, dtype, kind=kind).ap()

    xt_h = din("xt", [SEQ, D])
    w1_h = din("w1", [D, D])
    v1_h = din("v1", [D, D])
    dt_h = din("dtw", [D, D])
    w2_h = din("w2", [D, D])
    bcw_h = din("bcw", [D, 2 * NST])
    ff1_h = din("ff1", [D, DFF])
    ff2_h = din("ff2", [DFF, D])
    wg_h = din("wg", [DI, D])
    wu_h = din("wu", [DI, D])
    wdn_h = din("wd", [D, DI])
    cols_h = din("cols", [128, 4 * KD])
    ff1b_h = din("ff1b", [128, KFF])
    alog_h = din("alog", [128, KD * NST])
    conv_h = din("convw", [128, KD * 3])
    rows_h = din("rows", [1, 8 * D])
    emat_h = din("emat", [2 * NST, 2 * NST * 128])
    bcbc_h = din("bcbc", [2 * NST, 1])
    out_h = nc.dram_tensor("out", [SEQ, D], F32, kind="ExternalOutput").ap()

    x1_s = dscr("x1s", [KD, 128, SEQ], F32)
    v_s = dscr("vss", [KD, 128, SEQ], F32)
    xc_s = dscr("xcs", [KD, 128, SEQ], F32)
    dr_s = dscr("drs", [KD, 128, SEQ], F32)
    bc_s = dscr("bcs", [2 * NST, SEQ], F32)
    z_s = dscr("zs", [KD, 128, SEQ])
    r2x_s = dscr("r2xs", [SEQ, D], F32)
    h1_s = dscr("h1s", [KFF, 128, SEQ])
    x3_s = dscr("x3s", [SEQ, D], F32)
    xq_s = dscr("xqs", [KD, 128, SEQ])
    hq_s = dscr("hqs", [KI, 128, SEQ])
    dbg = {}
    for nm, shp in (("dbg_xn", [SEQ, D]), ("dbg_xm", [SEQ, D]),
                    ("dbg_grep", [128, 6 * D]), ("dbg_mo", [SEQ, D])):
        if nm in DEBUG_SET:
            dbg[nm] = dscr(nm, shp)

    import contextlib
    with contextlib.ExitStack() as _st:
        en = _st.enter_context
        ones_row = en(nc.sbuf_tensor("ones_row", [1, T], BF16))
        rows2 = en(nc.sbuf_tensor("rows2", [1, 8 * D], BF16))
        e_mat = en(nc.sbuf_tensor("e_mat", [2 * NST, 2 * NST * 128], BF16))
        cols = en(nc.sbuf_tensor("colsb", [128, 4 * KD], F32))
        ff1b = en(nc.sbuf_tensor("ff1bb", [128, KFF], F32))
        acoef = en(nc.sbuf_tensor("acoef", [128, KD * NST], F32))
        convw = en(nc.sbuf_tensor("convwb", [128, KD * 3], F32))
        grep = en(nc.sbuf_tensor("grep", [128, 6 * D], BF16))
        grep2 = en(nc.sbuf_tensor("grep2", [128, 2 * D], BF16))
        bcbc = en(nc.sbuf_tensor("bcbcb", [2 * NST, 1], F32))
        h0 = en(nc.sbuf_tensor("h0b", [128, KD * NST], F32))
        inj = en(nc.sbuf_tensor("injb", [128, NST], F32))
        epsc = en(nc.sbuf_tensor("epscb", [128, 1], F32))
        arena_t = en(nc.sbuf_tensor("arena", [128, AR_BYTES // 2], BF16))
        pb0 = en(nc.psum_tensor("pb0", [128, 512], F32))
        pb1 = en(nc.psum_tensor("pb1", [128, 512], F32))
        pb2 = en(nc.psum_tensor("pb2", [128, 512], F32))
        pb3 = en(nc.psum_tensor("pb3", [128, 512], F32))
        pw0 = en(nc.psum_tensor("pw0", [128, 1024], F32))
        pw1 = en(nc.psum_tensor("pw1", [128, 1024], F32))
        St = en(nc.semaphore("St"))
        Sv = en(nc.semaphore("Sv"))
        Ss = en(nc.semaphore("Ss"))
        Sg = en(nc.semaphore("Sg"))
        Sq = en(nc.semaphore("Sq"))
        Eq = en(nc.semaphore("Eq"))
        Eg = en(nc.semaphore("Eg"))
        pb = [pb0, pb1, pb2, pb3]
        pw = [pw0, pw1]
        P = Prog(nc, {"t": St, "v": Sv, "s": Ss, "g": Sg, "q": Sq},
                 {"q": Eq, "g": Eg})
        AR = Arena(arena_t[:], arena_t[:].bitcast(F32))

        P.dma("g", rows2[:], rows_h[:, :], writes=("rows2",))
        P.dma("g", e_mat[:], emat_h[:, :], writes=("emat",))
        P.dma("q", bcbc[:], bcbc_h[:, :], writes=("bcbc",))
        P.dma("g", cols[:], cols_h[:, :], writes=("cols",))
        P.dma("g", ff1b[:], ff1b_h[:, :], writes=("ff1b",))
        P.dma("g", acoef[:], alog_h[:, :], writes=("acoef",))
        P.dma("g", convw[:], conv_h[:, :], writes=("convw",))
        P.run("v", lambda e: e.memset(ones_row[:], 1.0), (), ("ones",))
        P.run("s", lambda e: e.activation(acoef[:], acoef[:], AF.Exp),
              ("acoef",), ("acoef",))
        P.run("v", lambda e: e.tensor_single_scalar(acoef[:], acoef[:], -1.0,
                                                    ALU.mult),
              ("acoef",), ("acoef",))
        P.run("v", lambda e: e.memset(h0[:], 0.0), (), ("h0",))
        P.run("v", lambda e: e.memset(epsc[:], EPS), (), ("epsc",))
        for j in range(8):
            P.mm_group(
                [(lambda e, j=j, h=h: e.matmul(
                    pw[0][:, h * 512:(h + 1) * 512], ones_row[:1, 0:128],
                    rows2[:1, j * D + h * 512: j * D + (h + 1) * 512],
                    start=True, stop=True)) for h in range(2)],
                ("rows2", "ones"), ("pw0",))
            dstt = grep if j < 6 else grep2
            dk = "grep" if j < 6 else "grep2"
            jj = j if j < 6 else j - 6
            P.run("v", lambda e, jj=jj, dstt=dstt: e.tensor_copy(
                dstt[:, jj * D:(jj + 1) * D], pw[0][:]), ("pw0",), (dk,))
        if "dbg_grep" in dbg:
            P.dma("q", dbg["dbg_grep"][:, :], grep[:], reads=("grep",))
        P.barrier()

        def _phase0():
            AR.reset()
            wa = AR.bf(KD * D)
            wb = AR.bf(KD * D)
            act0, act0_b = AR.fp(TB * D)
            act1 = AR.f(TB * D)
            act2 = AR.f(TB * D)
            xmb = AR.bf(TB * D)
            tmpq = AR.bf(TB * D)
            fmx = AR.bf(KD * T)
            ev0, ev0_b = AR.fp(KD * T)
            ev1, ev1_b = AR.fp(KD * T)
            stat = AR.f(16)
            for k in range(KD if "p1a" in PHASES else 0):
                P.dma("g", wa[:, k * D:(k + 1) * D],
                      w1_h[k * 128:(k + 1) * 128, :], writes=("wa",))
                P.dma("g", wb[:, k * D:(k + 1) * D],
                      v1_h[k * 128:(k + 1) * 128, :], writes=("wb",))
            for c in range(NCH if "p1a" in PHASES else 0):
                P.dma("g", act0_b.rearrange("p (b f) -> p b f", f=2 * D),
                      xt_h.bitcast(U16)[c * T:(c + 1) * T, :].rearrange(
                          "(b p) f -> p b f", p=128),
                      writes=("act0",))
                ln_tok_multi(P, [
                    (act0[:, tb * D:(tb + 1) * D],
                     act1[:, tb * D:(tb + 1) * D],
                     tmpq[:, tb * D:(tb + 1) * D],
                     stat[:, tb * 4:(tb + 1) * 4],
                     "act0", "act1", f"tmpq{tb}", f"stat{tb}",
                     grep[:, 0:D], grep[:, D:2 * D], "grep")
                    for tb in range(TB)], epsc[:])
                ln_tok_multi(P, [
                    (act1[:, tb * D:(tb + 1) * D],
                     act2[:, tb * D:(tb + 1) * D],
                     tmpq[:, tb * D:(tb + 1) * D],
                     stat[:, tb * 4:(tb + 1) * 4],
                     "act1", "act2", f"tmpq{tb}", f"stat{tb}",
                     grep[:, 2 * D:3 * D], grep[:, 3 * D:4 * D], "grep")
                    for tb in range(TB)], epsc[:])
                if "dbg_xn" in dbg:
                    P.dma("q", dbg["dbg_xn"][c * T:(c + 1) * T, :].rearrange(
                        "(b p) f -> p b f", p=128),
                        act1.rearrange("p (b f) -> p b f", f=D),
                        reads=("act1",))
                if "dbg_xm" in dbg:
                    P.dma("q", dbg["dbg_xm"][c * T:(c + 1) * T, :].rearrange(
                        "(b p) f -> p b f", p=128),
                        act2.rearrange("p (b f) -> p b f", f=D),
                        reads=("act2",))
                P.run("v", lambda e: e.tensor_copy(xmb, act2),
                      ("act2",), ("xmb",))
                xbar_tok2fm(P, xmb, fmx, "xmb", "fmx", KD, TB)
                for m in range(KD):
                    p = pb[m % 2]
                    kpb = f"pb{m % 2}"
                    P.mm_group(
                        [(lambda e, k=k, m=m, p=p: e.matmul(
                            p[:], wa[:, k * D + m * 128: k * D + (m + 1) * 128],
                            fmx[:, k * T:(k + 1) * T],
                            start=(k == 0), stop=(k == KD - 1)))
                         for k in range(KD)],
                        ("wa", "fmx"), (kpb,))
                    P.run("s", lambda e, m=m, p=p: e.activation(
                        ev0[:, m * T:(m + 1) * T], p[:], AF.Identity,
                        bias=cols[:, m:m + 1]),
                        (kpb, "cols"), (f"ev0{m % 2}",))
                    P.dma("q", x1_s.bitcast(U16)[m, :,
                          2 * c * T:2 * (c + 1) * T],
                          ev0_b[:, 2 * m * T:2 * (m + 1) * T],
                          reads=(f"ev0{m % 2}",))
                for m in range(KD):
                    p = pb[2 + m % 2]
                    kpb = f"pb{2 + m % 2}"
                    P.mm_group(
                        [(lambda e, k=k, m=m, p=p: e.matmul(
                            p[:], wb[:, k * D + m * 128: k * D + (m + 1) * 128],
                            fmx[:, k * T:(k + 1) * T],
                            start=(k == 0), stop=(k == KD - 1)))
                         for k in range(KD)],
                        ("wb", "fmx"), (kpb,))
                    P.run("s", lambda e, m=m, p=p: e.activation(
                        ev1[:, m * T:(m + 1) * T], p[:], AF.Silu,
                        bias=cols[:, KD + m:KD + m + 1]),
                        (kpb, "cols"), (f"ev1{m % 2}",))
                    P.dma("q", v_s.bitcast(U16)[m, :,
                          2 * c * T:2 * (c + 1) * T],
                          ev1_b[:, 2 * m * T:2 * (m + 1) * T],
                          reads=(f"ev1{m % 2}",))
            P.barrier()
        _phase0()

        def _phase1():
            AR.reset()
            wa = AR.bf(KD * D)
            bcw_sb = AR.bf(KD * 2 * NST)
            TH = T + 2
            x1h, x1h_b = AR.fp(KD * TH)
            xcb, xcb_b2 = AR.fp(KD * T)
            xcbb = AR.bf(KD * T)
            evd, evd_b = AR.fp(KD * T)
            bcf, bcf_b = AR.fp(T, parts=2 * NST)
            for k in range(KD if "p1b" in PHASES else 0):
                P.dma("g", wa[:, k * D:(k + 1) * D],
                      dt_h[k * 128:(k + 1) * 128, :], writes=("wa",))
                P.dma("g", bcw_sb[:, k * 2 * NST:(k + 1) * 2 * NST],
                      bcw_h[k * 128:(k + 1) * 128, :], writes=("bcw",))
            for c in range(NCH if "p1b" in PHASES else 0):
                for k in range(KD):
                    xk = x1h[:, k * TH:(k + 1) * TH]
                    lo = c * T - 1
                    hi = (c + 1) * T + 1
                    sl = 0
                    if c == 0:
                        P.run("v", lambda e, xk=xk: e.memset(xk[:, 0:1], 0.0),
                              ("x1h",), ("x1h",))
                        lo, sl = 0, 1
                    if c == NCH - 1:
                        P.run("v", lambda e, xk=xk: e.memset(
                            xk[:, TH - 1:TH], 0.0), ("x1h",), ("x1h",))
                        hi = SEQ
                    P.dma("g", x1h_b[:, 2 * (k * TH + sl):
                          2 * (k * TH + sl) + 2 * (hi - lo)],
                          x1_s.bitcast(U16)[k, :, 2 * lo:2 * hi],
                          writes=("x1h",))
                for k in range(KD):
                    xk = x1h[:, k * TH:(k + 1) * TH]
                    ck = xcb[:, k * T:(k + 1) * T]
                    P.run("v", lambda e, xk=xk, ck=ck, k=k:
                          e.tensor_single_scalar(
                              ck, xk[:, 0:T], convw[:, 3 * k:3 * k + 1],
                              ALU.mult),
                          ("x1h", "convw"), ("xcb",))
                    for tap in (1, 2):
                        P.run("v", lambda e, xk=xk, ck=ck, k=k, tap=tap:
                              e.scalar_tensor_tensor(
                                  ck, xk[:, tap:tap + T],
                                  convw[:, 3 * k + tap:3 * k + tap + 1],
                                  ck, ALU.mult, ALU.add),
                              ("x1h", "convw", "xcb"), ("xcb",))
                    P.run("s", lambda e, ck=ck: e.activation(ck, ck, AF.Silu),
                          ("xcb",), ("xcb",))
                    P.run("v", lambda e, ck=ck, k=k: e.tensor_copy(
                        xcbb[:, k * T:(k + 1) * T], ck),
                        ("xcb",), ("xcbb",))
                    P.dma("q", xc_s.bitcast(U16)[k, :,
                          2 * c * T:2 * (c + 1) * T],
                          xcb_b2[:, 2 * k * T:2 * (k + 1) * T],
                          reads=("xcb",))
                for m in range(KD):
                    p = pb[m % 2]
                    kpb = f"pb{m % 2}"
                    P.mm_group(
                        [(lambda e, k=k, m=m, p=p: e.matmul(
                            p[:], wa[:, k * D + m * 128: k * D + (m + 1) * 128],
                            xcbb[:, k * T:(k + 1) * T],
                            start=(k == 0), stop=(k == KD - 1)))
                         for k in range(KD)],
                        ("wa", "xcbb"), (kpb,))
                    P.run("s", lambda e, m=m, p=p: e.activation(
                        evd[:, m * T:(m + 1) * T], p[:], AF.Identity,
                        bias=cols[:, 2 * KD + m:2 * KD + m + 1]),
                        (kpb, "cols"), ("evd",))
                    P.dma("q", dr_s.bitcast(U16)[m, :,
                          2 * c * T:2 * (c + 1) * T],
                          evd_b[:, 2 * m * T:2 * (m + 1) * T],
                          reads=("evd",))
                P.mm_group(
                    [(lambda e, k=k: e.matmul(
                        pb[2][0:2 * NST, :],
                        bcw_sb[:, k * 2 * NST:(k + 1) * 2 * NST],
                        xcbb[:, k * T:(k + 1) * T],
                        start=(k == 0), stop=(k == KD - 1)))
                     for k in range(KD)],
                    ("bcw", "xcbb"), ("pb2",))
                P.run("s", lambda e: e.activation(bcf, pb[2][0:2 * NST, :],
                                                  AF.Identity, bias=bcbc[:]),
                      ("pb2", "bcbc"), ("bcf",))
                P.dma("q", bc_s.bitcast(U16)[:, 2 * c * T:2 * (c + 1) * T],
                  bcf_b, reads=("bcf",))
            P.barrier()
        _phase1()

        def _phase2():
            AR.reset()
            fmA, fmA_b = AR.fp(KD * T)
            fmB, fmB_b = AR.fp(KD * T)
            fmC = AR.bf(KD * T)
            fmD = AR.bf(KD * T)
            fmZ = fmC
            big0 = [AR.bf(NST * T), AR.bf(NST * T)]
            big1 = [AR.bf(NST * T), AR.bf(NST * T)]
            big2 = AR.bf(NST * T)
            brep = AR.bf(NST * T)
            crep = AR.bf(NST * T)
            bcch = AR.bf(T, parts=2 * NST)
            for c in range(NCH if "p2" in PHASES else 0):
                for k in range(KD):
                    P.dma("g", fmA_b[:, 2 * k * T:2 * (k + 1) * T],
                          dr_s.bitcast(U16)[k, :,
                          2 * c * T:2 * (c + 1) * T], writes=("fmA",))
                    P.dma("g", fmB_b[:, 2 * k * T:2 * (k + 1) * T],
                          xc_s.bitcast(U16)[k, :, 2 * c * T:2 * (c + 1) * T],
                          writes=("fmB",))
                    P.dma("g", fmD[:, k * T:(k + 1) * T],
                          v_s[k, :, c * T:(c + 1) * T], writes=("fmD",))
                P.dma("g", bcch, bc_s[:, c * T:(c + 1) * T], writes=("bcch",))
                for n in range(NST):
                    for half, dst, kdst, ev_eng in ((0, brep, "brep", "v"),
                                                    (1, crep, "crep", "s")):
                        p = pb[n % 2 + 2 * half]
                        kpb = f"pb{n % 2 + 2 * half}"
                        col = (half * NST + n) * 128
                        P.mm_group(
                            [(lambda e, col=col, p=p: e.matmul(
                                p[:], e_mat[:, col:col + 128], bcch,
                                start=True, stop=True))],
                            ("emat", "bcch"), (kpb,))
                        if ev_eng == "v":
                            P.run("v", lambda e, n=n, p=p, dst=dst:
                                  e.tensor_copy(dst[:, n * T:(n + 1) * T], p[:]),
                                  (kpb,), (kdst,))
                        else:
                            P.run("s", lambda e, n=n, p=p, dst=dst:
                                  e.activation(dst[:, n * T:(n + 1) * T], p[:],
                                               AF.Identity),
                                  (kpb,), (kdst,))
                for k in range(KD):
                    ak = fmA[:, k * T:(k + 1) * T]
                    ck = fmC[:, k * T:(k + 1) * T]
                    P.run("s", lambda e, ak=ak, ck=ck: e.activation(
                        ck, ak, AF.Abs), ("fmA",), ("fmC",))
                    P.run("s", lambda e, ck=ck: e.activation(
                        ck, ck, AF.Exp, scale=-1.0), ("fmC",), ("fmC",))
                    P.run("s", lambda e, ck=ck: e.activation(
                        ck, ck, AF.Ln, bias=1.0), ("fmC",), ("fmC",))
                    P.run("v", lambda e, ak=ak, ck=ck: e.scalar_tensor_tensor(
                        ak, ak, 0.0, ck, ALU.max, ALU.add),
                        ("fmA", "fmC"), ("fmA",))
                for k in range(KD):
                    ak = fmA[:, k * T:(k + 1) * T]
                    bk = fmB[:, k * T:(k + 1) * T]
                    ck = fmC[:, k * T:(k + 1) * T]
                    b0 = big0[k % 2]
                    b1 = big1[k % 2]
                    k0 = f"big0{k % 2}"
                    k1 = f"big1{k % 2}"
                    P.run("v", lambda e, ak=ak, bk=bk, ck=ck: e.tensor_mul(
                        ck, ak, bk), ("fmA", "fmB"), ("fmC",))
                    for n in range(NST):
                        P.run("s", lambda e, ak=ak, n=n, k=k, b0=b0:
                              e.activation(
                                  b0[:, n * T:(n + 1) * T], ak, AF.Exp,
                                  scale=acoef[:, k * NST + n:k * NST + n + 1]),
                              ("fmA", "acoef"), (k0,))
                    for n in range(NST):
                        P.run("g", lambda e, ck=ck, n=n, b1=b1: e.tensor_mul(
                            b1[:, n * T:(n + 1) * T], ck,
                            brep[:, n * T:(n + 1) * T]),
                            ("fmC", "brep"), (k1,))
                    vA = b0.rearrange("p (n t) -> p n t", t=T)[:, :, 0:1]
                    vB = b1.rearrange("p (n t) -> p n t", t=T)[:, :, 0:1]
                    h0k = h0[:, k * NST:(k + 1) * NST].rearrange(
                        "p (n o) -> p n o", o=1)
                    injv = inj[:].rearrange("p (n o) -> p n o", o=1)
                    P.run("v", lambda e, vA=vA, h0k=h0k, injv=injv:
                          e.tensor_tensor(injv, vA, h0k, ALU.mult),
                          (k0, "h0"), ("inj",))
                    P.run("v", lambda e, vB=vB, injv=injv: e.tensor_tensor(
                        vB, vB, injv, ALU.add), (k1, "inj"), (k1,))
                    P.run("v", lambda e, vA=vA: e.memset(vA, 0.0), (k0,), (k0,))
                    P.run("v", lambda e, b0=b0, b1=b1: e.tensor_tensor_scan(
                        big2, b0, b1, 0.0, ALU.mult, ALU.add),
                        (k0, k1), ("big2",))
                    vH = big2.rearrange("p (n t) -> p n t", t=T)[:, :, T - 1:T]
                    P.run("v", lambda e, vH=vH, h0k=h0k: e.tensor_copy(h0k, vH),
                          ("big2",), ("h0",))
                    P.run("v", lambda e: e.tensor_mul(big2, big2, crep),
                          ("big2", "crep"), ("big2",))
                    w = NST * T // 2
                    while w >= T:
                        P.run("v", lambda e, w=w: e.tensor_add(
                            big2[:, 0:w], big2[:, 0:w], big2[:, w:2 * w]),
                            ("big2",), ("big2",))
                        w //= 2
                    zk = fmZ[:, k * T:(k + 1) * T]
                    P.run("v", lambda e, zk=zk, bk=bk, k=k:
                          e.tensor_single_scalar(
                              zk, bk, cols[:, 3 * KD + k:3 * KD + k + 1],
                              ALU.mult),
                          ("fmB", "cols"), ("fmC",))
                    P.run("v", lambda e, zk=zk: e.tensor_add(
                        zk, zk, big2[:, 0:T]), ("fmC", "big2"), ("fmC",))
                    P.run("v", lambda e, zk=zk, k=k: e.tensor_mul(
                        zk, zk, fmD[:, k * T:(k + 1) * T]),
                        ("fmC", "fmD"), ("fmC",))
                    P.dma("q", z_s[k, :, c * T:(c + 1) * T], zk, reads=("fmC",))
            P.barrier()
        _phase2()

        def _phase3():
            AR.reset()
            wa = AR.bf(KD * D)
            fm0 = AR.bf(KD * T)
            act0, act0_b = AR.fp(TB * D)
            act1, act1_b = AR.fp(TB * D)
            for k in range(KD if "p3a" in PHASES else 0):
                P.dma("g", wa[:, k * D:(k + 1) * D],
                      w2_h[k * 128:(k + 1) * 128, :], writes=("wa",))
            for c in range(NCH if "p3a" in PHASES else 0):
                for k in range(KD):
                    P.dma("g", fm0[:, k * T:(k + 1) * T],
                          z_s[k, :, c * T:(c + 1) * T], writes=("fm0",))
                P.dma("g", act0_b.rearrange("p (b f) -> p b f", f=2 * D),
                      xt_h.bitcast(U16)[c * T:(c + 1) * T, :].rearrange(
                          "(b p) f -> p b f", p=128),
                      writes=("act0",))
                for tb in range(TB):
                    p = pw[tb % 2]
                    kpw = f"pw{tb % 2}"
                    fns = []
                    for h in range(2):
                        for k in range(KD):
                            fns.append(lambda e, k=k, h=h, tb=tb, p=p: e.matmul(
                                p[:, h * 512:(h + 1) * 512],
                                fm0[:, k * T + tb * 128: k * T + tb * 128 + 128],
                                wa[:, k * D + h * 512: k * D + (h + 1) * 512],
                                start=(k == 0), stop=(k == KD - 1)))
                    P.mm_group(fns, ("fm0", "wa"), (kpw,))
                    if "dbg_mo" in dbg:
                        P.run("s", lambda e, tb=tb, p=p: e.activation(
                            act1[:, tb * D:(tb + 1) * D], p[:], AF.Identity),
                            (kpw,), ("act1",))
                        P.dma("g", dbg["dbg_mo"][
                            c * T + tb * 128: c * T + (tb + 1) * 128, :],
                            act1[:, tb * D:(tb + 1) * D], reads=("act1",))
                    P.run("v", lambda e, tb=tb, p=p: e.tensor_add(
                        act1[:, tb * D:(tb + 1) * D], p[:], grep2[:, 0:D]),
                        (kpw, "grep2"), ("act1",))
                    P.run("v", lambda e, tb=tb: e.tensor_add(
                        act1[:, tb * D:(tb + 1) * D],
                        act1[:, tb * D:(tb + 1) * D],
                        act0[:, tb * D:(tb + 1) * D]),
                        ("act1", "act0"), ("act1",))
                P.dma("q",
                      r2x_s.bitcast(U16)[c * T:(c + 1) * T, :].rearrange(
                          "(b p) f -> p b f", p=128),
                      act1_b.rearrange("p (b f) -> p b f", f=2 * D),
                      reads=("act1",))
            P.barrier()
        _phase3()

        def _phase4():
            AR.reset()
            wc = AR.bf(KD * DFF)
            act0, act0_b = AR.fp(TB * D)
            act1 = AR.bf(TB * D)
            tmpq = AR.bf(TB * D)
            fm0 = AR.bf(KD * T)
            evg = AR.bf(4 * T)
            stat = AR.f(16)
            for k in range(KD if "p3b" in PHASES else 0):
                P.dma("g", wc[:, k * DFF:(k + 1) * DFF],
                      ff1_h[k * 128:(k + 1) * 128, :], writes=("wc",))
            for c in range(NCH if "p3b" in PHASES else 0):
                P.dma("g", act0_b.rearrange("p (b f) -> p b f", f=2 * D),
                      r2x_s.bitcast(U16)[c * T:(c + 1) * T, :].rearrange(
                          "(b p) f -> p b f", p=128),
                      writes=("act0",))
                ln_tok_multi(P, [
                    (act0[:, tb * D:(tb + 1) * D],
                     act1[:, tb * D:(tb + 1) * D],
                     tmpq[:, tb * D:(tb + 1) * D],
                     stat[:, tb * 4:(tb + 1) * 4],
                     "act0", "act1", f"tmpq{tb}", f"stat{tb}",
                     grep[:, 4 * D:5 * D], grep[:, 5 * D:6 * D], "grep")
                    for tb in range(TB)], epsc[:])
                xbar_tok2fm(P, act1, fm0, "act1", "fm0", KD, TB)
                for m in range(KFF):
                    p = pb[m % 2]
                    kpb = f"pb{m % 2}"
                    P.mm_group(
                        [(lambda e, k=k, m=m, p=p: e.matmul(
                            p[:],
                            wc[:, k * DFF + m * 128: k * DFF + (m + 1) * 128],
                            fm0[:, k * T:(k + 1) * T],
                            start=(k == 0), stop=(k == KD - 1)))
                         for k in range(KD)],
                        ("wc", "fm0"), (kpb,))
                    P.run("s", lambda e, m=m, p=p: e.activation(
                        evg[:, (m % 4) * T:(m % 4 + 1) * T], p[:], AF.Gelu,
                        bias=ff1b[:, m:m + 1]),
                        (kpb, "ff1b"), (f"evg{m % 4}",))
                    P.dma("q", h1_s[m, :, c * T:(c + 1) * T],
                          evg[:, (m % 4) * T:(m % 4 + 1) * T],
                          reads=(f"evg{m % 4}",))
            P.barrier()
        _phase4()

        def _phase5():
            AR.reset()
            wd_sb = AR.bf(KFF * D)
            h1c = AR.bf(KFF * T)
            act0 = AR.bf(TB * D)
            act1, act1_b = AR.fp(TB * D)
            act2, act2_b = AR.fp(TB * D)
            tmpq = AR.bf(TB * D)
            tq0 = AR.f(2 * D)
            tq1 = AR.f(2 * D)
            tq2 = AR.bf(2 * D)
            sc0 = AR.f(2 * (D // 16))
            sc1 = AR.f(2 * (D // 16))
            xqb = AR.bf(TB * D)
            fmq = AR.bf(KD * T)
            stat = AR.f(16)
            for kc in range(KFF if "p3c" in PHASES else 0):
                P.dma("g", wd_sb[:, kc * D:(kc + 1) * D],
                      ff2_h[kc * 128:(kc + 1) * 128, :], writes=("wd",))
            for c in range(NCH if "p3c" in PHASES else 0):
                for kc in range(KFF):
                    P.dma("g", h1c[:, kc * T:(kc + 1) * T],
                          h1_s[kc, :, c * T:(c + 1) * T], writes=("h1c",))
                P.dma("g", act0.rearrange("p (b f) -> p b f", f=D),
                      xt_h[c * T:(c + 1) * T, :].rearrange(
                          "(b p) f -> p b f", p=128),
                      writes=("act0",))
                P.dma("g", act1_b.rearrange("p (b f) -> p b f", f=2 * D),
                      r2x_s.bitcast(U16)[c * T:(c + 1) * T, :].rearrange(
                          "(b p) f -> p b f", p=128),
                      writes=("act1",))
                for tb in range(TB):
                    p = pw[tb % 2]
                    kpw = f"pw{tb % 2}"
                    fns = []
                    for h in range(2):
                        for kc in range(KFF):
                            fns.append(lambda e, kc=kc, h=h, tb=tb, p=p:
                                       e.matmul(
                                p[:, h * 512:(h + 1) * 512],
                                h1c[:, kc * T + tb * 128:
                                    kc * T + tb * 128 + 128],
                                wd_sb[:, kc * D + h * 512:
                                      kc * D + (h + 1) * 512],
                                start=(kc == 0), stop=(kc == KFF - 1)))
                    P.mm_group(fns, ("h1c", "wd"), (kpw,))
                    P.run("v", lambda e, tb=tb, p=p: e.tensor_add(
                        act2[:, tb * D:(tb + 1) * D], p[:], grep2[:, D:2 * D]),
                        (kpw, "grep2"), ("act2",))
                    P.run("v", lambda e, tb=tb: e.tensor_add(
                        act2[:, tb * D:(tb + 1) * D],
                        act2[:, tb * D:(tb + 1) * D],
                        act1[:, tb * D:(tb + 1) * D]),
                        ("act2", "act1"), ("act2",))
                    P.run("v", lambda e, tb=tb: e.tensor_add(
                        act2[:, tb * D:(tb + 1) * D],
                        act2[:, tb * D:(tb + 1) * D],
                        act0[:, tb * D:(tb + 1) * D]),
                        ("act2", "act0"), ("act2",))
                ln_tok_multi(P, [
                    (act2[:, tb * D:(tb + 1) * D],
                     act2[:, tb * D:(tb + 1) * D],
                     tmpq[:, tb * D:(tb + 1) * D],
                     stat[:, tb * 4:(tb + 1) * 4],
                     "act2", "act2", f"tmpq{tb}", f"stat{tb}",
                     None, None, None)
                    for tb in range(TB)], epsc[:])
                P.dma("q",
                      x3_s.bitcast(U16)[c * T:(c + 1) * T, :].rearrange(
                          "(b p) f -> p b f", p=128),
                      act2_b.rearrange("p (b f) -> p b f", f=2 * D),
                      reads=("act2",))
                for j in range(2):
                    quant_tok_multi(P, [
                        (act2[:, tb * D:(tb + 1) * D],
                         xqb[:, tb * D:(tb + 1) * D],
                         tq0[:, i * D:(i + 1) * D],
                         tq1[:, i * D:(i + 1) * D],
                         tq2[:, i * D:(i + 1) * D],
                         sc0[:, i * (D // 16):(i + 1) * (D // 16)],
                         sc1[:, i * (D // 16):(i + 1) * (D // 16)],
                         "act2", "xqb", f"tq{i}", f"sc{i}")
                        for i, tb in enumerate((2 * j, 2 * j + 1))],
                        g_adds=True)
                xbar_tok2fm(P, xqb, fmq, "xqb", "fmq", KD, TB)
                for k in range(KD):
                    P.dma("q", xq_s[k, :, c * T:(c + 1) * T],
                          fmq[:, k * T:(k + 1) * T], reads=("fmq",))
            P.barrier()
        _phase5()

        def _phase6():
            NSL = [(i * 512, 512) for i in range(5)] + [(2560, 256)]
            AR.reset()
            wgq = AR.bf(KD * DI)
            wuq = AR.bf(KD * DI)
            arsave = AR.off
            wtmp, wtmp_b = AR.fp(D)
            qt0 = AR.f(D)
            qt1 = AR.f(D)
            qt2 = AR.bf(D)
            qq = AR.bf(D)
            sc0 = AR.f(D // 16)
            sc1 = AR.f(D // 16)
            for src_h, dst, kdst in (((wg_h, wgq, "wgq"), (wu_h, wuq, "wuq"))
                                     if "p4w" in PHASES else ()):
                for nt in range(KI):
                    P.dma("g", wtmp_b,
                          src_h.bitcast(U16)[nt * 128:(nt + 1) * 128, :],
                          writes=("wtmp",))
                    quant_tok(P, wtmp, qq, qt0, qt1, qt2, sc0, sc1,
                              "wtmp", "qq", "qt", "qsc")
                    out = dst.rearrange("p (kc n) -> p kc n",
                                        kc=KD)[:, :, nt * 128:(nt + 1) * 128]
                    P.dma("q", out, qq, reads=("qq",), writes=(kdst,),
                          transpose=True)
            P.barrier()
            AR.off = arsave
            xqc = AR.bf(KD * T)
            sg = AR.bf(512)
            hband = AR.f(DI)
            hqb = AR.bf(DI)
            qt0 = AR.f(DI)
            qt1 = AR.f(DI)
            qt2 = AR.bf(DI)
            sc0 = AR.f(DI // 16)
            sc1 = AR.f(DI // 16)
            hqfm = AR.bf(KI * T)
            for c in range(NCH if "p4b" in PHASES else 0):
                for k in range(KD):
                    P.dma("g", xqc[:, k * T:(k + 1) * T],
                          xq_s[k, :, c * T:(c + 1) * T], writes=("xqc",))
                for tb in range(TB):
                    for off, nsz in NSL:
                        fg = [(lambda e, k=k, tb=tb, off=off, nsz=nsz: e.matmul(
                            pb[0][:, 0:nsz],
                            xqc[:, k * T + tb * 128: k * T + tb * 128 + 128],
                            wgq[:, k * DI + off: k * DI + off + nsz],
                            start=(k == 0), stop=(k == KD - 1)))
                            for k in range(KD)]
                        fu = [(lambda e, k=k, tb=tb, off=off, nsz=nsz: e.matmul(
                            pb[1][:, 0:nsz],
                            xqc[:, k * T + tb * 128: k * T + tb * 128 + 128],
                            wuq[:, k * DI + off: k * DI + off + nsz],
                            start=(k == 0), stop=(k == KD - 1)))
                            for k in range(KD)]
                        P.mm_group(fg, ("xqc", "wgq"), ("pb0",))
                        P.mm_group(fu, ("xqc", "wuq"), ("pb1",))
                        P.run("s", lambda e, nsz=nsz: e.activation(
                            sg[:, 0:nsz], pb[0][:, 0:nsz], AF.Silu),
                            ("pb0",), ("sg",))
                        P.run("v", lambda e, off=off, nsz=nsz: e.tensor_tensor(
                            hband[:, off:off + nsz], sg[:, 0:nsz],
                            pb[1][:, 0:nsz], ALU.mult),
                            ("sg", "pb1"), ("hband",))
                    quant_tok_multi(P, [
                        (hband, hqb, qt0, qt1, qt2, sc0, sc1,
                         "hband", "hqb", "hqt", "hsc")], g_adds=True)
                    out = hqfm.rearrange("p (kc b t) -> p b kc t", kc=KI,
                                         b=TB, t=128)[:, tb:tb + 1]
                    P.dma("q", out, hqb, reads=("hqb",), writes=("hqfm",),
                          transpose=True)
                for kc in range(KI):
                    P.dma("q", hq_s[kc, :, c * T:(c + 1) * T],
                          hqfm[:, kc * T:(kc + 1) * T], reads=("hqfm",))
            P.barrier()
        _phase6()

        def _phase7():
            AR.reset()
            wdq = AR.bf(KI * D)
            arsave = AR.off
            wtmp, wtmp_b = AR.fp(DI)
            qt0 = AR.f(DI)
            qt1 = AR.f(DI)
            qt2 = AR.bf(DI)
            qq = AR.bf(DI)
            sc0 = AR.f(DI // 16)
            sc1 = AR.f(DI // 16)
            for k in range(KD if "p4dw" in PHASES else 0):
                P.dma("g", wtmp_b,
                      wdn_h.bitcast(U16)[k * 128:(k + 1) * 128, :],
                      writes=("wtmp",))
                quant_tok(P, wtmp, qq, qt0, qt1, qt2, sc0, sc1,
                          "wtmp", "qq", "qt", "qsc")
                out = wdq.rearrange("p (kc n) -> p kc n",
                                    kc=KI)[:, :, k * 128:(k + 1) * 128]
                P.dma("q", out, qq, reads=("qq",), writes=("wdq",),
                      transpose=True)
            P.barrier()
            AR.off = arsave
            hqc = AR.bf(KI * T)
            act0, act0_b = AR.fp(TB * D)
            act1 = AR.f(TB * D)
            tmpq = AR.bf(TB * D)
            outb, outb_b = AR.fp(TB * D)
            stat = AR.f(16)
            for c in range(NCH if "p4c" in PHASES else 0):
                for kc in range(KI):
                    P.dma("g", hqc[:, kc * T:(kc + 1) * T],
                          hq_s[kc, :, c * T:(c + 1) * T], writes=("hqc",))
                P.dma("g", act0_b.rearrange("p (b f) -> p b f", f=2 * D),
                      x3_s.bitcast(U16)[c * T:(c + 1) * T, :].rearrange(
                          "(b p) f -> p b f", p=128),
                      writes=("act0",))
                for tb in range(TB):
                    p = pw[tb % 2]
                    kpw = f"pw{tb % 2}"
                    fns = []
                    for h in range(2):
                        for kc in range(KI):
                            fns.append(lambda e, kc=kc, h=h, tb=tb, p=p:
                                       e.matmul(
                                p[:, h * 512:(h + 1) * 512],
                                hqc[:, kc * T + tb * 128:
                                    kc * T + tb * 128 + 128],
                                wdq[:, kc * D + h * 512: kc * D + (h + 1) * 512],
                                start=(kc == 0), stop=(kc == KI - 1)))
                    P.mm_group(fns, ("hqc", "wdq"), (kpw,))
                    P.run("v", lambda e, tb=tb, p=p: e.tensor_add(
                        act1[:, tb * D:(tb + 1) * D], p[:],
                        act0[:, tb * D:(tb + 1) * D]),
                        (kpw, "act0"), ("act1",))
                ln_tok_multi(P, [
                    (act1[:, tb * D:(tb + 1) * D],
                     outb[:, tb * D:(tb + 1) * D],
                     tmpq[:, tb * D:(tb + 1) * D],
                     stat[:, tb * 4:(tb + 1) * 4],
                     "act1", "outb", f"tmpq{tb}", f"stat{tb}",
                     None, None, None)
                    for tb in range(TB)], epsc[:])
                P.dma("q",
                      out_h.bitcast(U16)[c * T:(c + 1) * T, :].rearrange(
                          "(b p) f -> p b f", p=128),
                      outb_b.rearrange("p (b f) -> p b f", f=2 * D),
                      reads=("outb",))
            P.barrier()
        _phase7()

        with nc.Block() as block:
            for ename, reg in (("tensor", "t"), ("vector", "v"),
                               ("scalar", "s"), ("gpsimd", "g"),
                               ("sync", "q")):
                def mk(reg):
                    def body(eng):
                        for th in P.thunks[reg]:
                            th(eng)
                    return body
                getattr(block, ename)(mk(reg))

    return nc


def _fmcols(vec, kd=KD):
    return np.ascontiguousarray(vec.reshape(kd, 128).T)


def kernel(**inputs):
    g = {k.replace("L0_attn_", "").replace("L0_ffn_", "f_"): np.asarray(v)
         for k, v in inputs.items()}
    x = np.ascontiguousarray(g["x"], dtype=np.float32)
    B = x.shape[0]

    cols = np.concatenate(
        [_fmcols(g["w1_b"]), _fmcols(g["v1_b"]), _fmcols(g["dt_b"]),
         _fmcols(g["D_p"])], axis=1).astype(np.float32)
    rows = np.concatenate(
        [g["ln1_g"], g["ln1_b"], g["ln_g"], g["ln_b"], g["ln2_g"],
         g["ln2_b"], g["w2_b"], g["ff2_b"]]).reshape(1, 8 * D).astype(
             np.float32)
    alog = np.ascontiguousarray(
        g["A_log"].reshape(KD, 128, NST).transpose(1, 0, 2).reshape(
            128, KD * NST)).astype(np.float32)
    convw = np.ascontiguousarray(
        g["conv_w"].reshape(D, 3).reshape(KD, 128, 3).transpose(
            1, 0, 2).reshape(128, KD * 3)).astype(np.float32)
    ff1b = _fmcols(g["ff1_b"], KFF).astype(np.float32)
    bcw = np.concatenate([g["Bp_w"], g["Cp_w"]], axis=1).astype(np.float32)
    bcb = np.concatenate([g["Bp_b"], g["Cp_b"]]).astype(np.float32)
    emat = np.eye(2 * NST, dtype=np.float32)[:, :, None].repeat(
        128, axis=2).reshape(2 * NST, 2 * NST * 128)

    shared = {
        "w1": g["w1_w"], "v1": g["v1_w"], "dtw": g["dt_w"], "w2": g["w2_w"],
        "bcw": bcw, "ff1": g["ff1_w"], "ff2": g["ff2_w"],
        "wg": g["f_w_gate"], "wu": g["f_w_up"], "wd": g["f_w_down"],
        "cols": cols, "ff1b": ff1b, "alog": alog, "convw": convw,
        "rows": rows, "emat": emat,
        "bcbc": bcb.reshape(2 * NST, 1),
    }
    shared = {k: np.ascontiguousarray(v, dtype=np.float32)
              for k, v in shared.items()}

    nc = build_nc()
    in_maps = [dict(shared, xt=np.ascontiguousarray(x[b])) for b in range(B)]
    res = run_bass_kernel_spmd(nc, in_maps, core_ids=list(range(B)))
    out = np.stack([np.asarray(res.results[b]["out"]) for b in range(B)])
    return out.astype(np.float32)


if __name__ == "__main__":
    nc = build_nc()
    print("build ok; instructions:", sum(1 for _ in nc.all_instructions()))


# version 07
# speedup vs baseline: 1.1232x; 1.1232x over previous
import numpy as np

import concourse.bass as bass
import concourse.mybir as mybir
from concourse.bass_utils import run_bass_kernel_spmd

F32 = mybir.dt.float32
BF16 = mybir.dt.bfloat16
U16 = mybir.dt.uint16
ALU = mybir.AluOpType
AF = mybir.ActivationFunctionType
AX = mybir.AxisListType

D = 1024
KD = D // 128
NST = 16
DFF = 3072
KFF = DFF // 128
DI = 2816
KI = DI // 128
SEQ = 8192
T = 512
TB = T // 128
EPS = 1e-5
AR_BYTES = 165 * 1024

DEBUG_SET = set()
PHASES = {"prep", "p1a", "p1b", "p2", "p3a", "p3b", "p3c", "p4w", "p4b",
          "p4dw", "p4c"}


class Prog:
    CENG = ("t", "v", "s", "g", "q")
    FENCE_DIST = 2

    def __init__(self, nc, sems, dsems):
        self.nc = nc
        self.sem = sems
        self.dsem = dsems
        self.cnt = {e: 0 for e in self.CENG}
        self.dcnt = {"q": 0, "g": 0}
        self.wm = {e: {} for e in self.CENG}
        self.bufs = {}
        self.thunks = {e: [] for e in self.CENG}

    def _buf(self, key):
        return self.bufs.setdefault(key, {"w": None, "r": {}})

    def _sem_of(self, sk):
        return self.sem[sk] if isinstance(sk, str) else self.dsem[sk[1]]

    def _record_wait(self, e, sk, val):
        if val <= 0 or self.wm[e].get(sk, 0) >= val:
            return
        sem = self._sem_of(sk)
        self.thunks[e].append(
            lambda eng, sem=sem, val=val: eng.wait_ge(sem, val))
        self.wm[e][sk] = val

    def _dma_val(self, sk, c):
        if isinstance(sk, tuple):
            return self.dcnt[sk[1]]
        return c

    def _record_waits(self, e, reads, writes):
        need = {}
        for key in tuple(reads) + tuple(writes):
            b = self._buf(key)
            if b["w"] is not None:
                sk, c = b["w"]
                if sk != e:
                    need[sk] = max(need.get(sk, 0), self._dma_val(sk, c))
                elif e in self.cnt and c >= self.cnt[e] - self.FENCE_DIST:
                    need[sk] = max(need.get(sk, 0), c)
        for key in writes:
            for sk, c in self._buf(key)["r"].items():
                if sk != e:
                    need[sk] = max(need.get(sk, 0), self._dma_val(sk, c))
                elif e in self.cnt and c >= self.cnt[e] - self.FENCE_DIST:
                    need[sk] = max(need.get(sk, 0), c)
        for sk, c in sorted(need.items(), key=str):
            self._record_wait(e, sk, c)

    def _mark(self, sk_self, c, reads, writes):
        for key in writes:
            b = self._buf(key)
            b["w"] = (sk_self, c)
            b["r"] = {}
        for key in reads:
            self._buf(key)["r"][sk_self] = c

    def run(self, e, fn, reads=(), writes=()):
        self._record_waits(e, reads, writes)
        self.cnt[e] += 1
        sem = self.sem[e]

        def thunk(eng, fn=fn, sem=sem):
            fn(eng).then_inc(sem, 1)

        self.thunks[e].append(thunk)
        self._mark(e, self.cnt[e], reads, writes)

    def mm_group(self, fns, reads=(), writes=()):
        self._record_waits("t", reads, writes)
        self.cnt["t"] += 1
        sem = self.sem["t"]

        def thunk(eng, fns=tuple(fns), sem=sem):
            instr = None
            for f in fns:
                instr = f(eng)
            instr.then_inc(sem, 1)

        self.thunks["t"].append(thunk)
        self._mark("t", self.cnt["t"], reads, writes)

    def dma(self, qe, out, in_, reads=(), writes=(), transpose=False):
        self._record_waits(qe, reads, writes)
        self.dcnt[qe] += 16
        dsem = self.dsem[qe]

        def thunk(eng, out=out, in_=in_, t=transpose, dsem=dsem):
            if t:
                eng.dma_start(out=out, in_=in_, transpose=True).then_inc(
                    dsem, 16)
            else:
                eng.dma_start(out=out, in_=in_).then_inc(dsem, 16)

        self.thunks[qe].append(thunk)
        self._mark(("D", qe), self.dcnt[qe], reads, writes)

    def barrier(self):
        snap = dict(self.cnt)
        dsnap = dict(self.dcnt)
        for e in self.CENG:
            for f in self.CENG:
                if f != e:
                    self._record_wait(e, f, snap[f])
            for qd in ("q", "g"):
                self._record_wait(e, ("D", qd), dsnap[qd])
        self.bufs = {}


class Arena:

    def __init__(self, ar_bf16, ar_f32):
        self.b = ar_bf16
        self.f4 = ar_f32
        self.off = 0

    def reset(self):
        self.off = 0

    def _take(self, nbytes, align=128):
        o = (self.off + align - 1) // align * align
        self.off = o + nbytes
        assert self.off <= AR_BYTES, f"arena overflow: {self.off}"
        return o

    def bf(self, elems, parts=128):
        o = self._take(elems * 2)
        return self.b[0:parts, o // 2: o // 2 + elems]

    def f(self, elems, parts=128):
        o = self._take(elems * 4)
        return self.f4[0:parts, o // 4: o // 4 + elems]

    def fp(self, elems, parts=128):
        o = self._take(elems * 4)
        return (self.f4[0:parts, o // 4: o // 4 + elems],
                self.b[0:parts, o // 2: o // 2 + 2 * elems].bitcast(U16))


def ln_tok(P, x_ap, out_ap, tmp_ap, stat, kx, kout, ktmp, kstat, eps_ap,
           grep_ap=None, brep_ap=None, kgrep=None):
    W = x_ap.shape[-1]
    s1, m, var, sd = (stat[:, i:i + 1] for i in range(4))
    P.run("v", lambda e: e.reduce_sum(s1, x_ap, AX.X), (kx,), (kstat,))
    P.run("s", lambda e: e.activation(m, s1, AF.Identity, scale=1.0 / W),
          (kstat,), (kstat,))
    P.run("v", lambda e: e.tensor_single_scalar(out_ap, x_ap, m,
                                                ALU.subtract),
          (kx, kstat), (kout,))
    P.run("v", lambda e: e.tensor_mul(tmp_ap, out_ap, out_ap),
          (kout,), (ktmp,))
    P.run("v", lambda e: e.reduce_sum(var, tmp_ap, AX.X), (ktmp,), (kstat,))
    P.run("s", lambda e: e.activation(var, var, AF.Identity, scale=1.0 / W),
          (kstat,), (kstat,))
    P.run("s", lambda e: e.activation(sd, var, AF.Sqrt, bias=eps_ap),
          (kstat, "epsc"), (kstat,))
    P.run("v", lambda e: e.reciprocal(sd, sd), (kstat,), (kstat,))
    P.run("v", lambda e: e.tensor_single_scalar(out_ap, out_ap, sd,
                                                ALU.mult),
          (kout, kstat), (kout,))
    if grep_ap is not None:
        P.run("v", lambda e: e.tensor_mul(out_ap, out_ap, grep_ap),
              (kout, kgrep), (kout,))
        P.run("v", lambda e: e.tensor_add(out_ap, out_ap, brep_ap),
              (kout, kgrep), (kout,))


def ln_tok_multi(P, jobs, eps_ap):
    js = []
    for (x, o, t, st, kx, ko, kt, ks, g, b, kg) in jobs:
        js.append((x, o, t, st[:, 0:1], st[:, 1:2], st[:, 2:3], st[:, 3:4],
                   kx, ko, kt, ks, g, b, kg, 1.0 / x.shape[-1]))
    for x, o, t, s1, m, var, sd, kx, ko, kt, ks, g, b, kg, iw in js:
        P.run("v", lambda e, s1=s1, x=x: e.reduce_sum(s1, x, AX.X),
              (kx,), (ks,))
    for x, o, t, s1, m, var, sd, kx, ko, kt, ks, g, b, kg, iw in js:
        P.run("s", lambda e, m=m, s1=s1, iw=iw: e.activation(
            m, s1, AF.Identity, scale=iw), (ks,), (ks,))
    for x, o, t, s1, m, var, sd, kx, ko, kt, ks, g, b, kg, iw in js:
        P.run("v", lambda e, o=o, x=x, m=m: e.tensor_single_scalar(
            o, x, m, ALU.subtract), (kx, ks), (ko,))
    for x, o, t, s1, m, var, sd, kx, ko, kt, ks, g, b, kg, iw in js:
        P.run("v", lambda e, t=t, o=o: e.tensor_mul(t, o, o), (ko,), (kt,))
    for x, o, t, s1, m, var, sd, kx, ko, kt, ks, g, b, kg, iw in js:
        P.run("v", lambda e, var=var, t=t: e.reduce_sum(var, t, AX.X),
              (kt,), (ks,))
    for x, o, t, s1, m, var, sd, kx, ko, kt, ks, g, b, kg, iw in js:
        P.run("s", lambda e, var=var, iw=iw: e.activation(
            var, var, AF.Identity, scale=iw), (ks,), (ks,))
    for x, o, t, s1, m, var, sd, kx, ko, kt, ks, g, b, kg, iw in js:
        P.run("s", lambda e, sd=sd, var=var: e.activation(
            sd, var, AF.Sqrt, bias=eps_ap), (ks, "epsc"), (ks,))
    for x, o, t, s1, m, var, sd, kx, ko, kt, ks, g, b, kg, iw in js:
        P.run("v", lambda e, sd=sd: e.reciprocal(sd, sd), (ks,), (ks,))
    for x, o, t, s1, m, var, sd, kx, ko, kt, ks, g, b, kg, iw in js:
        P.run("v", lambda e, o=o, sd=sd: e.tensor_single_scalar(
            o, o, sd, ALU.mult), (ko, ks), (ko,))
    if js[0][11] is not None:
        for x, o, t, s1, m, var, sd, kx, ko, kt, ks, g, b, kg, iw in js:
            P.run("v", lambda e, o=o, g=g: e.tensor_mul(o, o, g),
                  (ko, kg), (ko,))
        for x, o, t, s1, m, var, sd, kx, ko, kt, ks, g, b, kg, iw in js:
            P.run("v", lambda e, o=o, b=b: e.tensor_add(o, o, b),
                  (ko, kg), (ko,))


MIDS = [(0.25, 0.5), (0.75, 0.5), (1.25, 0.5), (1.75, 0.5),
        (2.5, 1.0), (3.5, 1.0), (5.0, 2.0)]


def quant_tok_multi(P, jobs, g_adds=False):
    js = []
    for (x, o, t0, t1, t2, sc0, sc1, kx, ko, kt, ks) in jobs:
        nb = x.shape[-1] // 16
        js.append((x.rearrange("p (s n) -> p s n", n=16), o,
                   o.rearrange("p (s n) -> p s n", n=16),
                   t0, t0.rearrange("p (s n) -> p s n", n=16),
                   t1, t2, t2.rearrange("p (s n) -> p s n", n=16),
                   sc0, sc1,
                   sc0.rearrange("p (s o) -> p s o", o=1).to_broadcast(
                       [128, nb, 16]),
                   sc1.rearrange("p (s o) -> p s o", o=1).to_broadcast(
                       [128, nb, 16]),
                   kx, ko, kt, ks))
    AE = "g" if g_adds else "v"
    for xv, o, ov, t0, t0v, t1, t2, t2v, sc0, sc1, rb, sb, kx, ko, kt, ks in js:
        P.run("v", lambda e, sc0=sc0, xv=xv: e.tensor_reduce(
            sc0, xv, AX.X, ALU.max, apply_absolute_value=True), (kx,), (ks,))
    for xv, o, ov, t0, t0v, t1, t2, t2v, sc0, sc1, rb, sb, kx, ko, kt, ks in js:
        P.run("v", lambda e, sc1=sc1, sc0=sc0: e.tensor_scalar(
            sc1, sc0, 1.0 / 6.0, 1e-12, ALU.mult, ALU.max), (ks,), (ks,))
    for xv, o, ov, t0, t0v, t1, t2, t2v, sc0, sc1, rb, sb, kx, ko, kt, ks in js:
        P.run("v", lambda e, sc0=sc0, sc1=sc1: e.reciprocal(sc0, sc1),
              (ks,), (ks,))
    for xv, o, ov, t0, t0v, t1, t2, t2v, sc0, sc1, rb, sb, kx, ko, kt, ks in js:
        P.run("v", lambda e, t0v=t0v, xv=xv, rb=rb: e.tensor_tensor(
            t0v, xv, rb, ALU.mult), (kx, ks), (kt,))
    for xv, o, ov, t0, t0v, t1, t2, t2v, sc0, sc1, rb, sb, kx, ko, kt, ks in js:
        P.run("v", lambda e, t1=t1, t0=t0: e.scalar_tensor_tensor(
            t1, t0, -1.0, t0, ALU.mult, ALU.max), (kt,), (kt,))
    m0, i0 = MIDS[0]
    for xv, o, ov, t0, t0v, t1, t2, t2v, sc0, sc1, rb, sb, kx, ko, kt, ks in js:
        P.run("v", lambda e, t2=t2, t1=t1: e.tensor_scalar(
            t2, t1, m0, i0, ALU.is_gt, ALU.mult), (kt,), (kt,))
    for m_i, inc in MIDS[1:]:
        for xv, o, ov, t0, t0v, t1, t2, t2v, sc0, sc1, rb, sb, kx, ko, kt, \
                ks in js:
            P.run("v", lambda e, o=o, t1=t1, m_i=m_i, inc=inc: e.tensor_scalar(
                o, t1, m_i, inc, ALU.is_gt, ALU.mult), (kt,), (ko,))
        for xv, o, ov, t0, t0v, t1, t2, t2v, sc0, sc1, rb, sb, kx, ko, kt, \
                ks in js:
            P.run(AE, lambda e, t2=t2, o=o: e.tensor_add(t2, t2, o),
                  (kt, ko), (kt,))
    for xv, o, ov, t0, t0v, t1, t2, t2v, sc0, sc1, rb, sb, kx, ko, kt, ks in js:
        P.run("v", lambda e, t1=t1, t0=t0: e.tensor_scalar(
            t1, t0, 0.0, 2.0, ALU.is_ge, ALU.mult), (kt,), (kt,))
    for xv, o, ov, t0, t0v, t1, t2, t2v, sc0, sc1, rb, sb, kx, ko, kt, ks in js:
        P.run("v", lambda e, t1=t1: e.tensor_single_scalar(
            t1, t1, -1.0, ALU.add), (kt,), (kt,))
    for xv, o, ov, t0, t0v, t1, t2, t2v, sc0, sc1, rb, sb, kx, ko, kt, ks in js:
        P.run("v", lambda e, t2=t2, t1=t1: e.tensor_mul(t2, t2, t1),
              (kt,), (kt,))
    for xv, o, ov, t0, t0v, t1, t2, t2v, sc0, sc1, rb, sb, kx, ko, kt, ks in js:
        P.run("v", lambda e, ov=ov, t2v=t2v, sb=sb: e.tensor_tensor(
            ov, t2v, sb, ALU.mult), (kt, ks), (ko,))


def quant_tok(P, x_ap, out_ap, t0, t1, t2, sc0, sc1, kx, kout, kt, ksc):
    W = x_ap.shape[-1]
    nb = W // 16
    xv = x_ap.rearrange("p (s n) -> p s n", n=16)
    P.run("v", lambda e: e.tensor_reduce(sc0, xv, AX.X, ALU.max,
                                         apply_absolute_value=True),
          (kx,), (ksc,))
    P.run("v", lambda e: e.tensor_scalar(sc1, sc0, 1.0 / 6.0, 1e-12,
                                         ALU.mult, ALU.max), (ksc,), (ksc,))
    P.run("v", lambda e: e.reciprocal(sc0, sc1), (ksc,), (ksc,))
    recb = sc0.rearrange("p (s o) -> p s o", o=1).to_broadcast([128, nb, 16])
    sclb = sc1.rearrange("p (s o) -> p s o", o=1).to_broadcast([128, nb, 16])
    t0v = t0.rearrange("p (s n) -> p s n", n=16)
    P.run("v", lambda e: e.tensor_tensor(t0v, xv, recb, ALU.mult),
          (kx, ksc), (kt,))
    P.run("v", lambda e: e.scalar_tensor_tensor(t1, t0, -1.0, t0, ALU.mult,
                                                ALU.max), (kt,), (kt,))
    m0, i0 = MIDS[0]
    P.run("v", lambda e: e.tensor_scalar(t2, t1, m0, i0, ALU.is_gt,
                                         ALU.mult), (kt,), (kt,))
    for m_i, inc in MIDS[1:]:
        P.run("v", lambda e, m_i=m_i, inc=inc: e.tensor_scalar(
            out_ap, t1, m_i, inc, ALU.is_gt, ALU.mult), (kt,), (kout,))
        P.run("v", lambda e: e.tensor_add(t2, t2, out_ap), (kt, kout), (kt,))
    P.run("v", lambda e: e.tensor_scalar(t1, t0, 0.0, 2.0, ALU.is_ge,
                                         ALU.mult), (kt,), (kt,))
    P.run("v", lambda e: e.tensor_single_scalar(t1, t1, -1.0, ALU.add),
          (kt,), (kt,))
    P.run("v", lambda e: e.tensor_mul(t2, t2, t1), (kt,), (kt,))
    t2v = t2.rearrange("p (s n) -> p s n", n=16)
    ov = out_ap.rearrange("p (s n) -> p s n", n=16)
    P.run("v", lambda e: e.tensor_tensor(ov, t2v, sclb, ALU.mult),
          (kt, ksc), (kout,))


def xbar_tok2fm(P, tok_ap, fm_ap, kin, kout, kd, tb_blocks):
    W = kd * 128
    for tb in range(tb_blocks):
        out = fm_ap.rearrange("p (k b t) -> p b k t", k=kd, b=tb_blocks,
                              t=128)[:, tb:tb + 1]
        P.dma("q", out, tok_ap[:, tb * W:(tb + 1) * W], reads=(kin,),
              writes=(kout,), transpose=True)


def build_nc():
    nc = bass.Bass("TRN2", target_bir_lowering=False)
    NCH = SEQ // T

    def din(name, shape):
        return nc.dram_tensor(name, shape, F32, kind="ExternalInput").ap()

    def dscr(name, shape, dtype=BF16):
        kind = "ExternalOutput" if name in DEBUG_SET else "Internal"
        return nc.dram_tensor(name, shape, dtype, kind=kind).ap()

    xt_h = din("xt", [SEQ, D])
    w1_h = din("w1", [D, D])
    v1_h = din("v1", [D, D])
    dt_h = din("dtw", [D, D])
    w2_h = din("w2", [D, D])
    bcw_h = din("bcw", [D, 2 * NST])
    ff1_h = din("ff1", [D, DFF])
    ff2_h = din("ff2", [DFF, D])
    wg_h = din("wg", [DI, D])
    wu_h = din("wu", [DI, D])
    wdn_h = din("wd", [D, DI])
    cols_h = din("cols", [128, 4 * KD])
    ff1b_h = din("ff1b", [128, KFF])
    alog_h = din("alog", [128, KD * NST])
    conv_h = din("convw", [128, KD * 3])
    rows_h = din("rows", [1, 8 * D])
    emat_h = din("emat", [2 * NST, 2 * NST * 128])
    bcbc_h = din("bcbc", [2 * NST, 1])
    out_h = nc.dram_tensor("out", [SEQ, D], F32, kind="ExternalOutput").ap()

    x1_s = dscr("x1s", [KD, 128, SEQ], F32)
    v_s = dscr("vss", [KD, 128, SEQ], F32)
    xc_s = dscr("xcs", [KD, 128, SEQ], F32)
    dr_s = dscr("drs", [KD, 128, SEQ], F32)
    bc_s = dscr("bcs", [2 * NST, SEQ], F32)
    z_s = dscr("zs", [KD, 128, SEQ])
    r2x_s = dscr("r2xs", [SEQ, D], F32)
    h1_s = dscr("h1s", [KFF, 128, SEQ])
    x3_s = dscr("x3s", [SEQ, D], F32)
    xq_s = dscr("xqs", [KD, 128, SEQ])
    hq_s = dscr("hqs", [KI, 128, SEQ])
    dbg = {}
    for nm, shp in (("dbg_xn", [SEQ, D]), ("dbg_xm", [SEQ, D]),
                    ("dbg_grep", [128, 6 * D]), ("dbg_mo", [SEQ, D])):
        if nm in DEBUG_SET:
            dbg[nm] = dscr(nm, shp)

    import contextlib
    with contextlib.ExitStack() as _st:
        en = _st.enter_context
        ones_row = en(nc.sbuf_tensor("ones_row", [1, T], BF16))
        rows2 = en(nc.sbuf_tensor("rows2", [1, 8 * D], BF16))
        e_mat = en(nc.sbuf_tensor("e_mat", [2 * NST, 2 * NST * 128], BF16))
        cols = en(nc.sbuf_tensor("colsb", [128, 4 * KD], F32))
        ff1b = en(nc.sbuf_tensor("ff1bb", [128, KFF], F32))
        acoef = en(nc.sbuf_tensor("acoef", [128, KD * NST], F32))
        convw = en(nc.sbuf_tensor("convwb", [128, KD * 3], F32))
        grep = en(nc.sbuf_tensor("grep", [128, 6 * D], BF16))
        grep2 = en(nc.sbuf_tensor("grep2", [128, 2 * D], BF16))
        bcbc = en(nc.sbuf_tensor("bcbcb", [2 * NST, 1], F32))
        h0 = en(nc.sbuf_tensor("h0b", [128, KD * NST], F32))
        inj = en(nc.sbuf_tensor("injb", [128, NST], F32))
        epsc = en(nc.sbuf_tensor("epscb", [128, 1], F32))
        arena_t = en(nc.sbuf_tensor("arena", [128, AR_BYTES // 2], BF16))
        pb0 = en(nc.psum_tensor("pb0", [128, 512], F32))
        pb1 = en(nc.psum_tensor("pb1", [128, 512], F32))
        pb2 = en(nc.psum_tensor("pb2", [128, 512], F32))
        pb3 = en(nc.psum_tensor("pb3", [128, 512], F32))
        pw0 = en(nc.psum_tensor("pw0", [128, 1024], F32))
        pw1 = en(nc.psum_tensor("pw1", [128, 1024], F32))
        St = en(nc.semaphore("St"))
        Sv = en(nc.semaphore("Sv"))
        Ss = en(nc.semaphore("Ss"))
        Sg = en(nc.semaphore("Sg"))
        Sq = en(nc.semaphore("Sq"))
        Eq = en(nc.semaphore("Eq"))
        Eg = en(nc.semaphore("Eg"))
        pb = [pb0, pb1, pb2, pb3]
        pw = [pw0, pw1]
        P = Prog(nc, {"t": St, "v": Sv, "s": Ss, "g": Sg, "q": Sq},
                 {"q": Eq, "g": Eg})
        AR = Arena(arena_t[:], arena_t[:].bitcast(F32))

        P.dma("g", rows2[:], rows_h[:, :], writes=("rows2",))
        P.dma("g", e_mat[:], emat_h[:, :], writes=("emat",))
        P.dma("q", bcbc[:], bcbc_h[:, :], writes=("bcbc",))
        P.dma("g", cols[:], cols_h[:, :], writes=("cols",))
        P.dma("g", ff1b[:], ff1b_h[:, :], writes=("ff1b",))
        P.dma("g", acoef[:], alog_h[:, :], writes=("acoef",))
        P.dma("g", convw[:], conv_h[:, :], writes=("convw",))
        P.run("v", lambda e: e.memset(ones_row[:], 1.0), (), ("ones",))
        P.run("s", lambda e: e.activation(acoef[:], acoef[:], AF.Exp),
              ("acoef",), ("acoef",))
        P.run("v", lambda e: e.tensor_single_scalar(acoef[:], acoef[:], -1.0,
                                                    ALU.mult),
              ("acoef",), ("acoef",))
        P.run("v", lambda e: e.memset(h0[:], 0.0), (), ("h0",))
        P.run("v", lambda e: e.memset(epsc[:], EPS), (), ("epsc",))
        for j in range(8):
            P.mm_group(
                [(lambda e, j=j, h=h: e.matmul(
                    pw[0][:, h * 512:(h + 1) * 512], ones_row[:1, 0:128],
                    rows2[:1, j * D + h * 512: j * D + (h + 1) * 512],
                    start=True, stop=True)) for h in range(2)],
                ("rows2", "ones"), ("pw0",))
            dstt = grep if j < 6 else grep2
            dk = "grep" if j < 6 else "grep2"
            jj = j if j < 6 else j - 6
            P.run("v", lambda e, jj=jj, dstt=dstt: e.tensor_copy(
                dstt[:, jj * D:(jj + 1) * D], pw[0][:]), ("pw0",), (dk,))
        if "dbg_grep" in dbg:
            P.dma("q", dbg["dbg_grep"][:, :], grep[:], reads=("grep",))
        P.barrier()

        def _phase0():
            AR.reset()
            wa = AR.bf(KD * D)
            wb = AR.bf(KD * D)
            act0, act0_b = AR.fp(TB * D)
            act1 = AR.f(TB * D)
            act2 = AR.f(TB * D)
            xmb = AR.bf(TB * D)
            tmpq = AR.bf(TB * D)
            fmx = AR.bf(KD * T)
            ev0, ev0_b = AR.fp(KD * T)
            ev1, ev1_b = AR.fp(KD * T)
            stat = AR.f(16)
            for k in range(KD if "p1a" in PHASES else 0):
                P.dma("g", wa[:, k * D:(k + 1) * D],
                      w1_h[k * 128:(k + 1) * 128, :], writes=("wa",))
                P.dma("g", wb[:, k * D:(k + 1) * D],
                      v1_h[k * 128:(k + 1) * 128, :], writes=("wb",))
            for c in range(NCH if "p1a" in PHASES else 0):
                P.dma("g", act0_b.rearrange("p (b f) -> p b f", f=2 * D),
                      xt_h.bitcast(U16)[c * T:(c + 1) * T, :].rearrange(
                          "(b p) f -> p b f", p=128),
                      writes=("act0",))
                ln_tok_multi(P, [
                    (act0[:, tb * D:(tb + 1) * D],
                     act1[:, tb * D:(tb + 1) * D],
                     tmpq[:, tb * D:(tb + 1) * D],
                     stat[:, tb * 4:(tb + 1) * 4],
                     "act0", "act1", f"tmpq{tb}", f"stat{tb}",
                     grep[:, 0:D], grep[:, D:2 * D], "grep")
                    for tb in range(TB)], epsc[:])
                ln_tok_multi(P, [
                    (act1[:, tb * D:(tb + 1) * D],
                     act2[:, tb * D:(tb + 1) * D],
                     tmpq[:, tb * D:(tb + 1) * D],
                     stat[:, tb * 4:(tb + 1) * 4],
                     "act1", "act2", f"tmpq{tb}", f"stat{tb}",
                     grep[:, 2 * D:3 * D], grep[:, 3 * D:4 * D], "grep")
                    for tb in range(TB)], epsc[:])
                if "dbg_xn" in dbg:
                    P.dma("q", dbg["dbg_xn"][c * T:(c + 1) * T, :].rearrange(
                        "(b p) f -> p b f", p=128),
                        act1.rearrange("p (b f) -> p b f", f=D),
                        reads=("act1",))
                if "dbg_xm" in dbg:
                    P.dma("q", dbg["dbg_xm"][c * T:(c + 1) * T, :].rearrange(
                        "(b p) f -> p b f", p=128),
                        act2.rearrange("p (b f) -> p b f", f=D),
                        reads=("act2",))
                P.run("v", lambda e: e.tensor_copy(xmb, act2),
                      ("act2",), ("xmb",))
                xbar_tok2fm(P, xmb, fmx, "xmb", "fmx", KD, TB)
                for m in range(KD):
                    p = pb[m % 2]
                    kpb = f"pb{m % 2}"
                    P.mm_group(
                        [(lambda e, k=k, m=m, p=p: e.matmul(
                            p[:], wa[:, k * D + m * 128: k * D + (m + 1) * 128],
                            fmx[:, k * T:(k + 1) * T],
                            start=(k == 0), stop=(k == KD - 1)))
                         for k in range(KD)],
                        ("wa", "fmx"), (kpb,))
                    P.run("s", lambda e, m=m, p=p: e.activation(
                        ev0[:, m * T:(m + 1) * T], p[:], AF.Identity,
                        bias=cols[:, m:m + 1]),
                        (kpb, "cols"), (f"ev0{m % 2}",))
                    P.dma("q", x1_s.bitcast(U16)[m, :,
                          2 * c * T:2 * (c + 1) * T],
                          ev0_b[:, 2 * m * T:2 * (m + 1) * T],
                          reads=(f"ev0{m % 2}",))
                for m in range(KD):
                    p = pb[2 + m % 2]
                    kpb = f"pb{2 + m % 2}"
                    P.mm_group(
                        [(lambda e, k=k, m=m, p=p: e.matmul(
                            p[:], wb[:, k * D + m * 128: k * D + (m + 1) * 128],
                            fmx[:, k * T:(k + 1) * T],
                            start=(k == 0), stop=(k == KD - 1)))
                         for k in range(KD)],
                        ("wb", "fmx"), (kpb,))
                    P.run("s", lambda e, m=m, p=p: e.activation(
                        ev1[:, m * T:(m + 1) * T], p[:], AF.Silu,
                        bias=cols[:, KD + m:KD + m + 1]),
                        (kpb, "cols"), (f"ev1{m % 2}",))
                    P.dma("q", v_s.bitcast(U16)[m, :,
                          2 * c * T:2 * (c + 1) * T],
                          ev1_b[:, 2 * m * T:2 * (m + 1) * T],
                          reads=(f"ev1{m % 2}",))
            P.barrier()
        _phase0()

        def _phase1():
            AR.reset()
            wa = AR.bf(KD * D)
            bcw_sb = AR.bf(KD * 2 * NST)
            TH = T + 2
            x1h, x1h_b = AR.fp(KD * TH)
            xcb, xcb_b2 = AR.fp(KD * T)
            xcbb = AR.bf(KD * T)
            evd, evd_b = AR.fp(KD * T)
            bcf, bcf_b = AR.fp(T, parts=2 * NST)
            for k in range(KD if "p1b" in PHASES else 0):
                P.dma("g", wa[:, k * D:(k + 1) * D],
                      dt_h[k * 128:(k + 1) * 128, :], writes=("wa",))
                P.dma("g", bcw_sb[:, k * 2 * NST:(k + 1) * 2 * NST],
                      bcw_h[k * 128:(k + 1) * 128, :], writes=("bcw",))
            for c in range(NCH if "p1b" in PHASES else 0):
                for k in range(KD):
                    xk = x1h[:, k * TH:(k + 1) * TH]
                    lo = c * T - 1
                    hi = (c + 1) * T + 1
                    sl = 0
                    if c == 0:
                        P.run("v", lambda e, xk=xk: e.memset(xk[:, 0:1], 0.0),
                              ("x1h",), ("x1h",))
                        lo, sl = 0, 1
                    if c == NCH - 1:
                        P.run("v", lambda e, xk=xk: e.memset(
                            xk[:, TH - 1:TH], 0.0), ("x1h",), ("x1h",))
                        hi = SEQ
                    P.dma("g", x1h_b[:, 2 * (k * TH + sl):
                          2 * (k * TH + sl) + 2 * (hi - lo)],
                          x1_s.bitcast(U16)[k, :, 2 * lo:2 * hi],
                          writes=("x1h",))
                for k in range(KD):
                    xk = x1h[:, k * TH:(k + 1) * TH]
                    ck = xcb[:, k * T:(k + 1) * T]
                    P.run("v", lambda e, xk=xk, ck=ck, k=k:
                          e.tensor_single_scalar(
                              ck, xk[:, 0:T], convw[:, 3 * k:3 * k + 1],
                              ALU.mult),
                          ("x1h", "convw"), ("xcb",))
                    for tap in (1, 2):
                        P.run("v", lambda e, xk=xk, ck=ck, k=k, tap=tap:
                              e.scalar_tensor_tensor(
                                  ck, xk[:, tap:tap + T],
                                  convw[:, 3 * k + tap:3 * k + tap + 1],
                                  ck, ALU.mult, ALU.add),
                              ("x1h", "convw", "xcb"), ("xcb",))
                    P.run("s", lambda e, ck=ck: e.activation(ck, ck, AF.Silu),
                          ("xcb",), ("xcb",))
                    P.run("v", lambda e, ck=ck, k=k: e.tensor_copy(
                        xcbb[:, k * T:(k + 1) * T], ck),
                        ("xcb",), ("xcbb",))
                    P.dma("q", xc_s.bitcast(U16)[k, :,
                          2 * c * T:2 * (c + 1) * T],
                          xcb_b2[:, 2 * k * T:2 * (k + 1) * T],
                          reads=("xcb",))
                for m in range(KD):
                    p = pb[m % 2]
                    kpb = f"pb{m % 2}"
                    P.mm_group(
                        [(lambda e, k=k, m=m, p=p: e.matmul(
                            p[:], wa[:, k * D + m * 128: k * D + (m + 1) * 128],
                            xcbb[:, k * T:(k + 1) * T],
                            start=(k == 0), stop=(k == KD - 1)))
                         for k in range(KD)],
                        ("wa", "xcbb"), (kpb,))
                    P.run("s", lambda e, m=m, p=p: e.activation(
                        evd[:, m * T:(m + 1) * T], p[:], AF.Identity,
                        bias=cols[:, 2 * KD + m:2 * KD + m + 1]),
                        (kpb, "cols"), ("evd",))
                    P.dma("q", dr_s.bitcast(U16)[m, :,
                          2 * c * T:2 * (c + 1) * T],
                          evd_b[:, 2 * m * T:2 * (m + 1) * T],
                          reads=("evd",))
                P.mm_group(
                    [(lambda e, k=k: e.matmul(
                        pb[2][0:2 * NST, :],
                        bcw_sb[:, k * 2 * NST:(k + 1) * 2 * NST],
                        xcbb[:, k * T:(k + 1) * T],
                        start=(k == 0), stop=(k == KD - 1)))
                     for k in range(KD)],
                    ("bcw", "xcbb"), ("pb2",))
                P.run("s", lambda e: e.activation(bcf, pb[2][0:2 * NST, :],
                                                  AF.Identity, bias=bcbc[:]),
                      ("pb2", "bcbc"), ("bcf",))
                P.dma("q", bc_s.bitcast(U16)[:, 2 * c * T:2 * (c + 1) * T],
                  bcf_b, reads=("bcf",))
            P.barrier()
        _phase1()

        def _phase2():
            AR.reset()
            fmA, fmA_b = AR.fp(KD * T)
            fmB, fmB_b = AR.fp(KD * T)
            fmC = AR.bf(KD * T)
            fmD = AR.bf(KD * T)
            fmZ = fmC
            big0 = [AR.bf(NST * T), AR.bf(NST * T)]
            big1 = [AR.bf(NST * T), AR.bf(NST * T)]
            big2 = AR.bf(NST * T)
            brep = AR.bf(NST * T)
            crep = AR.bf(NST * T)
            bcch = AR.bf(T, parts=2 * NST)
            for c in range(NCH if "p2" in PHASES else 0):
                for k in range(KD):
                    P.dma("g", fmA_b[:, 2 * k * T:2 * (k + 1) * T],
                          dr_s.bitcast(U16)[k, :,
                          2 * c * T:2 * (c + 1) * T], writes=("fmA",))
                    P.dma("g", fmB_b[:, 2 * k * T:2 * (k + 1) * T],
                          xc_s.bitcast(U16)[k, :, 2 * c * T:2 * (c + 1) * T],
                          writes=("fmB",))
                    P.dma("g", fmD[:, k * T:(k + 1) * T],
                          v_s[k, :, c * T:(c + 1) * T], writes=("fmD",))
                P.dma("g", bcch, bc_s[:, c * T:(c + 1) * T], writes=("bcch",))
                for n in range(NST):
                    for half, dst, kdst, ev_eng in ((0, brep, "brep", "v"),
                                                    (1, crep, "crep", "s")):
                        p = pb[n % 2 + 2 * half]
                        kpb = f"pb{n % 2 + 2 * half}"
                        col = (half * NST + n) * 128
                        P.mm_group(
                            [(lambda e, col=col, p=p: e.matmul(
                                p[:], e_mat[:, col:col + 128], bcch,
                                start=True, stop=True))],
                            ("emat", "bcch"), (kpb,))
                        if ev_eng == "v":
                            P.run("v", lambda e, n=n, p=p, dst=dst:
                                  e.tensor_copy(dst[:, n * T:(n + 1) * T], p[:]),
                                  (kpb,), (kdst,))
                        else:
                            P.run("s", lambda e, n=n, p=p, dst=dst:
                                  e.activation(dst[:, n * T:(n + 1) * T], p[:],
                                               AF.Identity),
                                  (kpb,), (kdst,))
                for k in range(KD):
                    ak = fmA[:, k * T:(k + 1) * T]
                    ck = fmC[:, k * T:(k + 1) * T]
                    P.run("s", lambda e, ak=ak, ck=ck: e.activation(
                        ck, ak, AF.Abs), ("fmA",), ("fmC",))
                    P.run("s", lambda e, ck=ck: e.activation(
                        ck, ck, AF.Exp, scale=-1.0), ("fmC",), ("fmC",))
                    P.run("s", lambda e, ck=ck: e.activation(
                        ck, ck, AF.Ln, bias=1.0), ("fmC",), ("fmC",))
                    P.run("v", lambda e, ak=ak, ck=ck: e.scalar_tensor_tensor(
                        ak, ak, 0.0, ck, ALU.max, ALU.add),
                        ("fmA", "fmC"), ("fmA",))
                for k in range(KD):
                    ak = fmA[:, k * T:(k + 1) * T]
                    bk = fmB[:, k * T:(k + 1) * T]
                    ck = fmC[:, k * T:(k + 1) * T]
                    b0 = big0[k % 2]
                    b1 = big1[k % 2]
                    k0 = f"big0{k % 2}"
                    k1 = f"big1{k % 2}"
                    P.run("v", lambda e, ak=ak, bk=bk, ck=ck: e.tensor_mul(
                        ck, ak, bk), ("fmA", "fmB"), ("fmC",))
                    for n in range(NST):
                        P.run("s", lambda e, ak=ak, n=n, k=k, b0=b0:
                              e.activation(
                                  b0[:, n * T:(n + 1) * T], ak, AF.Exp,
                                  scale=acoef[:, k * NST + n:k * NST + n + 1]),
                              ("fmA", "acoef"), (k0,))
                    for n in range(NST):
                        P.run("g", lambda e, ck=ck, n=n, b1=b1: e.tensor_mul(
                            b1[:, n * T:(n + 1) * T], ck,
                            brep[:, n * T:(n + 1) * T]),
                            ("fmC", "brep"), (k1,))
                    vA = b0.rearrange("p (n t) -> p n t", t=T)[:, :, 0:1]
                    vB = b1.rearrange("p (n t) -> p n t", t=T)[:, :, 0:1]
                    h0k = h0[:, k * NST:(k + 1) * NST].rearrange(
                        "p (n o) -> p n o", o=1)
                    injv = inj[:].rearrange("p (n o) -> p n o", o=1)
                    P.run("v", lambda e, vA=vA, h0k=h0k, injv=injv:
                          e.tensor_tensor(injv, vA, h0k, ALU.mult),
                          (k0, "h0"), ("inj",))
                    P.run("v", lambda e, vB=vB, injv=injv: e.tensor_tensor(
                        vB, vB, injv, ALU.add), (k1, "inj"), (k1,))
                    P.run("v", lambda e, vA=vA: e.memset(vA, 0.0), (k0,), (k0,))
                    P.run("v", lambda e, b0=b0, b1=b1: e.tensor_tensor_scan(
                        big2, b0, b1, 0.0, ALU.mult, ALU.add),
                        (k0, k1), ("big2",))
                    vH = big2.rearrange("p (n t) -> p n t", t=T)[:, :, T - 1:T]
                    P.run("v", lambda e, vH=vH, h0k=h0k: e.tensor_copy(h0k, vH),
                          ("big2",), ("h0",))
                    P.run("v", lambda e: e.tensor_mul(big2, big2, crep),
                          ("big2", "crep"), ("big2",))
                    w = NST * T // 2
                    while w >= T:
                        P.run("v", lambda e, w=w: e.tensor_add(
                            big2[:, 0:w], big2[:, 0:w], big2[:, w:2 * w]),
                            ("big2",), ("big2",))
                        w //= 2
                    zk = fmZ[:, k * T:(k + 1) * T]
                    P.run("v", lambda e, zk=zk, bk=bk, k=k:
                          e.tensor_single_scalar(
                              zk, bk, cols[:, 3 * KD + k:3 * KD + k + 1],
                              ALU.mult),
                          ("fmB", "cols"), ("fmC",))
                    P.run("v", lambda e, zk=zk: e.tensor_add(
                        zk, zk, big2[:, 0:T]), ("fmC", "big2"), ("fmC",))
                    P.run("v", lambda e, zk=zk, k=k: e.tensor_mul(
                        zk, zk, fmD[:, k * T:(k + 1) * T]),
                        ("fmC", "fmD"), ("fmC",))
                    P.dma("q", z_s[k, :, c * T:(c + 1) * T], zk, reads=("fmC",))
            P.barrier()
        _phase2()

        def _phase3():
            AR.reset()
            wa = AR.bf(KD * D)
            fm0 = AR.bf(KD * T)
            act0, act0_b = AR.fp(TB * D)
            act1, act1_b = AR.fp(TB * D)
            for k in range(KD if "p3a" in PHASES else 0):
                P.dma("g", wa[:, k * D:(k + 1) * D],
                      w2_h[k * 128:(k + 1) * 128, :], writes=("wa",))
            for c in range(NCH if "p3a" in PHASES else 0):
                for k in range(KD):
                    P.dma("g", fm0[:, k * T:(k + 1) * T],
                          z_s[k, :, c * T:(c + 1) * T], writes=("fm0",))
                P.dma("g", act0_b.rearrange("p (b f) -> p b f", f=2 * D),
                      xt_h.bitcast(U16)[c * T:(c + 1) * T, :].rearrange(
                          "(b p) f -> p b f", p=128),
                      writes=("act0",))
                for tb in range(TB):
                    p = pw[tb % 2]
                    kpw = f"pw{tb % 2}"
                    fns = []
                    for h in range(2):
                        for k in range(KD):
                            fns.append(lambda e, k=k, h=h, tb=tb, p=p: e.matmul(
                                p[:, h * 512:(h + 1) * 512],
                                fm0[:, k * T + tb * 128: k * T + tb * 128 + 128],
                                wa[:, k * D + h * 512: k * D + (h + 1) * 512],
                                start=(k == 0), stop=(k == KD - 1)))
                    P.mm_group(fns, ("fm0", "wa"), (kpw,))
                    if "dbg_mo" in dbg:
                        P.run("s", lambda e, tb=tb, p=p: e.activation(
                            act1[:, tb * D:(tb + 1) * D], p[:], AF.Identity),
                            (kpw,), ("act1",))
                        P.dma("g", dbg["dbg_mo"][
                            c * T + tb * 128: c * T + (tb + 1) * 128, :],
                            act1[:, tb * D:(tb + 1) * D], reads=("act1",))
                    P.run("v", lambda e, tb=tb, p=p: e.tensor_add(
                        act1[:, tb * D:(tb + 1) * D], p[:], grep2[:, 0:D]),
                        (kpw, "grep2"), ("act1",))
                    P.run("v", lambda e, tb=tb: e.tensor_add(
                        act1[:, tb * D:(tb + 1) * D],
                        act1[:, tb * D:(tb + 1) * D],
                        act0[:, tb * D:(tb + 1) * D]),
                        ("act1", "act0"), ("act1",))
                P.dma("q",
                      r2x_s.bitcast(U16)[c * T:(c + 1) * T, :].rearrange(
                          "(b p) f -> p b f", p=128),
                      act1_b.rearrange("p (b f) -> p b f", f=2 * D),
                      reads=("act1",))
            P.barrier()
        _phase3()

        def _phase4():
            AR.reset()
            wc = AR.bf(KD * DFF)
            act0, act0_b = AR.fp(TB * D)
            act1 = AR.bf(TB * D)
            tmpq = AR.bf(TB * D)
            fm0 = AR.bf(KD * T)
            evg = AR.bf(4 * T)
            stat = AR.f(16)
            for k in range(KD if "p3b" in PHASES else 0):
                P.dma("g", wc[:, k * DFF:(k + 1) * DFF],
                      ff1_h[k * 128:(k + 1) * 128, :], writes=("wc",))
            for c in range(NCH if "p3b" in PHASES else 0):
                P.dma("g", act0_b.rearrange("p (b f) -> p b f", f=2 * D),
                      r2x_s.bitcast(U16)[c * T:(c + 1) * T, :].rearrange(
                          "(b p) f -> p b f", p=128),
                      writes=("act0",))
                ln_tok_multi(P, [
                    (act0[:, tb * D:(tb + 1) * D],
                     act1[:, tb * D:(tb + 1) * D],
                     tmpq[:, tb * D:(tb + 1) * D],
                     stat[:, tb * 4:(tb + 1) * 4],
                     "act0", "act1", f"tmpq{tb}", f"stat{tb}",
                     grep[:, 4 * D:5 * D], grep[:, 5 * D:6 * D], "grep")
                    for tb in range(TB)], epsc[:])
                xbar_tok2fm(P, act1, fm0, "act1", "fm0", KD, TB)
                for m in range(KFF):
                    p = pb[m % 2]
                    kpb = f"pb{m % 2}"
                    P.mm_group(
                        [(lambda e, k=k, m=m, p=p: e.matmul(
                            p[:],
                            wc[:, k * DFF + m * 128: k * DFF + (m + 1) * 128],
                            fm0[:, k * T:(k + 1) * T],
                            start=(k == 0), stop=(k == KD - 1)))
                         for k in range(KD)],
                        ("wc", "fm0"), (kpb,))
                    P.run("s", lambda e, m=m, p=p: e.activation(
                        evg[:, (m % 4) * T:(m % 4 + 1) * T], p[:], AF.Gelu,
                        bias=ff1b[:, m:m + 1]),
                        (kpb, "ff1b"), (f"evg{m % 4}",))
                    P.dma("q", h1_s[m, :, c * T:(c + 1) * T],
                          evg[:, (m % 4) * T:(m % 4 + 1) * T],
                          reads=(f"evg{m % 4}",))
            P.barrier()
        _phase4()

        def _phase5():
            AR.reset()
            wd_sb = AR.bf(KFF * D)
            h1c = AR.bf(KFF * T)
            act0 = AR.bf(TB * D)
            act1, act1_b = AR.fp(TB * D)
            act2, act2_b = AR.fp(TB * D)
            tmpq = AR.bf(TB * D)
            tq0 = AR.f(2 * D)
            tq1 = AR.f(2 * D)
            tq2 = AR.bf(2 * D)
            sc0 = AR.f(2 * (D // 16))
            sc1 = AR.f(2 * (D // 16))
            xqb = AR.bf(TB * D)
            fmq = AR.bf(KD * T)
            stat = AR.f(16)
            for kc in range(KFF if "p3c" in PHASES else 0):
                P.dma("g", wd_sb[:, kc * D:(kc + 1) * D],
                      ff2_h[kc * 128:(kc + 1) * 128, :], writes=("wd",))
            for c in range(NCH if "p3c" in PHASES else 0):
                for kc in range(KFF):
                    P.dma("g", h1c[:, kc * T:(kc + 1) * T],
                          h1_s[kc, :, c * T:(c + 1) * T], writes=("h1c",))
                P.dma("g", act0.rearrange("p (b f) -> p b f", f=D),
                      xt_h[c * T:(c + 1) * T, :].rearrange(
                          "(b p) f -> p b f", p=128),
                      writes=("act0",))
                P.dma("g", act1_b.rearrange("p (b f) -> p b f", f=2 * D),
                      r2x_s.bitcast(U16)[c * T:(c + 1) * T, :].rearrange(
                          "(b p) f -> p b f", p=128),
                      writes=("act1",))
                for tb in range(TB):
                    p = pw[tb % 2]
                    kpw = f"pw{tb % 2}"
                    fns = []
                    for h in range(2):
                        for kc in range(KFF):
                            fns.append(lambda e, kc=kc, h=h, tb=tb, p=p:
                                       e.matmul(
                                p[:, h * 512:(h + 1) * 512],
                                h1c[:, kc * T + tb * 128:
                                    kc * T + tb * 128 + 128],
                                wd_sb[:, kc * D + h * 512:
                                      kc * D + (h + 1) * 512],
                                start=(kc == 0), stop=(kc == KFF - 1)))
                    P.mm_group(fns, ("h1c", "wd"), (kpw,))
                    P.run("v", lambda e, tb=tb, p=p: e.tensor_add(
                        act2[:, tb * D:(tb + 1) * D], p[:], grep2[:, D:2 * D]),
                        (kpw, "grep2"), ("act2",))
                    P.run("v", lambda e, tb=tb: e.tensor_add(
                        act2[:, tb * D:(tb + 1) * D],
                        act2[:, tb * D:(tb + 1) * D],
                        act1[:, tb * D:(tb + 1) * D]),
                        ("act2", "act1"), ("act2",))
                    P.run("v", lambda e, tb=tb: e.tensor_add(
                        act2[:, tb * D:(tb + 1) * D],
                        act2[:, tb * D:(tb + 1) * D],
                        act0[:, tb * D:(tb + 1) * D]),
                        ("act2", "act0"), ("act2",))
                ln_tok_multi(P, [
                    (act2[:, tb * D:(tb + 1) * D],
                     act2[:, tb * D:(tb + 1) * D],
                     tmpq[:, tb * D:(tb + 1) * D],
                     stat[:, tb * 4:(tb + 1) * 4],
                     "act2", "act2", f"tmpq{tb}", f"stat{tb}",
                     None, None, None)
                    for tb in range(TB)], epsc[:])
                P.dma("q",
                      x3_s.bitcast(U16)[c * T:(c + 1) * T, :].rearrange(
                          "(b p) f -> p b f", p=128),
                      act2_b.rearrange("p (b f) -> p b f", f=2 * D),
                      reads=("act2",))
                for j in range(2):
                    quant_tok_multi(P, [
                        (act2[:, tb * D:(tb + 1) * D],
                         xqb[:, tb * D:(tb + 1) * D],
                         tq0[:, i * D:(i + 1) * D],
                         tq1[:, i * D:(i + 1) * D],
                         tq2[:, i * D:(i + 1) * D],
                         sc0[:, i * (D // 16):(i + 1) * (D // 16)],
                         sc1[:, i * (D // 16):(i + 1) * (D // 16)],
                         "act2", "xqb", f"tq{i}", f"sc{i}")
                        for i, tb in enumerate((2 * j, 2 * j + 1))],
                        g_adds=False)
                xbar_tok2fm(P, xqb, fmq, "xqb", "fmq", KD, TB)
                for k in range(KD):
                    P.dma("q", xq_s[k, :, c * T:(c + 1) * T],
                          fmq[:, k * T:(k + 1) * T], reads=("fmq",))
            P.barrier()
        _phase5()

        def _phase6():
            NSL = [(i * 512, 512) for i in range(5)] + [(2560, 256)]
            AR.reset()
            wgq = AR.bf(KD * DI)
            wuq = AR.bf(KD * DI)
            arsave = AR.off
            wtmp, wtmp_b = AR.fp(D)
            qt0 = AR.f(D)
            qt1 = AR.f(D)
            qt2 = AR.bf(D)
            qq = AR.bf(D)
            sc0 = AR.f(D // 16)
            sc1 = AR.f(D // 16)
            for src_h, dst, kdst in (((wg_h, wgq, "wgq"), (wu_h, wuq, "wuq"))
                                     if "p4w" in PHASES else ()):
                for nt in range(KI):
                    P.dma("g", wtmp_b,
                          src_h.bitcast(U16)[nt * 128:(nt + 1) * 128, :],
                          writes=("wtmp",))
                    quant_tok(P, wtmp, qq, qt0, qt1, qt2, sc0, sc1,
                              "wtmp", "qq", "qt", "qsc")
                    out = dst.rearrange("p (kc n) -> p kc n",
                                        kc=KD)[:, :, nt * 128:(nt + 1) * 128]
                    P.dma("q", out, qq, reads=("qq",), writes=(kdst,),
                          transpose=True)
            P.barrier()
            AR.off = arsave
            xqc = AR.bf(KD * T)
            sg = AR.bf(512)
            hband = AR.f(DI)
            hqb = AR.bf(DI)
            qt0 = AR.f(DI)
            qt1 = AR.f(DI)
            qt2 = AR.bf(DI)
            sc0 = AR.f(DI // 16)
            sc1 = AR.f(DI // 16)
            hqfm = AR.bf(KI * T)
            for c in range(NCH if "p4b" in PHASES else 0):
                for k in range(KD):
                    P.dma("g", xqc[:, k * T:(k + 1) * T],
                          xq_s[k, :, c * T:(c + 1) * T], writes=("xqc",))
                for tb in range(TB):
                    for off, nsz in NSL:
                        fg = [(lambda e, k=k, tb=tb, off=off, nsz=nsz: e.matmul(
                            pb[0][:, 0:nsz],
                            xqc[:, k * T + tb * 128: k * T + tb * 128 + 128],
                            wgq[:, k * DI + off: k * DI + off + nsz],
                            start=(k == 0), stop=(k == KD - 1)))
                            for k in range(KD)]
                        fu = [(lambda e, k=k, tb=tb, off=off, nsz=nsz: e.matmul(
                            pb[1][:, 0:nsz],
                            xqc[:, k * T + tb * 128: k * T + tb * 128 + 128],
                            wuq[:, k * DI + off: k * DI + off + nsz],
                            start=(k == 0), stop=(k == KD - 1)))
                            for k in range(KD)]
                        P.mm_group(fg, ("xqc", "wgq"), ("pb0",))
                        P.mm_group(fu, ("xqc", "wuq"), ("pb1",))
                        P.run("s", lambda e, nsz=nsz: e.activation(
                            sg[:, 0:nsz], pb[0][:, 0:nsz], AF.Silu),
                            ("pb0",), ("sg",))
                        P.run("v", lambda e, off=off, nsz=nsz: e.tensor_tensor(
                            hband[:, off:off + nsz], sg[:, 0:nsz],
                            pb[1][:, 0:nsz], ALU.mult),
                            ("sg", "pb1"), ("hband",))
                    quant_tok_multi(P, [
                        (hband, hqb, qt0, qt1, qt2, sc0, sc1,
                         "hband", "hqb", "hqt", "hsc")], g_adds=False)
                    out = hqfm.rearrange("p (kc b t) -> p b kc t", kc=KI,
                                         b=TB, t=128)[:, tb:tb + 1]
                    P.dma("q", out, hqb, reads=("hqb",), writes=("hqfm",),
                          transpose=True)
                for kc in range(KI):
                    P.dma("q", hq_s[kc, :, c * T:(c + 1) * T],
                          hqfm[:, kc * T:(kc + 1) * T], reads=("hqfm",))
            P.barrier()
        _phase6()

        def _phase7():
            AR.reset()
            wdq = AR.bf(KI * D)
            arsave = AR.off
            wtmp, wtmp_b = AR.fp(DI)
            qt0 = AR.f(DI)
            qt1 = AR.f(DI)
            qt2 = AR.bf(DI)
            qq = AR.bf(DI)
            sc0 = AR.f(DI // 16)
            sc1 = AR.f(DI // 16)
            for k in range(KD if "p4dw" in PHASES else 0):
                P.dma("g", wtmp_b,
                      wdn_h.bitcast(U16)[k * 128:(k + 1) * 128, :],
                      writes=("wtmp",))
                quant_tok(P, wtmp, qq, qt0, qt1, qt2, sc0, sc1,
                          "wtmp", "qq", "qt", "qsc")
                out = wdq.rearrange("p (kc n) -> p kc n",
                                    kc=KI)[:, :, k * 128:(k + 1) * 128]
                P.dma("q", out, qq, reads=("qq",), writes=("wdq",),
                      transpose=True)
            P.barrier()
            AR.off = arsave
            hqc = AR.bf(KI * T)
            act0, act0_b = AR.fp(TB * D)
            act1 = AR.f(TB * D)
            tmpq = AR.bf(TB * D)
            outb, outb_b = AR.fp(TB * D)
            stat = AR.f(16)
            for c in range(NCH if "p4c" in PHASES else 0):
                for kc in range(KI):
                    P.dma("g", hqc[:, kc * T:(kc + 1) * T],
                          hq_s[kc, :, c * T:(c + 1) * T], writes=("hqc",))
                P.dma("g", act0_b.rearrange("p (b f) -> p b f", f=2 * D),
                      x3_s.bitcast(U16)[c * T:(c + 1) * T, :].rearrange(
                          "(b p) f -> p b f", p=128),
                      writes=("act0",))
                for tb in range(TB):
                    p = pw[tb % 2]
                    kpw = f"pw{tb % 2}"
                    fns = []
                    for h in range(2):
                        for kc in range(KI):
                            fns.append(lambda e, kc=kc, h=h, tb=tb, p=p:
                                       e.matmul(
                                p[:, h * 512:(h + 1) * 512],
                                hqc[:, kc * T + tb * 128:
                                    kc * T + tb * 128 + 128],
                                wdq[:, kc * D + h * 512: kc * D + (h + 1) * 512],
                                start=(kc == 0), stop=(kc == KI - 1)))
                    P.mm_group(fns, ("hqc", "wdq"), (kpw,))
                    P.run("v", lambda e, tb=tb, p=p: e.tensor_add(
                        act1[:, tb * D:(tb + 1) * D], p[:],
                        act0[:, tb * D:(tb + 1) * D]),
                        (kpw, "act0"), ("act1",))
                ln_tok_multi(P, [
                    (act1[:, tb * D:(tb + 1) * D],
                     outb[:, tb * D:(tb + 1) * D],
                     tmpq[:, tb * D:(tb + 1) * D],
                     stat[:, tb * 4:(tb + 1) * 4],
                     "act1", "outb", f"tmpq{tb}", f"stat{tb}",
                     None, None, None)
                    for tb in range(TB)], epsc[:])
                P.dma("q",
                      out_h.bitcast(U16)[c * T:(c + 1) * T, :].rearrange(
                          "(b p) f -> p b f", p=128),
                      outb_b.rearrange("p (b f) -> p b f", f=2 * D),
                      reads=("outb",))
            P.barrier()
        _phase7()

        with nc.Block() as block:
            for ename, reg in (("tensor", "t"), ("vector", "v"),
                               ("scalar", "s"), ("gpsimd", "g"),
                               ("sync", "q")):
                def mk(reg):
                    def body(eng):
                        for th in P.thunks[reg]:
                            th(eng)
                    return body
                getattr(block, ename)(mk(reg))

    return nc


def _fmcols(vec, kd=KD):
    return np.ascontiguousarray(vec.reshape(kd, 128).T)


def kernel(**inputs):
    g = {k.replace("L0_attn_", "").replace("L0_ffn_", "f_"): np.asarray(v)
         for k, v in inputs.items()}
    x = np.ascontiguousarray(g["x"], dtype=np.float32)
    B = x.shape[0]

    cols = np.concatenate(
        [_fmcols(g["w1_b"]), _fmcols(g["v1_b"]), _fmcols(g["dt_b"]),
         _fmcols(g["D_p"])], axis=1).astype(np.float32)
    rows = np.concatenate(
        [g["ln1_g"], g["ln1_b"], g["ln_g"], g["ln_b"], g["ln2_g"],
         g["ln2_b"], g["w2_b"], g["ff2_b"]]).reshape(1, 8 * D).astype(
             np.float32)
    alog = np.ascontiguousarray(
        g["A_log"].reshape(KD, 128, NST).transpose(1, 0, 2).reshape(
            128, KD * NST)).astype(np.float32)
    convw = np.ascontiguousarray(
        g["conv_w"].reshape(D, 3).reshape(KD, 128, 3).transpose(
            1, 0, 2).reshape(128, KD * 3)).astype(np.float32)
    ff1b = _fmcols(g["ff1_b"], KFF).astype(np.float32)
    bcw = np.concatenate([g["Bp_w"], g["Cp_w"]], axis=1).astype(np.float32)
    bcb = np.concatenate([g["Bp_b"], g["Cp_b"]]).astype(np.float32)
    emat = np.eye(2 * NST, dtype=np.float32)[:, :, None].repeat(
        128, axis=2).reshape(2 * NST, 2 * NST * 128)

    shared = {
        "w1": g["w1_w"], "v1": g["v1_w"], "dtw": g["dt_w"], "w2": g["w2_w"],
        "bcw": bcw, "ff1": g["ff1_w"], "ff2": g["ff2_w"],
        "wg": g["f_w_gate"], "wu": g["f_w_up"], "wd": g["f_w_down"],
        "cols": cols, "ff1b": ff1b, "alog": alog, "convw": convw,
        "rows": rows, "emat": emat,
        "bcbc": bcb.reshape(2 * NST, 1),
    }
    shared = {k: np.ascontiguousarray(v, dtype=np.float32)
              for k, v in shared.items()}

    nc = build_nc()
    in_maps = [dict(shared, xt=np.ascontiguousarray(x[b])) for b in range(B)]
    res = run_bass_kernel_spmd(nc, in_maps, core_ids=list(range(B)))
    out = np.stack([np.asarray(res.results[b]["out"]) for b in range(B)])
    return out.astype(np.float32)


if __name__ == "__main__":
    nc = build_nc()
    print("build ok; instructions:", sum(1 for _ in nc.all_instructions()))


# version 08
# speedup vs baseline: 1.1398x; 1.0147x over previous
import numpy as np

import concourse.bass as bass
import concourse.mybir as mybir
from concourse.bass_utils import run_bass_kernel_spmd

F32 = mybir.dt.float32
BF16 = mybir.dt.bfloat16
U16 = mybir.dt.uint16
ALU = mybir.AluOpType
AF = mybir.ActivationFunctionType
AX = mybir.AxisListType

D = 1024
KD = D // 128
NST = 16
DFF = 3072
KFF = DFF // 128
DI = 2816
KI = DI // 128
SEQ = 8192
T = 512
TB = T // 128
EPS = 1e-5
AR_BYTES = 165 * 1024

DEBUG_SET = set()
PHASES = {"prep", "p1a", "p1b", "p2", "p3a", "p3b", "p3c", "p4w", "p4b",
          "p4dw", "p4c"}


class Prog:
    CENG = ("t", "v", "s", "g", "q")
    FENCE_DIST = 2

    def __init__(self, nc, sems, dsems):
        self.nc = nc
        self.sem = sems
        self.dsem = dsems
        self.cnt = {e: 0 for e in self.CENG}
        self.dcnt = {"q": 0, "g": 0}
        self.wm = {e: {} for e in self.CENG}
        self.bufs = {}
        self.thunks = {e: [] for e in self.CENG}

    def _buf(self, key):
        return self.bufs.setdefault(key, {"w": None, "r": {}})

    def _sem_of(self, sk):
        return self.sem[sk] if isinstance(sk, str) else self.dsem[sk[1]]

    def _record_wait(self, e, sk, val):
        if val <= 0 or self.wm[e].get(sk, 0) >= val:
            return
        sem = self._sem_of(sk)
        self.thunks[e].append(
            lambda eng, sem=sem, val=val: eng.wait_ge(sem, val))
        self.wm[e][sk] = val

    def _dma_val(self, sk, c):
        if isinstance(sk, tuple):
            return self.dcnt[sk[1]]
        return c

    def _record_waits(self, e, reads, writes):
        need = {}
        for key in tuple(reads) + tuple(writes):
            b = self._buf(key)
            if b["w"] is not None:
                sk, c = b["w"]
                if sk != e:
                    need[sk] = max(need.get(sk, 0), self._dma_val(sk, c))
                elif e in self.cnt and c >= self.cnt[e] - self.FENCE_DIST:
                    need[sk] = max(need.get(sk, 0), c)
        for key in writes:
            for sk, c in self._buf(key)["r"].items():
                if sk != e:
                    need[sk] = max(need.get(sk, 0), self._dma_val(sk, c))
                elif e in self.cnt and c >= self.cnt[e] - self.FENCE_DIST:
                    need[sk] = max(need.get(sk, 0), c)
        for sk, c in sorted(need.items(), key=str):
            self._record_wait(e, sk, c)

    def _mark(self, sk_self, c, reads, writes):
        for key in writes:
            b = self._buf(key)
            b["w"] = (sk_self, c)
            b["r"] = {}
        for key in reads:
            self._buf(key)["r"][sk_self] = c

    def run(self, e, fn, reads=(), writes=()):
        self._record_waits(e, reads, writes)
        self.cnt[e] += 1
        sem = self.sem[e]

        def thunk(eng, fn=fn, sem=sem):
            fn(eng).then_inc(sem, 1)

        self.thunks[e].append(thunk)
        self._mark(e, self.cnt[e], reads, writes)

    def mm_group(self, fns, reads=(), writes=()):
        self._record_waits("t", reads, writes)
        self.cnt["t"] += 1
        sem = self.sem["t"]

        def thunk(eng, fns=tuple(fns), sem=sem):
            instr = None
            for f in fns:
                instr = f(eng)
            instr.then_inc(sem, 1)

        self.thunks["t"].append(thunk)
        self._mark("t", self.cnt["t"], reads, writes)

    def dma(self, qe, out, in_, reads=(), writes=(), transpose=False):
        self._record_waits(qe, reads, writes)
        self.dcnt[qe] += 16
        dsem = self.dsem[qe]

        def thunk(eng, out=out, in_=in_, t=transpose, dsem=dsem):
            if t:
                eng.dma_start(out=out, in_=in_, transpose=True).then_inc(
                    dsem, 16)
            else:
                eng.dma_start(out=out, in_=in_).then_inc(dsem, 16)

        self.thunks[qe].append(thunk)
        self._mark(("D", qe), self.dcnt[qe], reads, writes)

    def barrier(self):
        snap = dict(self.cnt)
        dsnap = dict(self.dcnt)
        for e in self.CENG:
            for f in self.CENG:
                if f != e:
                    self._record_wait(e, f, snap[f])
            for qd in ("q", "g"):
                self._record_wait(e, ("D", qd), dsnap[qd])
        self.bufs = {}


class Arena:

    def __init__(self, ar_bf16, ar_f32):
        self.b = ar_bf16
        self.f4 = ar_f32
        self.off = 0

    def reset(self):
        self.off = 0

    def _take(self, nbytes, align=128):
        o = (self.off + align - 1) // align * align
        self.off = o + nbytes
        assert self.off <= AR_BYTES, f"arena overflow: {self.off}"
        return o

    def bf(self, elems, parts=128):
        o = self._take(elems * 2)
        return self.b[0:parts, o // 2: o // 2 + elems]

    def f(self, elems, parts=128):
        o = self._take(elems * 4)
        return self.f4[0:parts, o // 4: o // 4 + elems]

    def fp(self, elems, parts=128):
        o = self._take(elems * 4)
        return (self.f4[0:parts, o // 4: o // 4 + elems],
                self.b[0:parts, o // 2: o // 2 + 2 * elems].bitcast(U16))


def ln_tok(P, x_ap, out_ap, tmp_ap, stat, kx, kout, ktmp, kstat, eps_ap,
           grep_ap=None, brep_ap=None, kgrep=None):
    W = x_ap.shape[-1]
    s1, m, var, sd = (stat[:, i:i + 1] for i in range(4))
    P.run("v", lambda e: e.reduce_sum(s1, x_ap, AX.X), (kx,), (kstat,))
    P.run("s", lambda e: e.activation(m, s1, AF.Identity, scale=1.0 / W),
          (kstat,), (kstat,))
    P.run("v", lambda e: e.tensor_single_scalar(out_ap, x_ap, m,
                                                ALU.subtract),
          (kx, kstat), (kout,))
    P.run("v", lambda e: e.tensor_mul(tmp_ap, out_ap, out_ap),
          (kout,), (ktmp,))
    P.run("v", lambda e: e.reduce_sum(var, tmp_ap, AX.X), (ktmp,), (kstat,))
    P.run("s", lambda e: e.activation(var, var, AF.Identity, scale=1.0 / W),
          (kstat,), (kstat,))
    P.run("s", lambda e: e.activation(sd, var, AF.Sqrt, bias=eps_ap),
          (kstat, "epsc"), (kstat,))
    P.run("v", lambda e: e.reciprocal(sd, sd), (kstat,), (kstat,))
    P.run("v", lambda e: e.tensor_single_scalar(out_ap, out_ap, sd,
                                                ALU.mult),
          (kout, kstat), (kout,))
    if grep_ap is not None:
        P.run("v", lambda e: e.tensor_mul(out_ap, out_ap, grep_ap),
              (kout, kgrep), (kout,))
        P.run("v", lambda e: e.tensor_add(out_ap, out_ap, brep_ap),
              (kout, kgrep), (kout,))


def ln_tok_multi(P, jobs, eps_ap):
    js = []
    for (x, o, t, st, kx, ko, kt, ks, g, b, kg) in jobs:
        js.append((x, o, t, st[:, 0:1], st[:, 1:2], st[:, 2:3], st[:, 3:4],
                   kx, ko, kt, ks, g, b, kg, 1.0 / x.shape[-1]))
    for x, o, t, s1, m, var, sd, kx, ko, kt, ks, g, b, kg, iw in js:
        P.run("v", lambda e, s1=s1, x=x: e.reduce_sum(s1, x, AX.X),
              (kx,), (ks,))
    for x, o, t, s1, m, var, sd, kx, ko, kt, ks, g, b, kg, iw in js:
        P.run("s", lambda e, m=m, s1=s1, iw=iw: e.activation(
            m, s1, AF.Identity, scale=iw), (ks,), (ks,))
    for x, o, t, s1, m, var, sd, kx, ko, kt, ks, g, b, kg, iw in js:
        P.run("v", lambda e, o=o, x=x, m=m: e.tensor_single_scalar(
            o, x, m, ALU.subtract), (kx, ks), (ko,))
    for x, o, t, s1, m, var, sd, kx, ko, kt, ks, g, b, kg, iw in js:
        P.run("v", lambda e, t=t, o=o: e.tensor_mul(t, o, o), (ko,), (kt,))
    for x, o, t, s1, m, var, sd, kx, ko, kt, ks, g, b, kg, iw in js:
        P.run("v", lambda e, var=var, t=t: e.reduce_sum(var, t, AX.X),
              (kt,), (ks,))
    for x, o, t, s1, m, var, sd, kx, ko, kt, ks, g, b, kg, iw in js:
        P.run("s", lambda e, var=var, iw=iw: e.activation(
            var, var, AF.Identity, scale=iw), (ks,), (ks,))
    for x, o, t, s1, m, var, sd, kx, ko, kt, ks, g, b, kg, iw in js:
        P.run("s", lambda e, sd=sd, var=var: e.activation(
            sd, var, AF.Sqrt, bias=eps_ap), (ks, "epsc"), (ks,))
    for x, o, t, s1, m, var, sd, kx, ko, kt, ks, g, b, kg, iw in js:
        P.run("v", lambda e, sd=sd: e.reciprocal(sd, sd), (ks,), (ks,))
    for x, o, t, s1, m, var, sd, kx, ko, kt, ks, g, b, kg, iw in js:
        P.run("v", lambda e, o=o, sd=sd: e.tensor_single_scalar(
            o, o, sd, ALU.mult), (ko, ks), (ko,))
    if js[0][11] is not None:
        for x, o, t, s1, m, var, sd, kx, ko, kt, ks, g, b, kg, iw in js:
            P.run("v", lambda e, o=o, g=g: e.tensor_mul(o, o, g),
                  (ko, kg), (ko,))
        for x, o, t, s1, m, var, sd, kx, ko, kt, ks, g, b, kg, iw in js:
            P.run("v", lambda e, o=o, b=b: e.tensor_add(o, o, b),
                  (ko, kg), (ko,))


MIDS = [(0.25, 0.5), (0.75, 0.5), (1.25, 0.5), (1.75, 0.5),
        (2.5, 1.0), (3.5, 1.0), (5.0, 2.0)]


def quant_tok_multi(P, jobs, g_adds=False):
    js = []
    for (x, o, t0, t1, t2, sc0, sc1, kx, ko, kt, ks) in jobs:
        nb = x.shape[-1] // 16
        js.append((x.rearrange("p (s n) -> p s n", n=16), o,
                   o.rearrange("p (s n) -> p s n", n=16),
                   t0, t0.rearrange("p (s n) -> p s n", n=16),
                   t1, t2, t2.rearrange("p (s n) -> p s n", n=16),
                   sc0, sc1,
                   sc0.rearrange("p (s o) -> p s o", o=1).to_broadcast(
                       [128, nb, 16]),
                   sc1.rearrange("p (s o) -> p s o", o=1).to_broadcast(
                       [128, nb, 16]),
                   kx, ko, kt, ks))
    AE = "g" if g_adds else "v"
    for xv, o, ov, t0, t0v, t1, t2, t2v, sc0, sc1, rb, sb, kx, ko, kt, ks in js:
        P.run("v", lambda e, sc0=sc0, xv=xv: e.tensor_reduce(
            sc0, xv, AX.X, ALU.max, apply_absolute_value=True), (kx,), (ks,))
    for xv, o, ov, t0, t0v, t1, t2, t2v, sc0, sc1, rb, sb, kx, ko, kt, ks in js:
        P.run("v", lambda e, sc1=sc1, sc0=sc0: e.tensor_scalar(
            sc1, sc0, 1.0 / 6.0, 1e-12, ALU.mult, ALU.max), (ks,), (ks,))
    for xv, o, ov, t0, t0v, t1, t2, t2v, sc0, sc1, rb, sb, kx, ko, kt, ks in js:
        P.run("v", lambda e, sc0=sc0, sc1=sc1: e.reciprocal(sc0, sc1),
              (ks,), (ks,))
    for xv, o, ov, t0, t0v, t1, t2, t2v, sc0, sc1, rb, sb, kx, ko, kt, ks in js:
        P.run("v", lambda e, t0v=t0v, xv=xv, rb=rb: e.tensor_tensor(
            t0v, xv, rb, ALU.mult), (kx, ks), (kt,))
    for xv, o, ov, t0, t0v, t1, t2, t2v, sc0, sc1, rb, sb, kx, ko, kt, ks in js:
        P.run("v", lambda e, t1=t1, t0=t0: e.scalar_tensor_tensor(
            t1, t0, -1.0, t0, ALU.mult, ALU.max), (kt,), (kt,))
    m0, i0 = MIDS[0]
    for xv, o, ov, t0, t0v, t1, t2, t2v, sc0, sc1, rb, sb, kx, ko, kt, ks in js:
        P.run("v", lambda e, t2=t2, t1=t1: e.tensor_scalar(
            t2, t1, m0, i0, ALU.is_gt, ALU.mult), (kt,), (kt,))
    for m_i, inc in MIDS[1:]:
        for xv, o, ov, t0, t0v, t1, t2, t2v, sc0, sc1, rb, sb, kx, ko, kt, \
                ks in js:
            P.run("v", lambda e, o=o, t1=t1, m_i=m_i, inc=inc: e.tensor_scalar(
                o, t1, m_i, inc, ALU.is_gt, ALU.mult), (kt,), (ko,))
        for xv, o, ov, t0, t0v, t1, t2, t2v, sc0, sc1, rb, sb, kx, ko, kt, \
                ks in js:
            P.run(AE, lambda e, t2=t2, o=o: e.tensor_add(t2, t2, o),
                  (kt, ko), (kt,))
    for xv, o, ov, t0, t0v, t1, t2, t2v, sc0, sc1, rb, sb, kx, ko, kt, ks in js:
        P.run("v", lambda e, t1=t1, t0=t0: e.tensor_scalar(
            t1, t0, 0.0, 2.0, ALU.is_ge, ALU.mult), (kt,), (kt,))
    for xv, o, ov, t0, t0v, t1, t2, t2v, sc0, sc1, rb, sb, kx, ko, kt, ks in js:
        P.run("v", lambda e, t1=t1: e.tensor_single_scalar(
            t1, t1, -1.0, ALU.add), (kt,), (kt,))
    for xv, o, ov, t0, t0v, t1, t2, t2v, sc0, sc1, rb, sb, kx, ko, kt, ks in js:
        P.run("v", lambda e, t2=t2, t1=t1: e.tensor_mul(t2, t2, t1),
              (kt,), (kt,))
    for xv, o, ov, t0, t0v, t1, t2, t2v, sc0, sc1, rb, sb, kx, ko, kt, ks in js:
        P.run("v", lambda e, ov=ov, t2v=t2v, sb=sb: e.tensor_tensor(
            ov, t2v, sb, ALU.mult), (kt, ks), (ko,))


def quant_tok(P, x_ap, out_ap, t0, t1, t2, sc0, sc1, kx, kout, kt, ksc):
    W = x_ap.shape[-1]
    nb = W // 16
    xv = x_ap.rearrange("p (s n) -> p s n", n=16)
    P.run("v", lambda e: e.tensor_reduce(sc0, xv, AX.X, ALU.max,
                                         apply_absolute_value=True),
          (kx,), (ksc,))
    P.run("v", lambda e: e.tensor_scalar(sc1, sc0, 1.0 / 6.0, 1e-12,
                                         ALU.mult, ALU.max), (ksc,), (ksc,))
    P.run("v", lambda e: e.reciprocal(sc0, sc1), (ksc,), (ksc,))
    recb = sc0.rearrange("p (s o) -> p s o", o=1).to_broadcast([128, nb, 16])
    sclb = sc1.rearrange("p (s o) -> p s o", o=1).to_broadcast([128, nb, 16])
    t0v = t0.rearrange("p (s n) -> p s n", n=16)
    P.run("v", lambda e: e.tensor_tensor(t0v, xv, recb, ALU.mult),
          (kx, ksc), (kt,))
    P.run("v", lambda e: e.scalar_tensor_tensor(t1, t0, -1.0, t0, ALU.mult,
                                                ALU.max), (kt,), (kt,))
    m0, i0 = MIDS[0]
    P.run("v", lambda e: e.tensor_scalar(t2, t1, m0, i0, ALU.is_gt,
                                         ALU.mult), (kt,), (kt,))
    for m_i, inc in MIDS[1:]:
        P.run("v", lambda e, m_i=m_i, inc=inc: e.tensor_scalar(
            out_ap, t1, m_i, inc, ALU.is_gt, ALU.mult), (kt,), (kout,))
        P.run("v", lambda e: e.tensor_add(t2, t2, out_ap), (kt, kout), (kt,))
    P.run("v", lambda e: e.tensor_scalar(t1, t0, 0.0, 2.0, ALU.is_ge,
                                         ALU.mult), (kt,), (kt,))
    P.run("v", lambda e: e.tensor_single_scalar(t1, t1, -1.0, ALU.add),
          (kt,), (kt,))
    P.run("v", lambda e: e.tensor_mul(t2, t2, t1), (kt,), (kt,))
    t2v = t2.rearrange("p (s n) -> p s n", n=16)
    ov = out_ap.rearrange("p (s n) -> p s n", n=16)
    P.run("v", lambda e: e.tensor_tensor(ov, t2v, sclb, ALU.mult),
          (kt, ksc), (kout,))


def xbar_tok2fm(P, tok_ap, fm_ap, kin, kout, kd, tb_blocks):
    W = kd * 128
    for tb in range(tb_blocks):
        out = fm_ap.rearrange("p (k b t) -> p b k t", k=kd, b=tb_blocks,
                              t=128)[:, tb:tb + 1]
        P.dma("q", out, tok_ap[:, tb * W:(tb + 1) * W], reads=(kin,),
              writes=(kout,), transpose=True)


def build_nc():
    nc = bass.Bass("TRN2", target_bir_lowering=False)
    NCH = SEQ // T

    def din(name, shape):
        return nc.dram_tensor(name, shape, F32, kind="ExternalInput").ap()

    def dscr(name, shape, dtype=BF16):
        kind = "ExternalOutput" if name in DEBUG_SET else "Internal"
        return nc.dram_tensor(name, shape, dtype, kind=kind).ap()

    xt_h = din("xt", [SEQ, D])
    w1_h = din("w1", [D, D])
    v1_h = din("v1", [D, D])
    dt_h = din("dtw", [D, D])
    w2_h = din("w2", [D, D])
    bcw_h = din("bcw", [D, 2 * NST])
    ff1_h = din("ff1", [D, DFF])
    ff2_h = din("ff2", [DFF, D])
    wg_h = din("wg", [DI, D])
    wu_h = din("wu", [DI, D])
    wdn_h = din("wd", [D, DI])
    cols_h = din("cols", [128, 4 * KD])
    ff1b_h = din("ff1b", [128, KFF])
    alog_h = din("alog", [128, KD * NST])
    conv_h = din("convw", [128, KD * 3])
    rows_h = din("rows", [1, 8 * D])
    emat_h = din("emat", [2 * NST, 2 * NST * 128])
    bcbc_h = din("bcbc", [2 * NST, 1])
    out_h = nc.dram_tensor("out", [SEQ, D], F32, kind="ExternalOutput").ap()

    x1_s = dscr("x1s", [KD, 128, SEQ], F32)
    v_s = dscr("vss", [KD, 128, SEQ], F32)
    xc_s = dscr("xcs", [KD, 128, SEQ], F32)
    dr_s = dscr("drs", [KD, 128, SEQ], F32)
    bc_s = dscr("bcs", [2 * NST, SEQ], F32)
    z_s = dscr("zs", [KD, 128, SEQ])
    r2x_s = dscr("r2xs", [SEQ, D], F32)
    h1_s = dscr("h1s", [KFF, 128, SEQ])
    x3_s = dscr("x3s", [SEQ, D], F32)
    xq_s = dscr("xqs", [KD, 128, SEQ])
    hq_s = dscr("hqs", [KI, 128, SEQ])
    dbg = {}
    for nm, shp in (("dbg_xn", [SEQ, D]), ("dbg_xm", [SEQ, D]),
                    ("dbg_grep", [128, 6 * D]), ("dbg_mo", [SEQ, D])):
        if nm in DEBUG_SET:
            dbg[nm] = dscr(nm, shp)

    import contextlib
    with contextlib.ExitStack() as _st:
        en = _st.enter_context
        ones_row = en(nc.sbuf_tensor("ones_row", [1, T], BF16))
        rows2 = en(nc.sbuf_tensor("rows2", [1, 8 * D], BF16))
        e_mat = en(nc.sbuf_tensor("e_mat", [2 * NST, 2 * NST * 128], BF16))
        cols = en(nc.sbuf_tensor("colsb", [128, 4 * KD], F32))
        ff1b = en(nc.sbuf_tensor("ff1bb", [128, KFF], F32))
        acoef = en(nc.sbuf_tensor("acoef", [128, KD * NST], F32))
        convw = en(nc.sbuf_tensor("convwb", [128, KD * 3], F32))
        grep = en(nc.sbuf_tensor("grep", [128, 6 * D], BF16))
        grep2 = en(nc.sbuf_tensor("grep2", [128, 2 * D], BF16))
        bcbc = en(nc.sbuf_tensor("bcbcb", [2 * NST, 1], F32))
        h0 = en(nc.sbuf_tensor("h0b", [128, KD * NST], F32))
        inj = en(nc.sbuf_tensor("injb", [128, NST], F32))
        epsc = en(nc.sbuf_tensor("epscb", [128, 1], F32))
        arena_t = en(nc.sbuf_tensor("arena", [128, AR_BYTES // 2], BF16))
        pb0 = en(nc.psum_tensor("pb0", [128, 512], F32))
        pb1 = en(nc.psum_tensor("pb1", [128, 512], F32))
        pb2 = en(nc.psum_tensor("pb2", [128, 512], F32))
        pb3 = en(nc.psum_tensor("pb3", [128, 512], F32))
        pw0 = en(nc.psum_tensor("pw0", [128, 1024], F32))
        pw1 = en(nc.psum_tensor("pw1", [128, 1024], F32))
        St = en(nc.semaphore("St"))
        Sv = en(nc.semaphore("Sv"))
        Ss = en(nc.semaphore("Ss"))
        Sg = en(nc.semaphore("Sg"))
        Sq = en(nc.semaphore("Sq"))
        Eq = en(nc.semaphore("Eq"))
        Eg = en(nc.semaphore("Eg"))
        pb = [pb0, pb1, pb2, pb3]
        pw = [pw0, pw1]
        P = Prog(nc, {"t": St, "v": Sv, "s": Ss, "g": Sg, "q": Sq},
                 {"q": Eq, "g": Eg})
        AR = Arena(arena_t[:], arena_t[:].bitcast(F32))

        P.dma("g", rows2[:], rows_h[:, :], writes=("rows2",))
        P.dma("g", e_mat[:], emat_h[:, :], writes=("emat",))
        P.dma("q", bcbc[:], bcbc_h[:, :], writes=("bcbc",))
        P.dma("g", cols[:], cols_h[:, :], writes=("cols",))
        P.dma("g", ff1b[:], ff1b_h[:, :], writes=("ff1b",))
        P.dma("g", acoef[:], alog_h[:, :], writes=("acoef",))
        P.dma("g", convw[:], conv_h[:, :], writes=("convw",))
        P.run("v", lambda e: e.memset(ones_row[:], 1.0), (), ("ones",))
        P.run("s", lambda e: e.activation(acoef[:], acoef[:], AF.Exp),
              ("acoef",), ("acoef",))
        P.run("v", lambda e: e.tensor_single_scalar(acoef[:], acoef[:], -1.0,
                                                    ALU.mult),
              ("acoef",), ("acoef",))
        P.run("v", lambda e: e.memset(h0[:], 0.0), (), ("h0",))
        P.run("v", lambda e: e.memset(epsc[:], EPS), (), ("epsc",))
        for j in range(8):
            P.mm_group(
                [(lambda e, j=j, h=h: e.matmul(
                    pw[0][:, h * 512:(h + 1) * 512], ones_row[:1, 0:128],
                    rows2[:1, j * D + h * 512: j * D + (h + 1) * 512],
                    start=True, stop=True)) for h in range(2)],
                ("rows2", "ones"), ("pw0",))
            dstt = grep if j < 6 else grep2
            dk = "grep" if j < 6 else "grep2"
            jj = j if j < 6 else j - 6
            P.run("v", lambda e, jj=jj, dstt=dstt: e.tensor_copy(
                dstt[:, jj * D:(jj + 1) * D], pw[0][:]), ("pw0",), (dk,))
        if "dbg_grep" in dbg:
            P.dma("q", dbg["dbg_grep"][:, :], grep[:], reads=("grep",))
        P.barrier()

        def _phase0():
            AR.reset()
            wa = AR.bf(KD * D)
            wb = AR.bf(KD * D)
            act0d = [AR.fp(TB * D), AR.fp(TB * D)]
            act1 = AR.f(TB * D)
            act2 = AR.f(TB * D)
            xmb = AR.bf(TB * D)
            tmpq = AR.bf(TB * D)
            fmx = AR.bf(KD * T)
            ev0, ev0_b = AR.fp(KD * T)
            ev1, ev1_b = AR.fp(KD * T)
            stat = AR.f(16)
            for k in range(KD if "p1a" in PHASES else 0):
                P.dma("g", wa[:, k * D:(k + 1) * D],
                      w1_h[k * 128:(k + 1) * 128, :], writes=("wa",))
                P.dma("g", wb[:, k * D:(k + 1) * D],
                      v1_h[k * 128:(k + 1) * 128, :], writes=("wb",))
            for c in range(NCH if "p1a" in PHASES else 0):
                act0, act0_b = act0d[c % 2]
                ka0 = f"act0{c % 2}"
                P.dma("g", act0_b.rearrange("p (b f) -> p b f", f=2 * D),
                      xt_h.bitcast(U16)[c * T:(c + 1) * T, :].rearrange(
                          "(b p) f -> p b f", p=128),
                      writes=(ka0,))
                ln_tok_multi(P, [
                    (act0[:, tb * D:(tb + 1) * D],
                     act1[:, tb * D:(tb + 1) * D],
                     tmpq[:, tb * D:(tb + 1) * D],
                     stat[:, tb * 4:(tb + 1) * 4],
                     ka0, "act1", f"tmpq{tb}", f"stat{tb}",
                     grep[:, 0:D], grep[:, D:2 * D], "grep")
                    for tb in range(TB)], epsc[:])
                ln_tok_multi(P, [
                    (act1[:, tb * D:(tb + 1) * D],
                     act2[:, tb * D:(tb + 1) * D],
                     tmpq[:, tb * D:(tb + 1) * D],
                     stat[:, tb * 4:(tb + 1) * 4],
                     "act1", "act2", f"tmpq{tb}", f"stat{tb}",
                     grep[:, 2 * D:3 * D], grep[:, 3 * D:4 * D], "grep")
                    for tb in range(TB)], epsc[:])
                if "dbg_xn" in dbg:
                    P.dma("q", dbg["dbg_xn"][c * T:(c + 1) * T, :].rearrange(
                        "(b p) f -> p b f", p=128),
                        act1.rearrange("p (b f) -> p b f", f=D),
                        reads=("act1",))
                if "dbg_xm" in dbg:
                    P.dma("q", dbg["dbg_xm"][c * T:(c + 1) * T, :].rearrange(
                        "(b p) f -> p b f", p=128),
                        act2.rearrange("p (b f) -> p b f", f=D),
                        reads=("act2",))
                P.run("v", lambda e: e.tensor_copy(xmb, act2),
                      ("act2",), ("xmb",))
                xbar_tok2fm(P, xmb, fmx, "xmb", "fmx", KD, TB)
                for m in range(KD):
                    p = pb[m % 2]
                    kpb = f"pb{m % 2}"
                    P.mm_group(
                        [(lambda e, k=k, m=m, p=p: e.matmul(
                            p[:], wa[:, k * D + m * 128: k * D + (m + 1) * 128],
                            fmx[:, k * T:(k + 1) * T],
                            start=(k == 0), stop=(k == KD - 1)))
                         for k in range(KD)],
                        ("wa", "fmx"), (kpb,))
                    P.run("s", lambda e, m=m, p=p: e.activation(
                        ev0[:, m * T:(m + 1) * T], p[:], AF.Identity,
                        bias=cols[:, m:m + 1]),
                        (kpb, "cols"), (f"ev0{m % 2}",))
                    P.dma("q", x1_s.bitcast(U16)[m, :,
                          2 * c * T:2 * (c + 1) * T],
                          ev0_b[:, 2 * m * T:2 * (m + 1) * T],
                          reads=(f"ev0{m % 2}",))
                for m in range(KD):
                    p = pb[2 + m % 2]
                    kpb = f"pb{2 + m % 2}"
                    P.mm_group(
                        [(lambda e, k=k, m=m, p=p: e.matmul(
                            p[:], wb[:, k * D + m * 128: k * D + (m + 1) * 128],
                            fmx[:, k * T:(k + 1) * T],
                            start=(k == 0), stop=(k == KD - 1)))
                         for k in range(KD)],
                        ("wb", "fmx"), (kpb,))
                    P.run("s", lambda e, m=m, p=p: e.activation(
                        ev1[:, m * T:(m + 1) * T], p[:], AF.Silu,
                        bias=cols[:, KD + m:KD + m + 1]),
                        (kpb, "cols"), (f"ev1{m % 2}",))
                    P.dma("q", v_s.bitcast(U16)[m, :,
                          2 * c * T:2 * (c + 1) * T],
                          ev1_b[:, 2 * m * T:2 * (m + 1) * T],
                          reads=(f"ev1{m % 2}",))
            P.barrier()
        _phase0()

        def _phase1():
            AR.reset()
            wa = AR.bf(KD * D)
            bcw_sb = AR.bf(KD * 2 * NST)
            TH = T + 2
            x1h, x1h_b = AR.fp(KD * TH)
            xcb, xcb_b2 = AR.fp(KD * T)
            xcbb = AR.bf(KD * T)
            evd, evd_b = AR.fp(KD * T)
            bcf, bcf_b = AR.fp(T, parts=2 * NST)
            for k in range(KD if "p1b" in PHASES else 0):
                P.dma("g", wa[:, k * D:(k + 1) * D],
                      dt_h[k * 128:(k + 1) * 128, :], writes=("wa",))
                P.dma("g", bcw_sb[:, k * 2 * NST:(k + 1) * 2 * NST],
                      bcw_h[k * 128:(k + 1) * 128, :], writes=("bcw",))
            for c in range(NCH if "p1b" in PHASES else 0):
                for k in range(KD):
                    xk = x1h[:, k * TH:(k + 1) * TH]
                    lo = c * T - 1
                    hi = (c + 1) * T + 1
                    sl = 0
                    if c == 0:
                        P.run("v", lambda e, xk=xk: e.memset(xk[:, 0:1], 0.0),
                              ("x1h",), ("x1h",))
                        lo, sl = 0, 1
                    if c == NCH - 1:
                        P.run("v", lambda e, xk=xk: e.memset(
                            xk[:, TH - 1:TH], 0.0), ("x1h",), ("x1h",))
                        hi = SEQ
                    P.dma("g", x1h_b[:, 2 * (k * TH + sl):
                          2 * (k * TH + sl) + 2 * (hi - lo)],
                          x1_s.bitcast(U16)[k, :, 2 * lo:2 * hi],
                          writes=("x1h",))
                for k in range(KD):
                    xk = x1h[:, k * TH:(k + 1) * TH]
                    ck = xcb[:, k * T:(k + 1) * T]
                    P.run("v", lambda e, xk=xk, ck=ck, k=k:
                          e.tensor_single_scalar(
                              ck, xk[:, 0:T], convw[:, 3 * k:3 * k + 1],
                              ALU.mult),
                          ("x1h", "convw"), ("xcb",))
                    for tap in (1, 2):
                        P.run("v", lambda e, xk=xk, ck=ck, k=k, tap=tap:
                              e.scalar_tensor_tensor(
                                  ck, xk[:, tap:tap + T],
                                  convw[:, 3 * k + tap:3 * k + tap + 1],
                                  ck, ALU.mult, ALU.add),
                              ("x1h", "convw", "xcb"), ("xcb",))
                    P.run("s", lambda e, ck=ck: e.activation(ck, ck, AF.Silu),
                          ("xcb",), ("xcb",))
                    P.run("v", lambda e, ck=ck, k=k: e.tensor_copy(
                        xcbb[:, k * T:(k + 1) * T], ck),
                        ("xcb",), ("xcbb",))
                    P.dma("q", xc_s.bitcast(U16)[k, :,
                          2 * c * T:2 * (c + 1) * T],
                          xcb_b2[:, 2 * k * T:2 * (k + 1) * T],
                          reads=("xcb",))
                for m in range(KD):
                    p = pb[m % 2]
                    kpb = f"pb{m % 2}"
                    P.mm_group(
                        [(lambda e, k=k, m=m, p=p: e.matmul(
                            p[:], wa[:, k * D + m * 128: k * D + (m + 1) * 128],
                            xcbb[:, k * T:(k + 1) * T],
                            start=(k == 0), stop=(k == KD - 1)))
                         for k in range(KD)],
                        ("wa", "xcbb"), (kpb,))
                    P.run("s", lambda e, m=m, p=p: e.activation(
                        evd[:, m * T:(m + 1) * T], p[:], AF.Identity,
                        bias=cols[:, 2 * KD + m:2 * KD + m + 1]),
                        (kpb, "cols"), ("evd",))
                    P.dma("q", dr_s.bitcast(U16)[m, :,
                          2 * c * T:2 * (c + 1) * T],
                          evd_b[:, 2 * m * T:2 * (m + 1) * T],
                          reads=("evd",))
                P.mm_group(
                    [(lambda e, k=k: e.matmul(
                        pb[2][0:2 * NST, :],
                        bcw_sb[:, k * 2 * NST:(k + 1) * 2 * NST],
                        xcbb[:, k * T:(k + 1) * T],
                        start=(k == 0), stop=(k == KD - 1)))
                     for k in range(KD)],
                    ("bcw", "xcbb"), ("pb2",))
                P.run("s", lambda e: e.activation(bcf, pb[2][0:2 * NST, :],
                                                  AF.Identity, bias=bcbc[:]),
                      ("pb2", "bcbc"), ("bcf",))
                P.dma("q", bc_s.bitcast(U16)[:, 2 * c * T:2 * (c + 1) * T],
                  bcf_b, reads=("bcf",))
            P.barrier()
        _phase1()

        def _phase2():
            AR.reset()
            fmA, fmA_b = AR.fp(KD * T)
            fmB, fmB_b = AR.fp(KD * T)
            fmC = AR.bf(KD * T)
            fmD = AR.bf(KD * T)
            fmZ = fmC
            big0 = [AR.bf(NST * T), AR.bf(NST * T)]
            big1 = [AR.bf(NST * T), AR.bf(NST * T)]
            big2 = AR.bf(NST * T)
            brep = AR.bf(NST * T)
            crep = AR.bf(NST * T)
            bcch = AR.bf(T, parts=2 * NST)
            for c in range(NCH if "p2" in PHASES else 0):
                for k in range(KD):
                    P.dma("g", fmA_b[:, 2 * k * T:2 * (k + 1) * T],
                          dr_s.bitcast(U16)[k, :,
                          2 * c * T:2 * (c + 1) * T], writes=("fmA",))
                    P.dma("g", fmB_b[:, 2 * k * T:2 * (k + 1) * T],
                          xc_s.bitcast(U16)[k, :, 2 * c * T:2 * (c + 1) * T],
                          writes=("fmB",))
                    P.dma("g", fmD[:, k * T:(k + 1) * T],
                          v_s[k, :, c * T:(c + 1) * T], writes=("fmD",))
                P.dma("g", bcch, bc_s[:, c * T:(c + 1) * T], writes=("bcch",))
                for n in range(NST):
                    for half, dst, kdst, ev_eng in ((0, brep, "brep", "v"),
                                                    (1, crep, "crep", "s")):
                        p = pb[n % 2 + 2 * half]
                        kpb = f"pb{n % 2 + 2 * half}"
                        col = (half * NST + n) * 128
                        P.mm_group(
                            [(lambda e, col=col, p=p: e.matmul(
                                p[:], e_mat[:, col:col + 128], bcch,
                                start=True, stop=True))],
                            ("emat", "bcch"), (kpb,))
                        if ev_eng == "v":
                            P.run("v", lambda e, n=n, p=p, dst=dst:
                                  e.tensor_copy(dst[:, n * T:(n + 1) * T], p[:]),
                                  (kpb,), (kdst,))
                        else:
                            P.run("s", lambda e, n=n, p=p, dst=dst:
                                  e.activation(dst[:, n * T:(n + 1) * T], p[:],
                                               AF.Identity),
                                  (kpb,), (kdst,))
                for k in range(KD):
                    ak = fmA[:, k * T:(k + 1) * T]
                    ck = fmC[:, k * T:(k + 1) * T]
                    P.run("s", lambda e, ak=ak, ck=ck: e.activation(
                        ck, ak, AF.Abs), ("fmA",), ("fmC",))
                    P.run("s", lambda e, ck=ck: e.activation(
                        ck, ck, AF.Exp, scale=-1.0), ("fmC",), ("fmC",))
                    P.run("s", lambda e, ck=ck: e.activation(
                        ck, ck, AF.Ln, bias=1.0), ("fmC",), ("fmC",))
                    P.run("v", lambda e, ak=ak, ck=ck: e.scalar_tensor_tensor(
                        ak, ak, 0.0, ck, ALU.max, ALU.add),
                        ("fmA", "fmC"), ("fmA",))
                for k in range(KD):
                    ak = fmA[:, k * T:(k + 1) * T]
                    bk = fmB[:, k * T:(k + 1) * T]
                    ck = fmC[:, k * T:(k + 1) * T]
                    b0 = big0[k % 2]
                    b1 = big1[k % 2]
                    k0 = f"big0{k % 2}"
                    k1 = f"big1{k % 2}"
                    P.run("v", lambda e, ak=ak, bk=bk, ck=ck: e.tensor_mul(
                        ck, ak, bk), ("fmA", "fmB"), ("fmC",))
                    for n in range(NST):
                        P.run("s", lambda e, ak=ak, n=n, k=k, b0=b0:
                              e.activation(
                                  b0[:, n * T:(n + 1) * T], ak, AF.Exp,
                                  scale=acoef[:, k * NST + n:k * NST + n + 1]),
                              ("fmA", "acoef"), (k0,))
                    for n in range(NST):
                        P.run("g", lambda e, ck=ck, n=n, b1=b1: e.tensor_mul(
                            b1[:, n * T:(n + 1) * T], ck,
                            brep[:, n * T:(n + 1) * T]),
                            ("fmC", "brep"), (k1,))
                    vA = b0.rearrange("p (n t) -> p n t", t=T)[:, :, 0:1]
                    vB = b1.rearrange("p (n t) -> p n t", t=T)[:, :, 0:1]
                    h0k = h0[:, k * NST:(k + 1) * NST].rearrange(
                        "p (n o) -> p n o", o=1)
                    injv = inj[:].rearrange("p (n o) -> p n o", o=1)
                    P.run("v", lambda e, vA=vA, h0k=h0k, injv=injv:
                          e.tensor_tensor(injv, vA, h0k, ALU.mult),
                          (k0, "h0"), ("inj",))
                    P.run("v", lambda e, vB=vB, injv=injv: e.tensor_tensor(
                        vB, vB, injv, ALU.add), (k1, "inj"), (k1,))
                    P.run("v", lambda e, vA=vA: e.memset(vA, 0.0), (k0,), (k0,))
                    P.run("v", lambda e, b0=b0, b1=b1: e.tensor_tensor_scan(
                        big2, b0, b1, 0.0, ALU.mult, ALU.add),
                        (k0, k1), ("big2",))
                    vH = big2.rearrange("p (n t) -> p n t", t=T)[:, :, T - 1:T]
                    P.run("v", lambda e, vH=vH, h0k=h0k: e.tensor_copy(h0k, vH),
                          ("big2",), ("h0",))
                    P.run("v", lambda e: e.tensor_mul(big2, big2, crep),
                          ("big2", "crep"), ("big2",))
                    w = NST * T // 2
                    while w >= T:
                        P.run("v", lambda e, w=w: e.tensor_add(
                            big2[:, 0:w], big2[:, 0:w], big2[:, w:2 * w]),
                            ("big2",), ("big2",))
                        w //= 2
                    zk = fmZ[:, k * T:(k + 1) * T]
                    P.run("v", lambda e, zk=zk, bk=bk, k=k:
                          e.tensor_single_scalar(
                              zk, bk, cols[:, 3 * KD + k:3 * KD + k + 1],
                              ALU.mult),
                          ("fmB", "cols"), ("fmC",))
                    P.run("v", lambda e, zk=zk: e.tensor_add(
                        zk, zk, big2[:, 0:T]), ("fmC", "big2"), ("fmC",))
                    P.run("v", lambda e, zk=zk, k=k: e.tensor_mul(
                        zk, zk, fmD[:, k * T:(k + 1) * T]),
                        ("fmC", "fmD"), ("fmC",))
                    P.dma("q", z_s[k, :, c * T:(c + 1) * T], zk, reads=("fmC",))
            P.barrier()
        _phase2()

        def _phase3():
            AR.reset()
            wa = AR.bf(KD * D)
            fm0 = AR.bf(KD * T)
            act0, act0_b = AR.fp(TB * D)
            act1, act1_b = AR.fp(TB * D)
            for k in range(KD if "p3a" in PHASES else 0):
                P.dma("g", wa[:, k * D:(k + 1) * D],
                      w2_h[k * 128:(k + 1) * 128, :], writes=("wa",))
            for c in range(NCH if "p3a" in PHASES else 0):
                for k in range(KD):
                    P.dma("g", fm0[:, k * T:(k + 1) * T],
                          z_s[k, :, c * T:(c + 1) * T], writes=("fm0",))
                P.dma("g", act0_b.rearrange("p (b f) -> p b f", f=2 * D),
                      xt_h.bitcast(U16)[c * T:(c + 1) * T, :].rearrange(
                          "(b p) f -> p b f", p=128),
                      writes=("act0",))
                for tb in range(TB):
                    p = pw[tb % 2]
                    kpw = f"pw{tb % 2}"
                    fns = []
                    for h in range(2):
                        for k in range(KD):
                            fns.append(lambda e, k=k, h=h, tb=tb, p=p: e.matmul(
                                p[:, h * 512:(h + 1) * 512],
                                fm0[:, k * T + tb * 128: k * T + tb * 128 + 128],
                                wa[:, k * D + h * 512: k * D + (h + 1) * 512],
                                start=(k == 0), stop=(k == KD - 1)))
                    P.mm_group(fns, ("fm0", "wa"), (kpw,))
                    if "dbg_mo" in dbg:
                        P.run("s", lambda e, tb=tb, p=p: e.activation(
                            act1[:, tb * D:(tb + 1) * D], p[:], AF.Identity),
                            (kpw,), ("act1",))
                        P.dma("g", dbg["dbg_mo"][
                            c * T + tb * 128: c * T + (tb + 1) * 128, :],
                            act1[:, tb * D:(tb + 1) * D], reads=("act1",))
                    P.run("v", lambda e, tb=tb, p=p: e.tensor_add(
                        act1[:, tb * D:(tb + 1) * D], p[:], grep2[:, 0:D]),
                        (kpw, "grep2"), ("act1",))
                    P.run("v", lambda e, tb=tb: e.tensor_add(
                        act1[:, tb * D:(tb + 1) * D],
                        act1[:, tb * D:(tb + 1) * D],
                        act0[:, tb * D:(tb + 1) * D]),
                        ("act1", "act0"), ("act1",))
                P.dma("q",
                      r2x_s.bitcast(U16)[c * T:(c + 1) * T, :].rearrange(
                          "(b p) f -> p b f", p=128),
                      act1_b.rearrange("p (b f) -> p b f", f=2 * D),
                      reads=("act1",))
            P.barrier()
        _phase3()

        def _phase4():
            AR.reset()
            wc = AR.bf(KD * DFF)
            act0d = [AR.fp(TB * D), AR.fp(TB * D)]
            act1 = AR.bf(TB * D)
            tmpq = AR.bf(TB * D)
            fm0 = AR.bf(KD * T)
            evg = AR.bf(4 * T)
            stat = AR.f(16)
            for k in range(KD if "p3b" in PHASES else 0):
                P.dma("g", wc[:, k * DFF:(k + 1) * DFF],
                      ff1_h[k * 128:(k + 1) * 128, :], writes=("wc",))
            for c in range(NCH if "p3b" in PHASES else 0):
                act0, act0_b = act0d[c % 2]
                ka0 = f"act0{c % 2}"
                P.dma("g", act0_b.rearrange("p (b f) -> p b f", f=2 * D),
                      r2x_s.bitcast(U16)[c * T:(c + 1) * T, :].rearrange(
                          "(b p) f -> p b f", p=128),
                      writes=(ka0,))
                ln_tok_multi(P, [
                    (act0[:, tb * D:(tb + 1) * D],
                     act1[:, tb * D:(tb + 1) * D],
                     tmpq[:, tb * D:(tb + 1) * D],
                     stat[:, tb * 4:(tb + 1) * 4],
                     ka0, "act1", f"tmpq{tb}", f"stat{tb}",
                     grep[:, 4 * D:5 * D], grep[:, 5 * D:6 * D], "grep")
                    for tb in range(TB)], epsc[:])
                xbar_tok2fm(P, act1, fm0, "act1", "fm0", KD, TB)
                for m in range(KFF):
                    p = pb[m % 2]
                    kpb = f"pb{m % 2}"
                    P.mm_group(
                        [(lambda e, k=k, m=m, p=p: e.matmul(
                            p[:],
                            wc[:, k * DFF + m * 128: k * DFF + (m + 1) * 128],
                            fm0[:, k * T:(k + 1) * T],
                            start=(k == 0), stop=(k == KD - 1)))
                         for k in range(KD)],
                        ("wc", "fm0"), (kpb,))
                    P.run("s", lambda e, m=m, p=p: e.activation(
                        evg[:, (m % 4) * T:(m % 4 + 1) * T], p[:], AF.Gelu,
                        bias=ff1b[:, m:m + 1]),
                        (kpb, "ff1b"), (f"evg{m % 4}",))
                    P.dma("q", h1_s[m, :, c * T:(c + 1) * T],
                          evg[:, (m % 4) * T:(m % 4 + 1) * T],
                          reads=(f"evg{m % 4}",))
            P.barrier()
        _phase4()

        def _phase5():
            AR.reset()
            wd_sb = AR.bf(KFF * D)
            h1c = AR.bf(KFF * T)
            act0 = AR.bf(TB * D)
            act1, act1_b = AR.fp(TB * D)
            act2, act2_b = AR.fp(TB * D)
            tmpq = AR.bf(TB * D)
            tq0 = AR.f(2 * D)
            tq1 = AR.f(2 * D)
            tq2 = AR.bf(2 * D)
            sc0 = AR.f(2 * (D // 16))
            sc1 = AR.f(2 * (D // 16))
            xqb = AR.bf(TB * D)
            fmq = AR.bf(KD * T)
            stat = AR.f(16)
            for kc in range(KFF if "p3c" in PHASES else 0):
                P.dma("g", wd_sb[:, kc * D:(kc + 1) * D],
                      ff2_h[kc * 128:(kc + 1) * 128, :], writes=("wd",))
            for c in range(NCH if "p3c" in PHASES else 0):
                for kc in range(KFF):
                    P.dma("g", h1c[:, kc * T:(kc + 1) * T],
                          h1_s[kc, :, c * T:(c + 1) * T], writes=("h1c",))
                P.dma("g", act0.rearrange("p (b f) -> p b f", f=D),
                      xt_h[c * T:(c + 1) * T, :].rearrange(
                          "(b p) f -> p b f", p=128),
                      writes=("act0",))
                P.dma("g", act1_b.rearrange("p (b f) -> p b f", f=2 * D),
                      r2x_s.bitcast(U16)[c * T:(c + 1) * T, :].rearrange(
                          "(b p) f -> p b f", p=128),
                      writes=("act1",))
                for tb in range(TB):
                    p = pw[tb % 2]
                    kpw = f"pw{tb % 2}"
                    fns = []
                    for h in range(2):
                        for kc in range(KFF):
                            fns.append(lambda e, kc=kc, h=h, tb=tb, p=p:
                                       e.matmul(
                                p[:, h * 512:(h + 1) * 512],
                                h1c[:, kc * T + tb * 128:
                                    kc * T + tb * 128 + 128],
                                wd_sb[:, kc * D + h * 512:
                                      kc * D + (h + 1) * 512],
                                start=(kc == 0), stop=(kc == KFF - 1)))
                    P.mm_group(fns, ("h1c", "wd"), (kpw,))
                    P.run("v", lambda e, tb=tb, p=p: e.tensor_add(
                        act2[:, tb * D:(tb + 1) * D], p[:], grep2[:, D:2 * D]),
                        (kpw, "grep2"), ("act2",))
                    P.run("v", lambda e, tb=tb: e.tensor_add(
                        act2[:, tb * D:(tb + 1) * D],
                        act2[:, tb * D:(tb + 1) * D],
                        act1[:, tb * D:(tb + 1) * D]),
                        ("act2", "act1"), ("act2",))
                    P.run("v", lambda e, tb=tb: e.tensor_add(
                        act2[:, tb * D:(tb + 1) * D],
                        act2[:, tb * D:(tb + 1) * D],
                        act0[:, tb * D:(tb + 1) * D]),
                        ("act2", "act0"), ("act2",))
                ln_tok_multi(P, [
                    (act2[:, tb * D:(tb + 1) * D],
                     act2[:, tb * D:(tb + 1) * D],
                     tmpq[:, tb * D:(tb + 1) * D],
                     stat[:, tb * 4:(tb + 1) * 4],
                     "act2", "act2", f"tmpq{tb}", f"stat{tb}",
                     None, None, None)
                    for tb in range(TB)], epsc[:])
                P.dma("q",
                      x3_s.bitcast(U16)[c * T:(c + 1) * T, :].rearrange(
                          "(b p) f -> p b f", p=128),
                      act2_b.rearrange("p (b f) -> p b f", f=2 * D),
                      reads=("act2",))
                for j in range(2):
                    quant_tok_multi(P, [
                        (act2[:, tb * D:(tb + 1) * D],
                         xqb[:, tb * D:(tb + 1) * D],
                         tq0[:, i * D:(i + 1) * D],
                         tq1[:, i * D:(i + 1) * D],
                         tq2[:, i * D:(i + 1) * D],
                         sc0[:, i * (D // 16):(i + 1) * (D // 16)],
                         sc1[:, i * (D // 16):(i + 1) * (D // 16)],
                         "act2", "xqb", f"tq{i}", f"sc{i}")
                        for i, tb in enumerate((2 * j, 2 * j + 1))],
                        g_adds=False)
                xbar_tok2fm(P, xqb, fmq, "xqb", "fmq", KD, TB)
                for k in range(KD):
                    P.dma("q", xq_s[k, :, c * T:(c + 1) * T],
                          fmq[:, k * T:(k + 1) * T], reads=("fmq",))
            P.barrier()
        _phase5()

        def _phase6():
            NSL = [(i * 512, 512) for i in range(5)] + [(2560, 256)]
            AR.reset()
            wgq = AR.bf(KD * DI)
            wuq = AR.bf(KD * DI)
            arsave = AR.off
            wtmp, wtmp_b = AR.fp(D)
            qt0 = AR.f(D)
            qt1 = AR.f(D)
            qt2 = AR.bf(D)
            qq = AR.bf(D)
            sc0 = AR.f(D // 16)
            sc1 = AR.f(D // 16)
            for src_h, dst, kdst in (((wg_h, wgq, "wgq"), (wu_h, wuq, "wuq"))
                                     if "p4w" in PHASES else ()):
                for nt in range(KI):
                    P.dma("g", wtmp_b,
                          src_h.bitcast(U16)[nt * 128:(nt + 1) * 128, :],
                          writes=("wtmp",))
                    quant_tok(P, wtmp, qq, qt0, qt1, qt2, sc0, sc1,
                              "wtmp", "qq", "qt", "qsc")
                    out = dst.rearrange("p (kc n) -> p kc n",
                                        kc=KD)[:, :, nt * 128:(nt + 1) * 128]
                    P.dma("q", out, qq, reads=("qq",), writes=(kdst,),
                          transpose=True)
            P.barrier()
            AR.off = arsave
            xqc = AR.bf(KD * T)
            sg = AR.bf(512)
            hband = AR.f(DI)
            hqb = AR.bf(DI)
            qt0 = AR.f(DI)
            qt1 = AR.f(DI)
            qt2 = AR.bf(DI)
            sc0 = AR.f(DI // 16)
            sc1 = AR.f(DI // 16)
            hqfm = AR.bf(KI * T)
            for c in range(NCH if "p4b" in PHASES else 0):
                for k in range(KD):
                    P.dma("g", xqc[:, k * T:(k + 1) * T],
                          xq_s[k, :, c * T:(c + 1) * T], writes=("xqc",))
                for tb in range(TB):
                    for off, nsz in NSL:
                        fg = [(lambda e, k=k, tb=tb, off=off, nsz=nsz: e.matmul(
                            pb[0][:, 0:nsz],
                            xqc[:, k * T + tb * 128: k * T + tb * 128 + 128],
                            wgq[:, k * DI + off: k * DI + off + nsz],
                            start=(k == 0), stop=(k == KD - 1)))
                            for k in range(KD)]
                        fu = [(lambda e, k=k, tb=tb, off=off, nsz=nsz: e.matmul(
                            pb[1][:, 0:nsz],
                            xqc[:, k * T + tb * 128: k * T + tb * 128 + 128],
                            wuq[:, k * DI + off: k * DI + off + nsz],
                            start=(k == 0), stop=(k == KD - 1)))
                            for k in range(KD)]
                        P.mm_group(fg, ("xqc", "wgq"), ("pb0",))
                        P.mm_group(fu, ("xqc", "wuq"), ("pb1",))
                        P.run("s", lambda e, nsz=nsz: e.activation(
                            sg[:, 0:nsz], pb[0][:, 0:nsz], AF.Silu),
                            ("pb0",), ("sg",))
                        P.run("v", lambda e, off=off, nsz=nsz: e.tensor_tensor(
                            hband[:, off:off + nsz], sg[:, 0:nsz],
                            pb[1][:, 0:nsz], ALU.mult),
                            ("sg", "pb1"), ("hband",))
                    quant_tok_multi(P, [
                        (hband, hqb, qt0, qt1, qt2, sc0, sc1,
                         "hband", "hqb", "hqt", "hsc")], g_adds=False)
                    out = hqfm.rearrange("p (kc b t) -> p b kc t", kc=KI,
                                         b=TB, t=128)[:, tb:tb + 1]
                    P.dma("q", out, hqb, reads=("hqb",), writes=("hqfm",),
                          transpose=True)
                for kc in range(KI):
                    P.dma("q", hq_s[kc, :, c * T:(c + 1) * T],
                          hqfm[:, kc * T:(kc + 1) * T], reads=("hqfm",))
            P.barrier()
        _phase6()

        def _phase7():
            AR.reset()
            wdq = AR.bf(KI * D)
            arsave = AR.off
            wtmp, wtmp_b = AR.fp(DI)
            qt0 = AR.f(DI)
            qt1 = AR.f(DI)
            qt2 = AR.bf(DI)
            qq = AR.bf(DI)
            sc0 = AR.f(DI // 16)
            sc1 = AR.f(DI // 16)
            for k in range(KD if "p4dw" in PHASES else 0):
                P.dma("g", wtmp_b,
                      wdn_h.bitcast(U16)[k * 128:(k + 1) * 128, :],
                      writes=("wtmp",))
                quant_tok(P, wtmp, qq, qt0, qt1, qt2, sc0, sc1,
                          "wtmp", "qq", "qt", "qsc")
                out = wdq.rearrange("p (kc n) -> p kc n",
                                    kc=KI)[:, :, k * 128:(k + 1) * 128]
                P.dma("q", out, qq, reads=("qq",), writes=("wdq",),
                      transpose=True)
            P.barrier()
            AR.off = arsave
            hqcd = [AR.bf(KI * T), AR.bf(KI * T)]
            act0d = [AR.fp(TB * D), AR.fp(TB * D)]
            act1 = AR.f(TB * D)
            tmpq = AR.bf(TB * D)
            outb, outb_b = AR.fp(TB * D)
            stat = AR.f(16)
            for c in range(NCH if "p4c" in PHASES else 0):
                hqc = hqcd[c % 2]
                khq = f"hqc{c % 2}"
                act0, act0_b = act0d[c % 2]
                ka0 = f"act0{c % 2}"
                for kc in range(KI):
                    P.dma("g", hqc[:, kc * T:(kc + 1) * T],
                          hq_s[kc, :, c * T:(c + 1) * T], writes=(khq,))
                P.dma("g", act0_b.rearrange("p (b f) -> p b f", f=2 * D),
                      x3_s.bitcast(U16)[c * T:(c + 1) * T, :].rearrange(
                          "(b p) f -> p b f", p=128),
                      writes=(ka0,))
                for tb in range(TB):
                    p = pw[tb % 2]
                    kpw = f"pw{tb % 2}"
                    fns = []
                    for h in range(2):
                        for kc in range(KI):
                            fns.append(lambda e, kc=kc, h=h, tb=tb, p=p,
                                       hqc=hqc:
                                       e.matmul(
                                p[:, h * 512:(h + 1) * 512],
                                hqc[:, kc * T + tb * 128:
                                    kc * T + tb * 128 + 128],
                                wdq[:, kc * D + h * 512: kc * D + (h + 1) * 512],
                                start=(kc == 0), stop=(kc == KI - 1)))
                    P.mm_group(fns, (khq, "wdq"), (kpw,))
                    P.run("v", lambda e, tb=tb, p=p, act0=act0: e.tensor_add(
                        act1[:, tb * D:(tb + 1) * D], p[:],
                        act0[:, tb * D:(tb + 1) * D]),
                        (kpw, ka0), ("act1",))
                ln_tok_multi(P, [
                    (act1[:, tb * D:(tb + 1) * D],
                     outb[:, tb * D:(tb + 1) * D],
                     tmpq[:, tb * D:(tb + 1) * D],
                     stat[:, tb * 4:(tb + 1) * 4],
                     "act1", "outb", f"tmpq{tb}", f"stat{tb}",
                     None, None, None)
                    for tb in range(TB)], epsc[:])
                P.dma("q",
                      out_h.bitcast(U16)[c * T:(c + 1) * T, :].rearrange(
                          "(b p) f -> p b f", p=128),
                      outb_b.rearrange("p (b f) -> p b f", f=2 * D),
                      reads=("outb",))
            P.barrier()
        _phase7()

        with nc.Block() as block:
            for ename, reg in (("tensor", "t"), ("vector", "v"),
                               ("scalar", "s"), ("gpsimd", "g"),
                               ("sync", "q")):
                def mk(reg):
                    def body(eng):
                        for th in P.thunks[reg]:
                            th(eng)
                    return body
                getattr(block, ename)(mk(reg))

    return nc


def _fmcols(vec, kd=KD):
    return np.ascontiguousarray(vec.reshape(kd, 128).T)


def kernel(**inputs):
    g = {k.replace("L0_attn_", "").replace("L0_ffn_", "f_"): np.asarray(v)
         for k, v in inputs.items()}
    x = np.ascontiguousarray(g["x"], dtype=np.float32)
    B = x.shape[0]

    cols = np.concatenate(
        [_fmcols(g["w1_b"]), _fmcols(g["v1_b"]), _fmcols(g["dt_b"]),
         _fmcols(g["D_p"])], axis=1).astype(np.float32)
    rows = np.concatenate(
        [g["ln1_g"], g["ln1_b"], g["ln_g"], g["ln_b"], g["ln2_g"],
         g["ln2_b"], g["w2_b"], g["ff2_b"]]).reshape(1, 8 * D).astype(
             np.float32)
    alog = np.ascontiguousarray(
        g["A_log"].reshape(KD, 128, NST).transpose(1, 0, 2).reshape(
            128, KD * NST)).astype(np.float32)
    convw = np.ascontiguousarray(
        g["conv_w"].reshape(D, 3).reshape(KD, 128, 3).transpose(
            1, 0, 2).reshape(128, KD * 3)).astype(np.float32)
    ff1b = _fmcols(g["ff1_b"], KFF).astype(np.float32)
    bcw = np.concatenate([g["Bp_w"], g["Cp_w"]], axis=1).astype(np.float32)
    bcb = np.concatenate([g["Bp_b"], g["Cp_b"]]).astype(np.float32)
    emat = np.eye(2 * NST, dtype=np.float32)[:, :, None].repeat(
        128, axis=2).reshape(2 * NST, 2 * NST * 128)

    shared = {
        "w1": g["w1_w"], "v1": g["v1_w"], "dtw": g["dt_w"], "w2": g["w2_w"],
        "bcw": bcw, "ff1": g["ff1_w"], "ff2": g["ff2_w"],
        "wg": g["f_w_gate"], "wu": g["f_w_up"], "wd": g["f_w_down"],
        "cols": cols, "ff1b": ff1b, "alog": alog, "convw": convw,
        "rows": rows, "emat": emat,
        "bcbc": bcb.reshape(2 * NST, 1),
    }
    shared = {k: np.ascontiguousarray(v, dtype=np.float32)
              for k, v in shared.items()}

    nc = build_nc()
    in_maps = [dict(shared, xt=np.ascontiguousarray(x[b])) for b in range(B)]
    res = run_bass_kernel_spmd(nc, in_maps, core_ids=list(range(B)))
    out = np.stack([np.asarray(res.results[b]["out"]) for b in range(B)])
    return out.astype(np.float32)


if __name__ == "__main__":
    nc = build_nc()
    print("build ok; instructions:", sum(1 for _ in nc.all_instructions()))


# version 09 (final)
# speedup vs baseline: 1.1486x; 1.0078x over previous
import numpy as np

import concourse.bass as bass
import concourse.mybir as mybir
from concourse.bass_utils import run_bass_kernel_spmd

F32 = mybir.dt.float32
BF16 = mybir.dt.bfloat16
U16 = mybir.dt.uint16
ALU = mybir.AluOpType
AF = mybir.ActivationFunctionType
AX = mybir.AxisListType

D = 1024
KD = D // 128
NST = 16
DFF = 3072
KFF = DFF // 128
DI = 2816
KI = DI // 128
SEQ = 8192
T = 512
TB = T // 128
EPS = 1e-5
AR_BYTES = 165 * 1024

DEBUG_SET = set()
PHASES = {"prep", "p1a", "p1b", "p2", "p3a", "p3b", "p3c", "p4w", "p4b",
          "p4dw", "p4c"}


class Prog:
    CENG = ("t", "v", "s", "g", "q")
    FENCE_DIST = 2

    def __init__(self, nc, sems, dsems):
        self.nc = nc
        self.sem = sems
        self.dsem = dsems
        self.cnt = {e: 0 for e in self.CENG}
        self.dcnt = {"q": 0, "g": 0}
        self.wm = {e: {} for e in self.CENG}
        self.bufs = {}
        self.thunks = {e: [] for e in self.CENG}

    def _buf(self, key):
        return self.bufs.setdefault(key, {"w": None, "r": {}})

    def _sem_of(self, sk):
        return self.sem[sk] if isinstance(sk, str) else self.dsem[sk[1]]

    def _record_wait(self, e, sk, val):
        if val <= 0 or self.wm[e].get(sk, 0) >= val:
            return
        sem = self._sem_of(sk)
        self.thunks[e].append(
            lambda eng, sem=sem, val=val: eng.wait_ge(sem, val))
        self.wm[e][sk] = val

    def _dma_val(self, sk, c):
        if isinstance(sk, tuple):
            return self.dcnt[sk[1]]
        return c

    def _record_waits(self, e, reads, writes):
        need = {}
        for key in tuple(reads) + tuple(writes):
            b = self._buf(key)
            if b["w"] is not None:
                sk, c = b["w"]
                if sk != e:
                    need[sk] = max(need.get(sk, 0), self._dma_val(sk, c))
                elif e in self.cnt and c >= self.cnt[e] - self.FENCE_DIST:
                    need[sk] = max(need.get(sk, 0), c)
        for key in writes:
            for sk, c in self._buf(key)["r"].items():
                if sk != e:
                    need[sk] = max(need.get(sk, 0), self._dma_val(sk, c))
                elif e in self.cnt and c >= self.cnt[e] - self.FENCE_DIST:
                    need[sk] = max(need.get(sk, 0), c)
        for sk, c in sorted(need.items(), key=str):
            self._record_wait(e, sk, c)

    def _mark(self, sk_self, c, reads, writes):
        for key in writes:
            b = self._buf(key)
            b["w"] = (sk_self, c)
            b["r"] = {}
        for key in reads:
            self._buf(key)["r"][sk_self] = c

    def run(self, e, fn, reads=(), writes=()):
        self._record_waits(e, reads, writes)
        self.cnt[e] += 1
        sem = self.sem[e]

        def thunk(eng, fn=fn, sem=sem):
            fn(eng).then_inc(sem, 1)

        self.thunks[e].append(thunk)
        self._mark(e, self.cnt[e], reads, writes)

    def mm_group(self, fns, reads=(), writes=()):
        self._record_waits("t", reads, writes)
        self.cnt["t"] += 1
        sem = self.sem["t"]

        def thunk(eng, fns=tuple(fns), sem=sem):
            instr = None
            for f in fns:
                instr = f(eng)
            instr.then_inc(sem, 1)

        self.thunks["t"].append(thunk)
        self._mark("t", self.cnt["t"], reads, writes)

    def dma(self, qe, out, in_, reads=(), writes=(), transpose=False):
        self._record_waits(qe, reads, writes)
        self.dcnt[qe] += 16
        dsem = self.dsem[qe]

        def thunk(eng, out=out, in_=in_, t=transpose, dsem=dsem):
            if t:
                eng.dma_start(out=out, in_=in_, transpose=True).then_inc(
                    dsem, 16)
            else:
                eng.dma_start(out=out, in_=in_).then_inc(dsem, 16)

        self.thunks[qe].append(thunk)
        self._mark(("D", qe), self.dcnt[qe], reads, writes)

    def barrier(self):
        snap = dict(self.cnt)
        dsnap = dict(self.dcnt)
        for e in self.CENG:
            for f in self.CENG:
                if f != e:
                    self._record_wait(e, f, snap[f])
            for qd in ("q", "g"):
                self._record_wait(e, ("D", qd), dsnap[qd])
        self.bufs = {}


class Arena:

    def __init__(self, ar_bf16, ar_f32):
        self.b = ar_bf16
        self.f4 = ar_f32
        self.off = 0

    def reset(self):
        self.off = 0

    def _take(self, nbytes, align=128):
        o = (self.off + align - 1) // align * align
        self.off = o + nbytes
        assert self.off <= AR_BYTES, f"arena overflow: {self.off}"
        return o

    def bf(self, elems, parts=128):
        o = self._take(elems * 2)
        return self.b[0:parts, o // 2: o // 2 + elems]

    def f(self, elems, parts=128):
        o = self._take(elems * 4)
        return self.f4[0:parts, o // 4: o // 4 + elems]

    def fp(self, elems, parts=128):
        o = self._take(elems * 4)
        return (self.f4[0:parts, o // 4: o // 4 + elems],
                self.b[0:parts, o // 2: o // 2 + 2 * elems].bitcast(U16))


def ln_tok(P, x_ap, out_ap, tmp_ap, stat, kx, kout, ktmp, kstat, eps_ap,
           grep_ap=None, brep_ap=None, kgrep=None):
    W = x_ap.shape[-1]
    s1, m, var, sd = (stat[:, i:i + 1] for i in range(4))
    P.run("v", lambda e: e.reduce_sum(s1, x_ap, AX.X), (kx,), (kstat,))
    P.run("s", lambda e: e.activation(m, s1, AF.Identity, scale=1.0 / W),
          (kstat,), (kstat,))
    P.run("v", lambda e: e.tensor_single_scalar(out_ap, x_ap, m,
                                                ALU.subtract),
          (kx, kstat), (kout,))
    P.run("v", lambda e: e.tensor_mul(tmp_ap, out_ap, out_ap),
          (kout,), (ktmp,))
    P.run("v", lambda e: e.reduce_sum(var, tmp_ap, AX.X), (ktmp,), (kstat,))
    P.run("s", lambda e: e.activation(var, var, AF.Identity, scale=1.0 / W),
          (kstat,), (kstat,))
    P.run("s", lambda e: e.activation(sd, var, AF.Sqrt, bias=eps_ap),
          (kstat, "epsc"), (kstat,))
    P.run("v", lambda e: e.reciprocal(sd, sd), (kstat,), (kstat,))
    P.run("v", lambda e: e.tensor_single_scalar(out_ap, out_ap, sd,
                                                ALU.mult),
          (kout, kstat), (kout,))
    if grep_ap is not None:
        P.run("v", lambda e: e.tensor_mul(out_ap, out_ap, grep_ap),
              (kout, kgrep), (kout,))
        P.run("v", lambda e: e.tensor_add(out_ap, out_ap, brep_ap),
              (kout, kgrep), (kout,))


def ln_tok_multi(P, jobs, eps_ap):
    js = []
    for (x, o, t, st, kx, ko, kt, ks, g, b, kg) in jobs:
        js.append((x, o, t, st[:, 0:1], st[:, 1:2], st[:, 2:3], st[:, 3:4],
                   kx, ko, kt, ks, g, b, kg, 1.0 / x.shape[-1]))
    for x, o, t, s1, m, var, sd, kx, ko, kt, ks, g, b, kg, iw in js:
        P.run("v", lambda e, s1=s1, x=x: e.reduce_sum(s1, x, AX.X),
              (kx,), (ks,))
    for x, o, t, s1, m, var, sd, kx, ko, kt, ks, g, b, kg, iw in js:
        P.run("s", lambda e, m=m, s1=s1, iw=iw: e.activation(
            m, s1, AF.Identity, scale=iw), (ks,), (ks,))
    for x, o, t, s1, m, var, sd, kx, ko, kt, ks, g, b, kg, iw in js:
        P.run("v", lambda e, o=o, x=x, m=m: e.tensor_single_scalar(
            o, x, m, ALU.subtract), (kx, ks), (ko,))
    for x, o, t, s1, m, var, sd, kx, ko, kt, ks, g, b, kg, iw in js:
        P.run("v", lambda e, t=t, o=o: e.tensor_mul(t, o, o), (ko,), (kt,))
    for x, o, t, s1, m, var, sd, kx, ko, kt, ks, g, b, kg, iw in js:
        P.run("v", lambda e, var=var, t=t: e.reduce_sum(var, t, AX.X),
              (kt,), (ks,))
    for x, o, t, s1, m, var, sd, kx, ko, kt, ks, g, b, kg, iw in js:
        P.run("s", lambda e, var=var, iw=iw: e.activation(
            var, var, AF.Identity, scale=iw), (ks,), (ks,))
    for x, o, t, s1, m, var, sd, kx, ko, kt, ks, g, b, kg, iw in js:
        P.run("s", lambda e, sd=sd, var=var: e.activation(
            sd, var, AF.Sqrt, bias=eps_ap), (ks, "epsc"), (ks,))
    for x, o, t, s1, m, var, sd, kx, ko, kt, ks, g, b, kg, iw in js:
        P.run("v", lambda e, sd=sd: e.reciprocal(sd, sd), (ks,), (ks,))
    for x, o, t, s1, m, var, sd, kx, ko, kt, ks, g, b, kg, iw in js:
        P.run("v", lambda e, o=o, sd=sd: e.tensor_single_scalar(
            o, o, sd, ALU.mult), (ko, ks), (ko,))
    if js[0][11] is not None:
        for x, o, t, s1, m, var, sd, kx, ko, kt, ks, g, b, kg, iw in js:
            P.run("v", lambda e, o=o, g=g: e.tensor_mul(o, o, g),
                  (ko, kg), (ko,))
        for x, o, t, s1, m, var, sd, kx, ko, kt, ks, g, b, kg, iw in js:
            P.run("v", lambda e, o=o, b=b: e.tensor_add(o, o, b),
                  (ko, kg), (ko,))


MIDS = [(0.25, 0.5), (0.75, 0.5), (1.25, 0.5), (1.75, 0.5),
        (2.5, 1.0), (3.5, 1.0), (5.0, 2.0)]


def quant_tok_multi(P, jobs, g_adds=False):
    js = []
    for (x, o, t0, t1, t2, sc0, sc1, kx, ko, kt, ks) in jobs:
        nb = x.shape[-1] // 16
        js.append((x.rearrange("p (s n) -> p s n", n=16), o,
                   o.rearrange("p (s n) -> p s n", n=16),
                   t0, t0.rearrange("p (s n) -> p s n", n=16),
                   t1, t2, t2.rearrange("p (s n) -> p s n", n=16),
                   sc0, sc1,
                   sc0.rearrange("p (s o) -> p s o", o=1).to_broadcast(
                       [128, nb, 16]),
                   sc1.rearrange("p (s o) -> p s o", o=1).to_broadcast(
                       [128, nb, 16]),
                   kx, ko, kt, ks))
    AE = "g" if g_adds else "v"
    for xv, o, ov, t0, t0v, t1, t2, t2v, sc0, sc1, rb, sb, kx, ko, kt, ks in js:
        P.run("v", lambda e, sc0=sc0, xv=xv: e.tensor_reduce(
            sc0, xv, AX.X, ALU.max, apply_absolute_value=True), (kx,), (ks,))
    for xv, o, ov, t0, t0v, t1, t2, t2v, sc0, sc1, rb, sb, kx, ko, kt, ks in js:
        P.run("v", lambda e, sc1=sc1, sc0=sc0: e.tensor_scalar(
            sc1, sc0, 1.0 / 6.0, 1e-12, ALU.mult, ALU.max), (ks,), (ks,))
    for xv, o, ov, t0, t0v, t1, t2, t2v, sc0, sc1, rb, sb, kx, ko, kt, ks in js:
        P.run("v", lambda e, sc0=sc0, sc1=sc1: e.reciprocal(sc0, sc1),
              (ks,), (ks,))
    for xv, o, ov, t0, t0v, t1, t2, t2v, sc0, sc1, rb, sb, kx, ko, kt, ks in js:
        P.run("v", lambda e, t0v=t0v, xv=xv, rb=rb: e.tensor_tensor(
            t0v, xv, rb, ALU.mult), (kx, ks), (kt,))
    for xv, o, ov, t0, t0v, t1, t2, t2v, sc0, sc1, rb, sb, kx, ko, kt, ks in js:
        P.run("v", lambda e, t1=t1, t0=t0: e.scalar_tensor_tensor(
            t1, t0, -1.0, t0, ALU.mult, ALU.max), (kt,), (kt,))
    m0, i0 = MIDS[0]
    for xv, o, ov, t0, t0v, t1, t2, t2v, sc0, sc1, rb, sb, kx, ko, kt, ks in js:
        P.run("v", lambda e, t2=t2, t1=t1: e.tensor_scalar(
            t2, t1, m0, i0, ALU.is_gt, ALU.mult), (kt,), (kt,))
    for m_i, inc in MIDS[1:]:
        for xv, o, ov, t0, t0v, t1, t2, t2v, sc0, sc1, rb, sb, kx, ko, kt, \
                ks in js:
            P.run("v", lambda e, o=o, t1=t1, m_i=m_i, inc=inc: e.tensor_scalar(
                o, t1, m_i, inc, ALU.is_gt, ALU.mult), (kt,), (ko,))
        for xv, o, ov, t0, t0v, t1, t2, t2v, sc0, sc1, rb, sb, kx, ko, kt, \
                ks in js:
            P.run(AE, lambda e, t2=t2, o=o: e.tensor_add(t2, t2, o),
                  (kt, ko), (kt,))
    for xv, o, ov, t0, t0v, t1, t2, t2v, sc0, sc1, rb, sb, kx, ko, kt, ks in js:
        P.run("v", lambda e, t1=t1, t0=t0: e.tensor_scalar(
            t1, t0, 0.0, 2.0, ALU.is_ge, ALU.mult), (kt,), (kt,))
    for xv, o, ov, t0, t0v, t1, t2, t2v, sc0, sc1, rb, sb, kx, ko, kt, ks in js:
        P.run("v", lambda e, t1=t1: e.tensor_single_scalar(
            t1, t1, -1.0, ALU.add), (kt,), (kt,))
    for xv, o, ov, t0, t0v, t1, t2, t2v, sc0, sc1, rb, sb, kx, ko, kt, ks in js:
        P.run("v", lambda e, t2=t2, t1=t1: e.tensor_mul(t2, t2, t1),
              (kt,), (kt,))
    for xv, o, ov, t0, t0v, t1, t2, t2v, sc0, sc1, rb, sb, kx, ko, kt, ks in js:
        P.run("v", lambda e, ov=ov, t2v=t2v, sb=sb: e.tensor_tensor(
            ov, t2v, sb, ALU.mult), (kt, ks), (ko,))


def quant_tok(P, x_ap, out_ap, t0, t1, t2, sc0, sc1, kx, kout, kt, ksc):
    W = x_ap.shape[-1]
    nb = W // 16
    xv = x_ap.rearrange("p (s n) -> p s n", n=16)
    P.run("v", lambda e: e.tensor_reduce(sc0, xv, AX.X, ALU.max,
                                         apply_absolute_value=True),
          (kx,), (ksc,))
    P.run("v", lambda e: e.tensor_scalar(sc1, sc0, 1.0 / 6.0, 1e-12,
                                         ALU.mult, ALU.max), (ksc,), (ksc,))
    P.run("v", lambda e: e.reciprocal(sc0, sc1), (ksc,), (ksc,))
    recb = sc0.rearrange("p (s o) -> p s o", o=1).to_broadcast([128, nb, 16])
    sclb = sc1.rearrange("p (s o) -> p s o", o=1).to_broadcast([128, nb, 16])
    t0v = t0.rearrange("p (s n) -> p s n", n=16)
    P.run("v", lambda e: e.tensor_tensor(t0v, xv, recb, ALU.mult),
          (kx, ksc), (kt,))
    P.run("v", lambda e: e.scalar_tensor_tensor(t1, t0, -1.0, t0, ALU.mult,
                                                ALU.max), (kt,), (kt,))
    m0, i0 = MIDS[0]
    P.run("v", lambda e: e.tensor_scalar(t2, t1, m0, i0, ALU.is_gt,
                                         ALU.mult), (kt,), (kt,))
    for m_i, inc in MIDS[1:]:
        P.run("v", lambda e, m_i=m_i, inc=inc: e.tensor_scalar(
            out_ap, t1, m_i, inc, ALU.is_gt, ALU.mult), (kt,), (kout,))
        P.run("v", lambda e: e.tensor_add(t2, t2, out_ap), (kt, kout), (kt,))
    P.run("v", lambda e: e.tensor_scalar(t1, t0, 0.0, 2.0, ALU.is_ge,
                                         ALU.mult), (kt,), (kt,))
    P.run("v", lambda e: e.tensor_single_scalar(t1, t1, -1.0, ALU.add),
          (kt,), (kt,))
    P.run("v", lambda e: e.tensor_mul(t2, t2, t1), (kt,), (kt,))
    t2v = t2.rearrange("p (s n) -> p s n", n=16)
    ov = out_ap.rearrange("p (s n) -> p s n", n=16)
    P.run("v", lambda e: e.tensor_tensor(ov, t2v, sclb, ALU.mult),
          (kt, ksc), (kout,))


def xbar_tok2fm(P, tok_ap, fm_ap, kin, kout, kd, tb_blocks):
    W = kd * 128
    for tb in range(tb_blocks):
        out = fm_ap.rearrange("p (k b t) -> p b k t", k=kd, b=tb_blocks,
                              t=128)[:, tb:tb + 1]
        P.dma("q", out, tok_ap[:, tb * W:(tb + 1) * W], reads=(kin,),
              writes=(kout,), transpose=True)


def build_nc():
    nc = bass.Bass("TRN2", target_bir_lowering=False)
    NCH = SEQ // T

    def din(name, shape):
        return nc.dram_tensor(name, shape, F32, kind="ExternalInput").ap()

    def dscr(name, shape, dtype=BF16):
        kind = "ExternalOutput" if name in DEBUG_SET else "Internal"
        return nc.dram_tensor(name, shape, dtype, kind=kind).ap()

    xt_h = din("xt", [SEQ, D])
    w1_h = din("w1", [D, D])
    v1_h = din("v1", [D, D])
    dt_h = din("dtw", [D, D])
    w2_h = din("w2", [D, D])
    bcw_h = din("bcw", [D, 2 * NST])
    ff1_h = din("ff1", [D, DFF])
    ff2_h = din("ff2", [DFF, D])
    wg_h = din("wg", [DI, D])
    wu_h = din("wu", [DI, D])
    wdn_h = din("wd", [D, DI])
    cols_h = din("cols", [128, 4 * KD])
    ff1b_h = din("ff1b", [128, KFF])
    alog_h = din("alog", [128, KD * NST])
    conv_h = din("convw", [128, KD * 3])
    rows_h = din("rows", [1, 8 * D])
    emat_h = din("emat", [2 * NST, 2 * NST * 128])
    bcbc_h = din("bcbc", [2 * NST, 1])
    out_h = nc.dram_tensor("out", [SEQ, D], F32, kind="ExternalOutput").ap()

    x1_s = dscr("x1s", [KD, 128, SEQ], F32)
    v_s = dscr("vss", [KD, 128, SEQ], F32)
    xc_s = dscr("xcs", [KD, 128, SEQ], F32)
    dr_s = dscr("drs", [KD, 128, SEQ], F32)
    bc_s = dscr("bcs", [2 * NST, SEQ], F32)
    z_s = dscr("zs", [KD, 128, SEQ])
    r2x_s = dscr("r2xs", [SEQ, D], F32)
    h1_s = dscr("h1s", [KFF, 128, SEQ])
    x3_s = dscr("x3s", [SEQ, D], F32)
    xq_s = dscr("xqs", [KD, 128, SEQ])
    hq_s = dscr("hqs", [KI, 128, SEQ])
    dbg = {}
    for nm, shp in (("dbg_xn", [SEQ, D]), ("dbg_xm", [SEQ, D]),
                    ("dbg_grep", [128, 6 * D]), ("dbg_mo", [SEQ, D])):
        if nm in DEBUG_SET:
            dbg[nm] = dscr(nm, shp)

    import contextlib
    with contextlib.ExitStack() as _st:
        en = _st.enter_context
        ones_row = en(nc.sbuf_tensor("ones_row", [1, T], BF16))
        rows2 = en(nc.sbuf_tensor("rows2", [1, 8 * D], BF16))
        e_mat = en(nc.sbuf_tensor("e_mat", [2 * NST, 2 * NST * 128], BF16))
        cols = en(nc.sbuf_tensor("colsb", [128, 4 * KD], F32))
        ff1b = en(nc.sbuf_tensor("ff1bb", [128, KFF], F32))
        acoef = en(nc.sbuf_tensor("acoef", [128, KD * NST], F32))
        convw = en(nc.sbuf_tensor("convwb", [128, KD * 3], F32))
        grep = en(nc.sbuf_tensor("grep", [128, 6 * D], BF16))
        grep2 = en(nc.sbuf_tensor("grep2", [128, 2 * D], BF16))
        bcbc = en(nc.sbuf_tensor("bcbcb", [2 * NST, 1], F32))
        h0 = en(nc.sbuf_tensor("h0b", [128, KD * NST], F32))
        inj = en(nc.sbuf_tensor("injb", [128, NST], F32))
        epsc = en(nc.sbuf_tensor("epscb", [128, 1], F32))
        arena_t = en(nc.sbuf_tensor("arena", [128, AR_BYTES // 2], BF16))
        pb0 = en(nc.psum_tensor("pb0", [128, 512], F32))
        pb1 = en(nc.psum_tensor("pb1", [128, 512], F32))
        pb2 = en(nc.psum_tensor("pb2", [128, 512], F32))
        pb3 = en(nc.psum_tensor("pb3", [128, 512], F32))
        pw0 = en(nc.psum_tensor("pw0", [128, 1024], F32))
        pw1 = en(nc.psum_tensor("pw1", [128, 1024], F32))
        St = en(nc.semaphore("St"))
        Sv = en(nc.semaphore("Sv"))
        Ss = en(nc.semaphore("Ss"))
        Sg = en(nc.semaphore("Sg"))
        Sq = en(nc.semaphore("Sq"))
        Eq = en(nc.semaphore("Eq"))
        Eg = en(nc.semaphore("Eg"))
        pb = [pb0, pb1, pb2, pb3]
        pw = [pw0, pw1]
        P = Prog(nc, {"t": St, "v": Sv, "s": Ss, "g": Sg, "q": Sq},
                 {"q": Eq, "g": Eg})
        AR = Arena(arena_t[:], arena_t[:].bitcast(F32))

        P.dma("g", rows2[:], rows_h[:, :], writes=("rows2",))
        P.dma("g", e_mat[:], emat_h[:, :], writes=("emat",))
        P.dma("q", bcbc[:], bcbc_h[:, :], writes=("bcbc",))
        P.dma("g", cols[:], cols_h[:, :], writes=("cols",))
        P.dma("g", ff1b[:], ff1b_h[:, :], writes=("ff1b",))
        P.dma("g", acoef[:], alog_h[:, :], writes=("acoef",))
        P.dma("g", convw[:], conv_h[:, :], writes=("convw",))
        P.run("v", lambda e: e.memset(ones_row[:], 1.0), (), ("ones",))
        P.run("s", lambda e: e.activation(acoef[:], acoef[:], AF.Exp),
              ("acoef",), ("acoef",))
        P.run("v", lambda e: e.tensor_single_scalar(acoef[:], acoef[:], -1.0,
                                                    ALU.mult),
              ("acoef",), ("acoef",))
        P.run("v", lambda e: e.memset(h0[:], 0.0), (), ("h0",))
        P.run("v", lambda e: e.memset(epsc[:], EPS), (), ("epsc",))
        for j in range(8):
            P.mm_group(
                [(lambda e, j=j, h=h: e.matmul(
                    pw[0][:, h * 512:(h + 1) * 512], ones_row[:1, 0:128],
                    rows2[:1, j * D + h * 512: j * D + (h + 1) * 512],
                    start=True, stop=True)) for h in range(2)],
                ("rows2", "ones"), ("pw0",))
            dstt = grep if j < 6 else grep2
            dk = "grep" if j < 6 else "grep2"
            jj = j if j < 6 else j - 6
            P.run("v", lambda e, jj=jj, dstt=dstt: e.tensor_copy(
                dstt[:, jj * D:(jj + 1) * D], pw[0][:]), ("pw0",), (dk,))
        if "dbg_grep" in dbg:
            P.dma("q", dbg["dbg_grep"][:, :], grep[:], reads=("grep",))
        P.barrier()

        def _phase0():
            AR.reset()
            wa = AR.bf(KD * D)
            wb = AR.bf(KD * D)
            act0d = [AR.fp(TB * D), AR.fp(TB * D)]
            act1 = AR.f(TB * D)
            act2 = AR.f(TB * D)
            xmb = AR.bf(TB * D)
            tmpq = AR.bf(TB * D)
            fmx = AR.bf(KD * T)
            ev0, ev0_b = AR.fp(KD * T)
            ev1, ev1_b = AR.fp(KD * T)
            stat = AR.f(16)
            for k in range(KD if "p1a" in PHASES else 0):
                P.dma("g", wa[:, k * D:(k + 1) * D],
                      w1_h[k * 128:(k + 1) * 128, :], writes=("wa",))
                P.dma("g", wb[:, k * D:(k + 1) * D],
                      v1_h[k * 128:(k + 1) * 128, :], writes=("wb",))
            for c in range(NCH if "p1a" in PHASES else 0):
                act0, act0_b = act0d[c % 2]
                ka0 = f"act0{c % 2}"
                P.dma("g", act0_b.rearrange("p (b f) -> p b f", f=2 * D),
                      xt_h.bitcast(U16)[c * T:(c + 1) * T, :].rearrange(
                          "(b p) f -> p b f", p=128),
                      writes=(ka0,))
                ln_tok_multi(P, [
                    (act0[:, tb * D:(tb + 1) * D],
                     act1[:, tb * D:(tb + 1) * D],
                     tmpq[:, tb * D:(tb + 1) * D],
                     stat[:, tb * 4:(tb + 1) * 4],
                     ka0, "act1", f"tmpq{tb}", f"stat{tb}",
                     grep[:, 0:D], grep[:, D:2 * D], "grep")
                    for tb in range(TB)], epsc[:])
                ln_tok_multi(P, [
                    (act1[:, tb * D:(tb + 1) * D],
                     act2[:, tb * D:(tb + 1) * D],
                     tmpq[:, tb * D:(tb + 1) * D],
                     stat[:, tb * 4:(tb + 1) * 4],
                     "act1", "act2", f"tmpq{tb}", f"stat{tb}",
                     grep[:, 2 * D:3 * D], grep[:, 3 * D:4 * D], "grep")
                    for tb in range(TB)], epsc[:])
                if "dbg_xn" in dbg:
                    P.dma("q", dbg["dbg_xn"][c * T:(c + 1) * T, :].rearrange(
                        "(b p) f -> p b f", p=128),
                        act1.rearrange("p (b f) -> p b f", f=D),
                        reads=("act1",))
                if "dbg_xm" in dbg:
                    P.dma("q", dbg["dbg_xm"][c * T:(c + 1) * T, :].rearrange(
                        "(b p) f -> p b f", p=128),
                        act2.rearrange("p (b f) -> p b f", f=D),
                        reads=("act2",))
                P.run("v", lambda e: e.tensor_copy(xmb, act2),
                      ("act2",), ("xmb",))
                xbar_tok2fm(P, xmb, fmx, "xmb", "fmx", KD, TB)
                for m in range(KD):
                    p = pb[m % 2]
                    kpb = f"pb{m % 2}"
                    P.mm_group(
                        [(lambda e, k=k, m=m, p=p: e.matmul(
                            p[:], wa[:, k * D + m * 128: k * D + (m + 1) * 128],
                            fmx[:, k * T:(k + 1) * T],
                            start=(k == 0), stop=(k == KD - 1)))
                         for k in range(KD)],
                        ("wa", "fmx"), (kpb,))
                    P.run("s", lambda e, m=m, p=p: e.activation(
                        ev0[:, m * T:(m + 1) * T], p[:], AF.Identity,
                        bias=cols[:, m:m + 1]),
                        (kpb, "cols"), (f"ev0{m % 2}",))
                    P.dma("q", x1_s.bitcast(U16)[m, :,
                          2 * c * T:2 * (c + 1) * T],
                          ev0_b[:, 2 * m * T:2 * (m + 1) * T],
                          reads=(f"ev0{m % 2}",))
                for m in range(KD):
                    p = pb[2 + m % 2]
                    kpb = f"pb{2 + m % 2}"
                    P.mm_group(
                        [(lambda e, k=k, m=m, p=p: e.matmul(
                            p[:], wb[:, k * D + m * 128: k * D + (m + 1) * 128],
                            fmx[:, k * T:(k + 1) * T],
                            start=(k == 0), stop=(k == KD - 1)))
                         for k in range(KD)],
                        ("wb", "fmx"), (kpb,))
                    P.run("s", lambda e, m=m, p=p: e.activation(
                        ev1[:, m * T:(m + 1) * T], p[:], AF.Silu,
                        bias=cols[:, KD + m:KD + m + 1]),
                        (kpb, "cols"), (f"ev1{m % 2}",))
                    P.dma("q", v_s.bitcast(U16)[m, :,
                          2 * c * T:2 * (c + 1) * T],
                          ev1_b[:, 2 * m * T:2 * (m + 1) * T],
                          reads=(f"ev1{m % 2}",))
            P.barrier()
        _phase0()

        def _phase1():
            AR.reset()
            wa = AR.bf(KD * D)
            bcw_sb = AR.bf(KD * 2 * NST)
            TH = T + 2
            x1h, x1h_b = AR.fp(KD * TH)
            xcb, xcb_b2 = AR.fp(KD * T)
            xcbb = AR.bf(KD * T)
            evd, evd_b = AR.fp(KD * T)
            bcf, bcf_b = AR.fp(T, parts=2 * NST)
            for k in range(KD if "p1b" in PHASES else 0):
                P.dma("g", wa[:, k * D:(k + 1) * D],
                      dt_h[k * 128:(k + 1) * 128, :], writes=("wa",))
                P.dma("g", bcw_sb[:, k * 2 * NST:(k + 1) * 2 * NST],
                      bcw_h[k * 128:(k + 1) * 128, :], writes=("bcw",))
            for c in range(NCH if "p1b" in PHASES else 0):
                for k in range(KD):
                    xk = x1h[:, k * TH:(k + 1) * TH]
                    lo = c * T - 1
                    hi = (c + 1) * T + 1
                    sl = 0
                    if c == 0:
                        P.run("v", lambda e, xk=xk: e.memset(xk[:, 0:1], 0.0),
                              ("x1h",), ("x1h",))
                        lo, sl = 0, 1
                    if c == NCH - 1:
                        P.run("v", lambda e, xk=xk: e.memset(
                            xk[:, TH - 1:TH], 0.0), ("x1h",), ("x1h",))
                        hi = SEQ
                    P.dma("g", x1h_b[:, 2 * (k * TH + sl):
                          2 * (k * TH + sl) + 2 * (hi - lo)],
                          x1_s.bitcast(U16)[k, :, 2 * lo:2 * hi],
                          writes=("x1h",))
                for k in range(KD):
                    xk = x1h[:, k * TH:(k + 1) * TH]
                    ck = xcb[:, k * T:(k + 1) * T]
                    P.run("v", lambda e, xk=xk, ck=ck, k=k:
                          e.tensor_single_scalar(
                              ck, xk[:, 0:T], convw[:, 3 * k:3 * k + 1],
                              ALU.mult),
                          ("x1h", "convw"), ("xcb",))
                    for tap in (1, 2):
                        P.run("v", lambda e, xk=xk, ck=ck, k=k, tap=tap:
                              e.scalar_tensor_tensor(
                                  ck, xk[:, tap:tap + T],
                                  convw[:, 3 * k + tap:3 * k + tap + 1],
                                  ck, ALU.mult, ALU.add),
                              ("x1h", "convw", "xcb"), ("xcb",))
                    P.run("s", lambda e, ck=ck: e.activation(ck, ck, AF.Silu),
                          ("xcb",), ("xcb",))
                    P.run("v", lambda e, ck=ck, k=k: e.tensor_copy(
                        xcbb[:, k * T:(k + 1) * T], ck),
                        ("xcb",), ("xcbb",))
                    P.dma("q", xc_s.bitcast(U16)[k, :,
                          2 * c * T:2 * (c + 1) * T],
                          xcb_b2[:, 2 * k * T:2 * (k + 1) * T],
                          reads=("xcb",))
                for m in range(KD):
                    p = pb[m % 2]
                    kpb = f"pb{m % 2}"
                    P.mm_group(
                        [(lambda e, k=k, m=m, p=p: e.matmul(
                            p[:], wa[:, k * D + m * 128: k * D + (m + 1) * 128],
                            xcbb[:, k * T:(k + 1) * T],
                            start=(k == 0), stop=(k == KD - 1)))
                         for k in range(KD)],
                        ("wa", "xcbb"), (kpb,))
                    P.run("s", lambda e, m=m, p=p: e.activation(
                        evd[:, m * T:(m + 1) * T], p[:], AF.Identity,
                        bias=cols[:, 2 * KD + m:2 * KD + m + 1]),
                        (kpb, "cols"), ("evd",))
                    P.dma("q", dr_s.bitcast(U16)[m, :,
                          2 * c * T:2 * (c + 1) * T],
                          evd_b[:, 2 * m * T:2 * (m + 1) * T],
                          reads=("evd",))
                P.mm_group(
                    [(lambda e, k=k: e.matmul(
                        pb[2][0:2 * NST, :],
                        bcw_sb[:, k * 2 * NST:(k + 1) * 2 * NST],
                        xcbb[:, k * T:(k + 1) * T],
                        start=(k == 0), stop=(k == KD - 1)))
                     for k in range(KD)],
                    ("bcw", "xcbb"), ("pb2",))
                P.run("s", lambda e: e.activation(bcf, pb[2][0:2 * NST, :],
                                                  AF.Identity, bias=bcbc[:]),
                      ("pb2", "bcbc"), ("bcf",))
                P.dma("q", bc_s.bitcast(U16)[:, 2 * c * T:2 * (c + 1) * T],
                  bcf_b, reads=("bcf",))
            P.barrier()
        _phase1()

        def _phase2():
            AR.reset()
            fmA, fmA_b = AR.fp(KD * T)
            fmB, fmB_b = AR.fp(KD * T)
            fmC = AR.bf(KD * T)
            fmD = AR.bf(KD * T)
            fmZ = fmC
            big0 = [AR.bf(NST * T), AR.bf(NST * T)]
            big1 = [AR.bf(NST * T), AR.bf(NST * T)]
            big2 = AR.bf(NST * T)
            brep = AR.bf(NST * T)
            crep = AR.bf(NST * T)
            bcch = AR.bf(T, parts=2 * NST)
            for c in range(NCH if "p2" in PHASES else 0):
                for k in range(KD):
                    P.dma("g", fmA_b[:, 2 * k * T:2 * (k + 1) * T],
                          dr_s.bitcast(U16)[k, :,
                          2 * c * T:2 * (c + 1) * T], writes=("fmA",))
                    P.dma("g", fmB_b[:, 2 * k * T:2 * (k + 1) * T],
                          xc_s.bitcast(U16)[k, :, 2 * c * T:2 * (c + 1) * T],
                          writes=("fmB",))
                    P.dma("g", fmD[:, k * T:(k + 1) * T],
                          v_s[k, :, c * T:(c + 1) * T], writes=("fmD",))
                P.dma("g", bcch, bc_s[:, c * T:(c + 1) * T], writes=("bcch",))
                for n in range(NST):
                    for half, dst, kdst, ev_eng in ((0, brep, "brep", "v"),
                                                    (1, crep, "crep", "s")):
                        p = pb[n % 2 + 2 * half]
                        kpb = f"pb{n % 2 + 2 * half}"
                        col = (half * NST + n) * 128
                        P.mm_group(
                            [(lambda e, col=col, p=p: e.matmul(
                                p[:], e_mat[:, col:col + 128], bcch,
                                start=True, stop=True))],
                            ("emat", "bcch"), (kpb,))
                        if ev_eng == "v":
                            P.run("v", lambda e, n=n, p=p, dst=dst:
                                  e.tensor_copy(dst[:, n * T:(n + 1) * T], p[:]),
                                  (kpb,), (kdst,))
                        else:
                            P.run("s", lambda e, n=n, p=p, dst=dst:
                                  e.activation(dst[:, n * T:(n + 1) * T], p[:],
                                               AF.Identity),
                                  (kpb,), (kdst,))
                for k in range(KD):
                    ak = fmA[:, k * T:(k + 1) * T]
                    ck = fmC[:, k * T:(k + 1) * T]
                    P.run("s", lambda e, ak=ak, ck=ck: e.activation(
                        ck, ak, AF.Abs), ("fmA",), ("fmC",))
                    P.run("s", lambda e, ck=ck: e.activation(
                        ck, ck, AF.Exp, scale=-1.0), ("fmC",), ("fmC",))
                    P.run("s", lambda e, ck=ck: e.activation(
                        ck, ck, AF.Ln, bias=1.0), ("fmC",), ("fmC",))
                    P.run("v", lambda e, ak=ak, ck=ck: e.scalar_tensor_tensor(
                        ak, ak, 0.0, ck, ALU.max, ALU.add),
                        ("fmA", "fmC"), ("fmA",))
                for k in range(KD):
                    ak = fmA[:, k * T:(k + 1) * T]
                    bk = fmB[:, k * T:(k + 1) * T]
                    ck = fmC[:, k * T:(k + 1) * T]
                    b0 = big0[k % 2]
                    b1 = big1[k % 2]
                    k0 = f"big0{k % 2}"
                    k1 = f"big1{k % 2}"
                    P.run("v", lambda e, ak=ak, bk=bk, ck=ck: e.tensor_mul(
                        ck, ak, bk), ("fmA", "fmB"), ("fmC",))
                    for n in range(NST):
                        P.run("s", lambda e, ak=ak, n=n, k=k, b0=b0:
                              e.activation(
                                  b0[:, n * T:(n + 1) * T], ak, AF.Exp,
                                  scale=acoef[:, k * NST + n:k * NST + n + 1]),
                              ("fmA", "acoef"), (k0,))
                    for n in range(NST):
                        P.run("g", lambda e, ck=ck, n=n, b1=b1: e.tensor_mul(
                            b1[:, n * T:(n + 1) * T], ck,
                            brep[:, n * T:(n + 1) * T]),
                            ("fmC", "brep"), (k1,))
                    vA = b0.rearrange("p (n t) -> p n t", t=T)[:, :, 0:1]
                    vB = b1.rearrange("p (n t) -> p n t", t=T)[:, :, 0:1]
                    h0k = h0[:, k * NST:(k + 1) * NST].rearrange(
                        "p (n o) -> p n o", o=1)
                    injv = inj[:].rearrange("p (n o) -> p n o", o=1)
                    P.run("v", lambda e, vA=vA, h0k=h0k, injv=injv:
                          e.tensor_tensor(injv, vA, h0k, ALU.mult),
                          (k0, "h0"), ("inj",))
                    P.run("v", lambda e, vB=vB, injv=injv: e.tensor_tensor(
                        vB, vB, injv, ALU.add), (k1, "inj"), (k1,))
                    P.run("v", lambda e, vA=vA: e.memset(vA, 0.0), (k0,), (k0,))
                    P.run("v", lambda e, b0=b0, b1=b1: e.tensor_tensor_scan(
                        big2, b0, b1, 0.0, ALU.mult, ALU.add),
                        (k0, k1), ("big2",))
                    vH = big2.rearrange("p (n t) -> p n t", t=T)[:, :, T - 1:T]
                    P.run("v", lambda e, vH=vH, h0k=h0k: e.tensor_copy(h0k, vH),
                          ("big2",), ("h0",))
                    P.run("v", lambda e: e.tensor_mul(big2, big2, crep),
                          ("big2", "crep"), ("big2",))
                    w = NST * T // 2
                    while w >= T:
                        P.run("v", lambda e, w=w: e.tensor_add(
                            big2[:, 0:w], big2[:, 0:w], big2[:, w:2 * w]),
                            ("big2",), ("big2",))
                        w //= 2
                    zk = fmZ[:, k * T:(k + 1) * T]
                    P.run("v", lambda e, zk=zk, bk=bk, k=k:
                          e.tensor_single_scalar(
                              zk, bk, cols[:, 3 * KD + k:3 * KD + k + 1],
                              ALU.mult),
                          ("fmB", "cols"), ("fmC",))
                    P.run("v", lambda e, zk=zk: e.tensor_add(
                        zk, zk, big2[:, 0:T]), ("fmC", "big2"), ("fmC",))
                    P.run("v", lambda e, zk=zk, k=k: e.tensor_mul(
                        zk, zk, fmD[:, k * T:(k + 1) * T]),
                        ("fmC", "fmD"), ("fmC",))
                    P.dma("q", z_s[k, :, c * T:(c + 1) * T], zk, reads=("fmC",))
            P.barrier()
        _phase2()

        def _phase3():
            AR.reset()
            wa = AR.bf(KD * D)
            fm0d = [AR.bf(KD * T), AR.bf(KD * T)]
            act0d = [AR.fp(TB * D), AR.fp(TB * D)]
            act1, act1_b = AR.fp(TB * D)
            for k in range(KD if "p3a" in PHASES else 0):
                P.dma("g", wa[:, k * D:(k + 1) * D],
                      w2_h[k * 128:(k + 1) * 128, :], writes=("wa",))
            for c in range(NCH if "p3a" in PHASES else 0):
                fm0 = fm0d[c % 2]
                kf0 = f"fm0{c % 2}"
                act0, act0_b = act0d[c % 2]
                ka0 = f"act0{c % 2}"
                for k in range(KD):
                    P.dma("g", fm0[:, k * T:(k + 1) * T],
                          z_s[k, :, c * T:(c + 1) * T], writes=(kf0,))
                P.dma("g", act0_b.rearrange("p (b f) -> p b f", f=2 * D),
                      xt_h.bitcast(U16)[c * T:(c + 1) * T, :].rearrange(
                          "(b p) f -> p b f", p=128),
                      writes=(ka0,))
                for tb in range(TB):
                    p = pw[tb % 2]
                    kpw = f"pw{tb % 2}"
                    fns = []
                    for h in range(2):
                        for k in range(KD):
                            fns.append(lambda e, k=k, h=h, tb=tb, p=p,
                                       fm0=fm0: e.matmul(
                                p[:, h * 512:(h + 1) * 512],
                                fm0[:, k * T + tb * 128: k * T + tb * 128 + 128],
                                wa[:, k * D + h * 512: k * D + (h + 1) * 512],
                                start=(k == 0), stop=(k == KD - 1)))
                    P.mm_group(fns, (kf0, "wa"), (kpw,))
                    if "dbg_mo" in dbg:
                        P.run("s", lambda e, tb=tb, p=p: e.activation(
                            act1[:, tb * D:(tb + 1) * D], p[:], AF.Identity),
                            (kpw,), ("act1",))
                        P.dma("g", dbg["dbg_mo"][
                            c * T + tb * 128: c * T + (tb + 1) * 128, :],
                            act1[:, tb * D:(tb + 1) * D], reads=("act1",))
                    P.run("v", lambda e, tb=tb, p=p: e.tensor_add(
                        act1[:, tb * D:(tb + 1) * D], p[:], grep2[:, 0:D]),
                        (kpw, "grep2"), ("act1",))
                    P.run("v", lambda e, tb=tb, act0=act0: e.tensor_add(
                        act1[:, tb * D:(tb + 1) * D],
                        act1[:, tb * D:(tb + 1) * D],
                        act0[:, tb * D:(tb + 1) * D]),
                        ("act1", ka0), ("act1",))
                P.dma("q",
                      r2x_s.bitcast(U16)[c * T:(c + 1) * T, :].rearrange(
                          "(b p) f -> p b f", p=128),
                      act1_b.rearrange("p (b f) -> p b f", f=2 * D),
                      reads=("act1",))
            P.barrier()
        _phase3()

        def _phase4():
            AR.reset()
            wc = AR.bf(KD * DFF)
            act0d = [AR.fp(TB * D), AR.fp(TB * D)]
            act1 = AR.bf(TB * D)
            tmpq = AR.bf(TB * D)
            fm0 = AR.bf(KD * T)
            evg = AR.bf(4 * T)
            stat = AR.f(16)
            for k in range(KD if "p3b" in PHASES else 0):
                P.dma("g", wc[:, k * DFF:(k + 1) * DFF],
                      ff1_h[k * 128:(k + 1) * 128, :], writes=("wc",))
            for c in range(NCH if "p3b" in PHASES else 0):
                act0, act0_b = act0d[c % 2]
                ka0 = f"act0{c % 2}"
                P.dma("g", act0_b.rearrange("p (b f) -> p b f", f=2 * D),
                      r2x_s.bitcast(U16)[c * T:(c + 1) * T, :].rearrange(
                          "(b p) f -> p b f", p=128),
                      writes=(ka0,))
                ln_tok_multi(P, [
                    (act0[:, tb * D:(tb + 1) * D],
                     act1[:, tb * D:(tb + 1) * D],
                     tmpq[:, tb * D:(tb + 1) * D],
                     stat[:, tb * 4:(tb + 1) * 4],
                     ka0, "act1", f"tmpq{tb}", f"stat{tb}",
                     grep[:, 4 * D:5 * D], grep[:, 5 * D:6 * D], "grep")
                    for tb in range(TB)], epsc[:])
                xbar_tok2fm(P, act1, fm0, "act1", "fm0", KD, TB)
                for m in range(KFF):
                    p = pb[m % 2]
                    kpb = f"pb{m % 2}"
                    P.mm_group(
                        [(lambda e, k=k, m=m, p=p: e.matmul(
                            p[:],
                            wc[:, k * DFF + m * 128: k * DFF + (m + 1) * 128],
                            fm0[:, k * T:(k + 1) * T],
                            start=(k == 0), stop=(k == KD - 1)))
                         for k in range(KD)],
                        ("wc", "fm0"), (kpb,))
                    P.run("s", lambda e, m=m, p=p: e.activation(
                        evg[:, (m % 4) * T:(m % 4 + 1) * T], p[:], AF.Gelu,
                        bias=ff1b[:, m:m + 1]),
                        (kpb, "ff1b"), (f"evg{m % 4}",))
                    P.dma("q", h1_s[m, :, c * T:(c + 1) * T],
                          evg[:, (m % 4) * T:(m % 4 + 1) * T],
                          reads=(f"evg{m % 4}",))
            P.barrier()
        _phase4()

        def _phase5():
            AR.reset()
            wd_sb = AR.bf(KFF * D)
            h1c = AR.bf(KFF * T)
            act0 = AR.bf(TB * D)
            act1, act1_b = AR.fp(TB * D)
            act2, act2_b = AR.fp(TB * D)
            tmpq = AR.bf(TB * D)
            tq0 = AR.f(2 * D)
            tq1 = AR.f(2 * D)
            tq2 = AR.bf(2 * D)
            sc0 = AR.f(2 * (D // 16))
            sc1 = AR.f(2 * (D // 16))
            xqb = AR.bf(TB * D)
            fmq = AR.bf(KD * T)
            stat = AR.f(16)
            for kc in range(KFF if "p3c" in PHASES else 0):
                P.dma("g", wd_sb[:, kc * D:(kc + 1) * D],
                      ff2_h[kc * 128:(kc + 1) * 128, :], writes=("wd",))
            for c in range(NCH if "p3c" in PHASES else 0):
                for kc in range(KFF):
                    P.dma("g", h1c[:, kc * T:(kc + 1) * T],
                          h1_s[kc, :, c * T:(c + 1) * T], writes=("h1c",))
                P.dma("g", act0.rearrange("p (b f) -> p b f", f=D),
                      xt_h[c * T:(c + 1) * T, :].rearrange(
                          "(b p) f -> p b f", p=128),
                      writes=("act0",))
                P.dma("g", act1_b.rearrange("p (b f) -> p b f", f=2 * D),
                      r2x_s.bitcast(U16)[c * T:(c + 1) * T, :].rearrange(
                          "(b p) f -> p b f", p=128),
                      writes=("act1",))
                for tb in range(TB):
                    p = pw[tb % 2]
                    kpw = f"pw{tb % 2}"
                    fns = []
                    for h in range(2):
                        for kc in range(KFF):
                            fns.append(lambda e, kc=kc, h=h, tb=tb, p=p:
                                       e.matmul(
                                p[:, h * 512:(h + 1) * 512],
                                h1c[:, kc * T + tb * 128:
                                    kc * T + tb * 128 + 128],
                                wd_sb[:, kc * D + h * 512:
                                      kc * D + (h + 1) * 512],
                                start=(kc == 0), stop=(kc == KFF - 1)))
                    P.mm_group(fns, ("h1c", "wd"), (kpw,))
                    P.run("v", lambda e, tb=tb, p=p: e.tensor_add(
                        act2[:, tb * D:(tb + 1) * D], p[:], grep2[:, D:2 * D]),
                        (kpw, "grep2"), ("act2",))
                    P.run("v", lambda e, tb=tb: e.tensor_add(
                        act2[:, tb * D:(tb + 1) * D],
                        act2[:, tb * D:(tb + 1) * D],
                        act1[:, tb * D:(tb + 1) * D]),
                        ("act2", "act1"), ("act2",))
                    P.run("v", lambda e, tb=tb: e.tensor_add(
                        act2[:, tb * D:(tb + 1) * D],
                        act2[:, tb * D:(tb + 1) * D],
                        act0[:, tb * D:(tb + 1) * D]),
                        ("act2", "act0"), ("act2",))
                ln_tok_multi(P, [
                    (act2[:, tb * D:(tb + 1) * D],
                     act2[:, tb * D:(tb + 1) * D],
                     tmpq[:, tb * D:(tb + 1) * D],
                     stat[:, tb * 4:(tb + 1) * 4],
                     "act2", "act2", f"tmpq{tb}", f"stat{tb}",
                     None, None, None)
                    for tb in range(TB)], epsc[:])
                P.dma("q",
                      x3_s.bitcast(U16)[c * T:(c + 1) * T, :].rearrange(
                          "(b p) f -> p b f", p=128),
                      act2_b.rearrange("p (b f) -> p b f", f=2 * D),
                      reads=("act2",))
                for j in range(2):
                    quant_tok_multi(P, [
                        (act2[:, tb * D:(tb + 1) * D],
                         xqb[:, tb * D:(tb + 1) * D],
                         tq0[:, i * D:(i + 1) * D],
                         tq1[:, i * D:(i + 1) * D],
                         tq2[:, i * D:(i + 1) * D],
                         sc0[:, i * (D // 16):(i + 1) * (D // 16)],
                         sc1[:, i * (D // 16):(i + 1) * (D // 16)],
                         "act2", "xqb", f"tq{i}", f"sc{i}")
                        for i, tb in enumerate((2 * j, 2 * j + 1))],
                        g_adds=False)
                xbar_tok2fm(P, xqb, fmq, "xqb", "fmq", KD, TB)
                for k in range(KD):
                    P.dma("q", xq_s[k, :, c * T:(c + 1) * T],
                          fmq[:, k * T:(k + 1) * T], reads=("fmq",))
            P.barrier()
        _phase5()

        def _phase6():
            NSL = [(i * 512, 512) for i in range(5)] + [(2560, 256)]
            AR.reset()
            wgq = AR.bf(KD * DI)
            wuq = AR.bf(KD * DI)
            arsave = AR.off
            wtmp, wtmp_b = AR.fp(D)
            qt0 = AR.f(D)
            qt1 = AR.f(D)
            qt2 = AR.bf(D)
            qq = AR.bf(D)
            sc0 = AR.f(D // 16)
            sc1 = AR.f(D // 16)
            for src_h, dst, kdst in (((wg_h, wgq, "wgq"), (wu_h, wuq, "wuq"))
                                     if "p4w" in PHASES else ()):
                for nt in range(KI):
                    P.dma("g", wtmp_b,
                          src_h.bitcast(U16)[nt * 128:(nt + 1) * 128, :],
                          writes=("wtmp",))
                    quant_tok(P, wtmp, qq, qt0, qt1, qt2, sc0, sc1,
                              "wtmp", "qq", "qt", "qsc")
                    out = dst.rearrange("p (kc n) -> p kc n",
                                        kc=KD)[:, :, nt * 128:(nt + 1) * 128]
                    P.dma("q", out, qq, reads=("qq",), writes=(kdst,),
                          transpose=True)
            P.barrier()
            AR.off = arsave
            xqc = AR.bf(KD * T)
            sg = AR.bf(512)
            hband = AR.f(DI)
            hqb = AR.bf(DI)
            qt0 = AR.f(DI)
            qt1 = AR.f(DI)
            qt2 = AR.bf(DI)
            sc0 = AR.f(DI // 16)
            sc1 = AR.f(DI // 16)
            hqfm = AR.bf(KI * T)
            for c in range(NCH if "p4b" in PHASES else 0):
                for k in range(KD):
                    P.dma("g", xqc[:, k * T:(k + 1) * T],
                          xq_s[k, :, c * T:(c + 1) * T], writes=("xqc",))
                for tb in range(TB):
                    for off, nsz in NSL:
                        fg = [(lambda e, k=k, tb=tb, off=off, nsz=nsz: e.matmul(
                            pb[0][:, 0:nsz],
                            xqc[:, k * T + tb * 128: k * T + tb * 128 + 128],
                            wgq[:, k * DI + off: k * DI + off + nsz],
                            start=(k == 0), stop=(k == KD - 1)))
                            for k in range(KD)]
                        fu = [(lambda e, k=k, tb=tb, off=off, nsz=nsz: e.matmul(
                            pb[1][:, 0:nsz],
                            xqc[:, k * T + tb * 128: k * T + tb * 128 + 128],
                            wuq[:, k * DI + off: k * DI + off + nsz],
                            start=(k == 0), stop=(k == KD - 1)))
                            for k in range(KD)]
                        P.mm_group(fg, ("xqc", "wgq"), ("pb0",))
                        P.mm_group(fu, ("xqc", "wuq"), ("pb1",))
                        P.run("s", lambda e, nsz=nsz: e.activation(
                            sg[:, 0:nsz], pb[0][:, 0:nsz], AF.Silu),
                            ("pb0",), ("sg",))
                        P.run("v", lambda e, off=off, nsz=nsz: e.tensor_tensor(
                            hband[:, off:off + nsz], sg[:, 0:nsz],
                            pb[1][:, 0:nsz], ALU.mult),
                            ("sg", "pb1"), ("hband",))
                    quant_tok_multi(P, [
                        (hband, hqb, qt0, qt1, qt2, sc0, sc1,
                         "hband", "hqb", "hqt", "hsc")], g_adds=False)
                    out = hqfm.rearrange("p (kc b t) -> p b kc t", kc=KI,
                                         b=TB, t=128)[:, tb:tb + 1]
                    P.dma("q", out, hqb, reads=("hqb",), writes=("hqfm",),
                          transpose=True)
                for kc in range(KI):
                    P.dma("q", hq_s[kc, :, c * T:(c + 1) * T],
                          hqfm[:, kc * T:(kc + 1) * T], reads=("hqfm",))
            P.barrier()
        _phase6()

        def _phase7():
            AR.reset()
            wdq = AR.bf(KI * D)
            arsave = AR.off
            wtmp, wtmp_b = AR.fp(DI)
            qt0 = AR.f(DI)
            qt1 = AR.f(DI)
            qt2 = AR.bf(DI)
            qq = AR.bf(DI)
            sc0 = AR.f(DI // 16)
            sc1 = AR.f(DI // 16)
            for k in range(KD if "p4dw" in PHASES else 0):
                P.dma("g", wtmp_b,
                      wdn_h.bitcast(U16)[k * 128:(k + 1) * 128, :],
                      writes=("wtmp",))
                quant_tok(P, wtmp, qq, qt0, qt1, qt2, sc0, sc1,
                          "wtmp", "qq", "qt", "qsc")
                out = wdq.rearrange("p (kc n) -> p kc n",
                                    kc=KI)[:, :, k * 128:(k + 1) * 128]
                P.dma("q", out, qq, reads=("qq",), writes=("wdq",),
                      transpose=True)
            P.barrier()
            AR.off = arsave
            hqcd = [AR.bf(KI * T), AR.bf(KI * T)]
            act0d = [AR.fp(TB * D), AR.fp(TB * D)]
            act1 = AR.f(TB * D)
            tmpq = AR.bf(TB * D)
            outb, outb_b = AR.fp(TB * D)
            stat = AR.f(16)
            for c in range(NCH if "p4c" in PHASES else 0):
                hqc = hqcd[c % 2]
                khq = f"hqc{c % 2}"
                act0, act0_b = act0d[c % 2]
                ka0 = f"act0{c % 2}"
                for kc in range(KI):
                    P.dma("g", hqc[:, kc * T:(kc + 1) * T],
                          hq_s[kc, :, c * T:(c + 1) * T], writes=(khq,))
                P.dma("g", act0_b.rearrange("p (b f) -> p b f", f=2 * D),
                      x3_s.bitcast(U16)[c * T:(c + 1) * T, :].rearrange(
                          "(b p) f -> p b f", p=128),
                      writes=(ka0,))
                for tb in range(TB):
                    p = pw[tb % 2]
                    kpw = f"pw{tb % 2}"
                    fns = []
                    for h in range(2):
                        for kc in range(KI):
                            fns.append(lambda e, kc=kc, h=h, tb=tb, p=p,
                                       hqc=hqc:
                                       e.matmul(
                                p[:, h * 512:(h + 1) * 512],
                                hqc[:, kc * T + tb * 128:
                                    kc * T + tb * 128 + 128],
                                wdq[:, kc * D + h * 512: kc * D + (h + 1) * 512],
                                start=(kc == 0), stop=(kc == KI - 1)))
                    P.mm_group(fns, (khq, "wdq"), (kpw,))
                    P.run("v", lambda e, tb=tb, p=p, act0=act0: e.tensor_add(
                        act1[:, tb * D:(tb + 1) * D], p[:],
                        act0[:, tb * D:(tb + 1) * D]),
                        (kpw, ka0), ("act1",))
                ln_tok_multi(P, [
                    (act1[:, tb * D:(tb + 1) * D],
                     outb[:, tb * D:(tb + 1) * D],
                     tmpq[:, tb * D:(tb + 1) * D],
                     stat[:, tb * 4:(tb + 1) * 4],
                     "act1", "outb", f"tmpq{tb}", f"stat{tb}",
                     None, None, None)
                    for tb in range(TB)], epsc[:])
                P.dma("q",
                      out_h.bitcast(U16)[c * T:(c + 1) * T, :].rearrange(
                          "(b p) f -> p b f", p=128),
                      outb_b.rearrange("p (b f) -> p b f", f=2 * D),
                      reads=("outb",))
            P.barrier()
        _phase7()

        with nc.Block() as block:
            for ename, reg in (("tensor", "t"), ("vector", "v"),
                               ("scalar", "s"), ("gpsimd", "g"),
                               ("sync", "q")):
                def mk(reg):
                    def body(eng):
                        for th in P.thunks[reg]:
                            th(eng)
                    return body
                getattr(block, ename)(mk(reg))

    return nc


def _fmcols(vec, kd=KD):
    return np.ascontiguousarray(vec.reshape(kd, 128).T)


def kernel(**inputs):
    g = {k.replace("L0_attn_", "").replace("L0_ffn_", "f_"): np.asarray(v)
         for k, v in inputs.items()}
    x = np.ascontiguousarray(g["x"], dtype=np.float32)
    B = x.shape[0]

    cols = np.concatenate(
        [_fmcols(g["w1_b"]), _fmcols(g["v1_b"]), _fmcols(g["dt_b"]),
         _fmcols(g["D_p"])], axis=1).astype(np.float32)
    rows = np.concatenate(
        [g["ln1_g"], g["ln1_b"], g["ln_g"], g["ln_b"], g["ln2_g"],
         g["ln2_b"], g["w2_b"], g["ff2_b"]]).reshape(1, 8 * D).astype(
             np.float32)
    alog = np.ascontiguousarray(
        g["A_log"].reshape(KD, 128, NST).transpose(1, 0, 2).reshape(
            128, KD * NST)).astype(np.float32)
    convw = np.ascontiguousarray(
        g["conv_w"].reshape(D, 3).reshape(KD, 128, 3).transpose(
            1, 0, 2).reshape(128, KD * 3)).astype(np.float32)
    ff1b = _fmcols(g["ff1_b"], KFF).astype(np.float32)
    bcw = np.concatenate([g["Bp_w"], g["Cp_w"]], axis=1).astype(np.float32)
    bcb = np.concatenate([g["Bp_b"], g["Cp_b"]]).astype(np.float32)
    emat = np.eye(2 * NST, dtype=np.float32)[:, :, None].repeat(
        128, axis=2).reshape(2 * NST, 2 * NST * 128)

    shared = {
        "w1": g["w1_w"], "v1": g["v1_w"], "dtw": g["dt_w"], "w2": g["w2_w"],
        "bcw": bcw, "ff1": g["ff1_w"], "ff2": g["ff2_w"],
        "wg": g["f_w_gate"], "wu": g["f_w_up"], "wd": g["f_w_down"],
        "cols": cols, "ff1b": ff1b, "alog": alog, "convw": convw,
        "rows": rows, "emat": emat,
        "bcbc": bcb.reshape(2 * NST, 1),
    }
    shared = {k: np.ascontiguousarray(v, dtype=np.float32)
              for k, v in shared.items()}

    nc = build_nc()
    in_maps = [dict(shared, xt=np.ascontiguousarray(x[b])) for b in range(B)]
    res = run_bass_kernel_spmd(nc, in_maps, core_ids=list(range(B)))
    out = np.stack([np.asarray(res.results[b]["out"]) for b in range(B)])
    return out.astype(np.float32)


if __name__ == "__main__":
    nc = build_nc()
    print("build ok; instructions:", sum(1 for _ in nc.all_instructions()))
